# version 02
# baseline (speedup 1.0000x reference)
import contextlib
import numpy as np

import concourse.bass as bass
import concourse.mybir as mybir
from concourse.bass_utils import run_bass_kernel_spmd

F32 = mybir.dt.float32
BF16 = mybir.dt.bfloat16
I16 = mybir.dt.int16
I32 = mybir.dt.int32
U32 = mybir.dt.uint32
AF = mybir.ActivationFunctionType
ALU = mybir.AluOpType
AXT = mybir.AxisListType

B, T, C = 4, 4096, 1024
NCH = C // 128
TPC = 2048
NTILE = TPC // 128
NGRP = TPC // 512
EPS = 1e-5

N_STAGES = 6



def _prep_inputs(inp):
    pm = {}

    def T2(a):
        return np.ascontiguousarray(np.asarray(a).T)

    pm["w_l0f_router"] = T2(inp["L0_ffn_router_w"])
    pm["w_l0f_gate"] = np.ascontiguousarray(inp["L0_ffn_gate_w"])
    pm["w_l0f_up"] = np.ascontiguousarray(inp["L0_ffn_up_w"])
    pm["w_l0f_down"] = np.ascontiguousarray(inp["L0_ffn_down_w"])
    pm["w_l0f_sg"] = T2(inp["L0_ffn_shared_gate_w"])
    pm["w_l0f_su"] = T2(inp["L0_ffn_shared_up_w"])
    pm["w_l0f_sd"] = T2(inp["L0_ffn_shared_down_w"])
    pm["w_l0a_k"] = T2(inp["L0_attn_att_kw"])
    pm["w_l0a_v"] = T2(inp["L0_attn_att_vw"])
    pm["w_l0a_r"] = T2(inp["L0_attn_att_rw"])
    pm["w_l0a_o"] = T2(inp["L0_attn_att_ow"])
    pm["w_l0a_fk"] = T2(inp["L0_attn_f_kw"])
    pm["w_l0a_fr"] = T2(inp["L0_attn_f_rw"])
    pm["w_l0a_fv"] = T2(inp["L0_attn_f_vw"])
    pm["w_l1a_qkv"] = T2(inp["L1_attn_W_qkv"])
    pm["w_l1f_gate"] = np.ascontiguousarray(inp["L1_ffn_gate_w"])
    pm["w_l1f_w1"] = np.ascontiguousarray(inp["L1_ffn_w1"])
    pm["w_l1f_w2"] = np.ascontiguousarray(inp["L1_ffn_w2"])

    def chanvec(a):
        return np.ascontiguousarray(np.asarray(a).reshape(NCH, 128).T)

    for nm, src in [("c_ln1w", "L0_attn_ln1_w"), ("c_ln1b", "L0_attn_ln1_b"),
                    ("c_ln2w", "L0_attn_ln2_w"), ("c_ln2b", "L0_attn_ln2_b"),
                    ("c_decay", "L0_attn_time_decay"), ("c_tfirst", "L0_attn_time_first"),
                    ("c_tmk", "L0_attn_tmk"), ("c_tmv", "L0_attn_tmv"),
                    ("c_tmr", "L0_attn_tmr"), ("c_ftmk", "L0_attn_f_tmk"),
                    ("c_ftmr", "L0_attn_f_tmr")]:
        pm[nm] = chanvec(inp[src])
    pm["c_bqkv"] = np.ascontiguousarray(
        np.asarray(inp["L1_attn_b_qkv"]).reshape(24, 128).T)
    pm["r_bv"] = np.ascontiguousarray(
        np.asarray(inp["L1_attn_b_qkv"])[None, 2048:3072]).astype(np.float32)
    pm["c_l1b1"] = np.ascontiguousarray(
        np.asarray(inp["L1_ffn_b1"]).reshape(8, 32, 128).transpose(0, 2, 1))
    pm["r_l1b2"] = np.ascontiguousarray(np.asarray(inp["L1_ffn_b2"]))
    pm["k_cb16"] = np.ascontiguousarray(
        np.tile(np.asarray(inp["L0_ffn_correction_bias"])[None, :], (128, 1))
    ).astype(np.float32)
    pm["k_gb8"] = np.ascontiguousarray(
        np.tile(np.asarray(inp["L1_ffn_gate_b"])[None, :], (128, 1))
    ).astype(np.float32)
    pm["k_iota16"] = np.tile(np.arange(16, dtype=np.float32)[None, :], (128, 1))
    pm["k_iota8"] = np.tile(np.arange(8, dtype=np.float32)[None, :], (128, 1))
    sel = np.zeros((16, 16 * 128), np.float32)
    for e in range(16):
        sel[e, e * 128:(e + 1) * 128] = 1.0
    pm["k_sel16"] = sel
    pm["k_onesr"] = np.ones((65, 128), np.float32)
    pm["k_onesc"] = np.ones((128, 1), np.float32)
    pm["k_tri"] = np.ascontiguousarray(np.triu(np.ones((128, 128), np.float32)))
    pm["k_causal"] = np.where(np.tril(np.ones((128, 128), np.bool_)), 0.0,
                              -1e9).astype(np.float32)
    pm["k_ident"] = np.eye(128, dtype=np.float32)

    gt_src = np.empty(128, np.int64)
    for r in range(8):
        for zz in range(16):
            gt_src[r * 16 + zz] = (r // 2) * 32 + 2 * zz + (r % 2)

    x = np.asarray(inp["L0_attn_x"])
    cores = []
    for core in range(8):
        b, par = core // 2, core % 2
        xt = x[b].reshape(32, 128, C)[par::2].reshape(TPC, C)
        gt_dst = b * 32 + 2 * np.arange(16) + par
        pfx = (gt_src[:, None] < gt_dst[None, :]).astype(np.float32)
        d = {"x_in": np.ascontiguousarray(xt.T),
             "c_parv": np.full((128, 1), float(par), np.float32),
             "c_nparv": np.full((128, 1), 1.0 - par, np.float32),
             "k_pfx": pfx}
        d.update(pm)
        cores.append(d)
    return cores


def _gather_output(res):
    out = np.empty((B, T, C), np.float32)
    for core in range(8):
        b, par = core // 2, core % 2
        xo = res[core]["x_out"]
        out[b].reshape(32, 128, C)[par::2] = xo.T.reshape(NTILE, 128, C)
    return out



class Ctr:
    def __init__(self, nc, es, name):
        self.h = es.enter_context(nc.semaphore(name))
        self.n = 0

    def inc(self, instr, k=1):
        instr.then_inc(self.h, k)
        self.n += k
        return self.n

    def incd(self, instr, k=16):
        return self.inc(instr, k)

    def bump(self, eng, k=1):
        eng.sem_inc(self.h, k)
        self.n += k
        return self.n

    def note(self, k=16):
        self.n += k

    def wait(self, eng, n=None):
        eng.wait_ge(self.h, self.n if n is None else n)


class CtrView:

    def __init__(self, ctr):
        self.c = ctr
        self.base = ctr.n

    @property
    def h(self):
        return self.c.h

    @property
    def n(self):
        return self.c.n - self.base

    def inc(self, instr, k=1):
        self.c.inc(instr, k)
        return self.n

    def incd(self, instr, k=16):
        return self.inc(instr, k)

    def bump(self, eng, k=1):
        self.c.bump(eng, k)
        return self.n

    def note(self, k=16):
        self.c.n += k

    def wait(self, eng, n=None):
        eng.wait_ge(self.c.h, self.c.n if n is None else self.base + n)


class St:
    pass



def build_program(n_stages=N_STAGES, debug_taps=(), first_stage=1):
    nc = bass.Bass("TRN2", target_bir_lowering=False)
    es = contextlib.ExitStack()
    st = St()
    st.nc, st.es = nc, es
    st.debug_taps = tuple(debug_taps)
    st.uid = 0

    dr = {}

    def din(name, shape, dtype=F32):
        dr[name] = nc.dram_tensor(name, shape, dtype, kind="ExternalInput").ap()

    def dint(name, shape, dtype=F32):
        dr[name] = nc.dram_tensor(name, shape, dtype, kind="Internal").ap()
        return dr[name]

    din("x_in", [C, TPC])
    din("c_parv", [128, 1]); din("c_nparv", [128, 1])
    x_out = nc.dram_tensor("x_out", [C, TPC], F32, kind="ExternalOutput").ap()
    st.dbg = {}
    for tap in st.debug_taps:
        st.dbg[tap] = nc.dram_tensor(f"dbg{tap}", [C, TPC], F32,
                                     kind="ExternalOutput").ap()

    din("w_l0f_router", [C, 16]); din("w_l0f_gate", [16, C, 256])
    din("w_l0f_up", [16, C, 256]); din("w_l0f_down", [16, 256, C])
    din("w_l0f_sg", [C, 2048]); din("w_l0f_su", [C, 2048]); din("w_l0f_sd", [2048, C])
    din("w_l0a_k", [C, C]); din("w_l0a_v", [C, C]); din("w_l0a_r", [C, C])
    din("w_l0a_o", [C, C]); din("w_l0a_fk", [C, 4096]); din("w_l0a_fr", [C, C])
    din("w_l0a_fv", [4096, C]); din("w_l1a_qkv", [C, 3072])
    din("w_l1f_gate", [C, 8]); din("w_l1f_w1", [8, C, 4096]); din("w_l1f_w2", [8, 4096, C])
    CONSTS = [("c_ln1w", [128, 8]), ("c_ln1b", [128, 8]), ("c_ln2w", [128, 8]),
              ("c_ln2b", [128, 8]), ("c_decay", [128, 8]), ("c_tfirst", [128, 8]),
              ("c_tmk", [128, 8]), ("c_tmv", [128, 8]), ("c_tmr", [128, 8]),
              ("c_ftmk", [128, 8]), ("c_ftmr", [128, 8]), ("c_bqkv", [128, 24]),
              ("k_cb16", [128, 16]), ("k_gb8", [128, 8]), ("k_iota16", [128, 16]),
              ("k_iota8", [128, 8]), ("k_causal", [128, 128])]
    for nm, sh in CONSTS:
        din(nm, sh)
    din("c_l1b1", [8, 128, 32]); din("r_l1b2", [8, C])
    din("k_sel16", [16, 16 * 128]); din("k_onesr", [65, 128]); din("k_onesc", [128, 1])
    din("k_tri", [128, 128]); din("k_ident", [128, 128]); din("k_pfx", [128, 16])
    din("r_bv", [1, 1024])

    wb = {}
    W_ORDER = ["w_l0f_router", "w_l0f_sg", "w_l0f_su", "w_l0f_sd", "w_l0f_gate",
               "w_l0f_up", "w_l0f_down", "w_l0a_k", "w_l0a_v", "w_l0a_r",
               "w_l0a_o", "w_l0a_fk", "w_l0a_fr", "w_l0a_fv", "w_l1f_gate",
               "w_l1f_w1", "w_l1f_w2", "w_l1a_qkv", "r_l1b2"]
    for nm in W_ORDER:
        wb[nm] = dint(nm + "_b", list(dr[nm].shape), BF16)
    st.dr, st.wb, st.dint = dr, wb, dint
    st.wc_done = {}

    with es:
        st.x = es.enter_context(nc.sbuf_tensor("x", [128, NCH, TPC], F32))
        st.parv = es.enter_context(nc.sbuf_tensor("parv", [128, 1], F32))
        st.nparv = es.enter_context(nc.sbuf_tensor("nparv", [128, 1], F32))
        st.consts = {}
        for nm, sh in CONSTS:
            st.consts[nm] = es.enter_context(nc.sbuf_tensor(nm + "_s", sh, F32))
        st.tri_b = es.enter_context(nc.sbuf_tensor("tri_b", [128, 128], BF16))
        st.ident_b = es.enter_context(nc.sbuf_tensor("ident_b", [128, 128], BF16))
        st.onesr_b = es.enter_context(nc.sbuf_tensor("onesr_b", [65, 128], BF16))
        st.onesc_b = es.enter_context(nc.sbuf_tensor("onesc_b", [128, 1], BF16))
        st.pfx_b = es.enter_context(nc.sbuf_tensor("pfx_b", [128, 16], BF16))

        st.cLD = Ctr(nc, es, "cLD")
        st.cKC = Ctr(nc, es, "cKC")
        st.cWCf = Ctr(nc, es, "cWCf")
        st.cWCa = Ctr(nc, es, "cWCa")
        st.cWC1 = Ctr(nc, es, "cWC1")
        st.cWS = Ctr(nc, es, "cWS")
        st.cTV = Ctr(nc, es, "cTV")
        st.cVT = Ctr(nc, es, "cVT")
        st.cSC = Ctr(nc, es, "cSC")
        st.cGP = Ctr(nc, es, "cGP")
        st.cDM = Ctr(nc, es, "cDM")
        st.cOUT = Ctr(nc, es, "cOUT")
        st._pool = {}
        st._stageidx = {"E": 0, "H": 0, "G": 0}

        def _generic(cls):
            i = st._stageidx[cls]
            st._stageidx[cls] = i + 1
            name = f"{cls}{i}"
            if name not in st._pool:
                st._pool[name] = Ctr(nc, es, name)
            return CtrView(st._pool[name])

        st.Ce = lambda name=None: _generic("E")
        st.Ch = lambda name=None: _generic("H")
        st.Cg = lambda name=None: _generic("G")

        def stage_reset():
            st._stageidx = {"E": 0, "H": 0, "G": 0}
        st.stage_reset = stage_reset

        blk = es.enter_context(nc.Block())
        st.blk = blk

        _emit_preload(st)
        EMITTERS = [(1, emit_l0ffn, "s1"), (2, emit_l0attn, "s2"),
                    (3, emit_l0ffn, "s3"), (4, emit_l1ffn, "s4"),
                    (5, emit_l1attn, "s5"), (6, emit_l1ffn, "s6")]
        for i, fn, tg in EMITTERS:
            if first_stage <= i <= n_stages:
                fn(st, tag=tg)
                if i in st.debug_taps:
                    _emit_tap(st, i)

        @blk.scalar
        def _(s):
            xh, xn = st.x_ready
            s.wait_ge(xh, xn)
            s.dma_start(out=x_out.rearrange("(k p) t -> p k t", p=128),
                        in_=st.x[:, :, :]).then_inc(st.cOUT.h, 16)
            st.cOUT.n += 16
            st.cOUT.wait(s)

        @blk.sync
        def _(s):
            st.cOUT.wait(s)

        @blk.gpsimd
        def _(g):
            st.cOUT.wait(g)

        @blk.vector
        def _(v):
            st.cOUT.wait(v)

        @blk.tensor
        def _(t):
            st.cOUT.wait(t)

    return nc


def _emit_tap(st, tap):
    blk, cOUT = st.blk, st.cOUT

    @blk.scalar
    def _(s):
        xh, xn = st.x_ready
        s.wait_ge(xh, xn)
        s.dma_start(out=st.dbg[tap].rearrange("(k p) t -> p k t", p=128),
                    in_=st.x[:, :, :]).then_inc(cOUT.h, 16)
        cOUT.n += 16
        cOUT.wait(s)


def _emit_preload(st):
    nc, blk, dr, wb = st.nc, st.blk, st.dr, st.wb
    cLD, cSC, cVT = st.cLD, st.cSC, st.cVT

    @blk.sync
    def _(s):
        s.dma_start(out=st.x[:, :, :],
                    in_=dr["x_in"].rearrange("(k p) t -> p k t", p=128)
                    ).then_inc(cLD.h, 16)
        cLD.n += 16
        s.dma_start(out=st.parv[:], in_=dr["c_parv"]).then_inc(cLD.h, 16)
        cLD.n += 16
        s.dma_start(out=st.nparv[:], in_=dr["c_nparv"]).then_inc(cLD.h, 16)
        cLD.n += 16
        for nm in st.consts:
            s.dma_start(out=st.consts[nm][:], in_=dr[nm]).then_inc(cLD.h, 16)
            cLD.n += 16

    @blk.gpsimd
    def _(g):
        for dst, src in [(st.tri_b, "k_tri"), (st.ident_b, "k_ident"),
                         (st.onesr_b, "k_onesr"), (st.onesc_b, "k_onesc"),
                         (st.pfx_b, "k_pfx")]:
            g.dma_start(out=dst[:], in_=dr[src]).then_inc(st.cKC.h, 16)
            st.cKC.n += 16
        GROUPS = [(st.cWCf, ["w_l0f_router", "w_l0f_sg", "w_l0f_su", "w_l0f_sd",
                             "w_l0f_gate", "w_l0f_up", "w_l0f_down"]),
                  (st.cWCa, ["w_l0a_k", "w_l0a_v", "w_l0a_r", "w_l0a_o",
                             "w_l0a_fk", "w_l0a_fr", "w_l0a_fv"]),
                  (st.cWC1, ["w_l1f_gate", "w_l1f_w1", "w_l1f_w2", "w_l1a_qkv",
                             "r_l1b2"])]
        for ctr, names_ in GROUPS:
            for nm in names_:
                src, dst = dr[nm], wb[nm]
                n = int(np.prod(src.shape))
                names = " ".join("abcdefg"[: len(src.shape)])
                pat = f"{names} -> ({names})"
                fs, fd = src.rearrange(pat), dst.rearrange(pat)
                step = 4 * 1024 * 1024
                for off in range(0, n, step):
                    ln = min(step, n - off)
                    g.dma_start(out=fd[off:off + ln],
                                in_=fs[off:off + ln]).then_inc(ctr.h, 16)
                    ctr.n += 16

    ld_done = cLD.n
    st.ld_done = ld_done
    st.x_ready = (st.cLD.h, ld_done)



def emit_l0ffn(st, tag):
    nc, blk, x = st.nc, st.blk, st.x
    dr, wb, consts = st.dr, st.wb, st.consts
    es = st.es
    cLD, cWS = st.cLD, st.cWS
    xr_h, xr_n = st.x_ready

    st.stage_reset()
    Ce, Ch, Cg = st.Ce, st.Ch, st.Cg
    sA_x, sA_q, sA_sm, sA_v, sA_sq, sA_r, sA_rep, sA_y = (
        Ce(), Ce(), Ce(), Ce(), Ce(), Ce(), Ce(), Ce())
    sA_lg, sA_sg, sA_cmb, sA_tp, sA_ev = Ce(), Ce(), Ce(), Ce(), Ce()
    sB_wgu = [Ch(), Ch()]
    sB_wge = [Cg(), Cg()]
    sB_wsd = [Ch(), Ch()]
    sB_wdn = [Ch(), Ch()]
    sB_fh, sB_sl, sB_hs = Ce(), Ce(), Ce()
    sB_dsf, sB_dde = Ce(), Ce()
    sB_cr, sB_eh, sB_esl, sB_hw = Ce(), Ce(), Ce(), Ce()
    sB_dn, sB_dv = Ce(), Ce()

    loc = contextlib.ExitStack()
    sb = lambda n, s, d: loc.enter_context(nc.sbuf_tensor(f"{n}_{tag}", s, d))
    with loc:
        yb = sb("yb", [128, NCH, TPC], BF16)
        cmbT = sb("cmbT", [16, TPC], BF16)
        hsb = sb("hsb", [128, 16, 512], BF16)
        hwb = sb("hwb", [128, 16, 2, 512], BF16)
        silu_t = sb("silu_t", [128, 2, 512], F32)
        hw_t = sb("hw_t", [128, 512], F32)
        sel16_b = sb("sel16_b", [16, 16 * 128], BF16)
        csel = st.Cg()
        wsg = sb("wsg", [128, NCH, 2, 256], BF16)
        wsu = sb("wsu", [128, NCH, 2, 256], BF16)
        wge = sb("wge", [128, NCH, 2, 256], BF16)
        wup = sb("wup", [128, NCH, 2, 256], BF16)

        gate_r = wb["w_l0f_gate"].rearrange("e (k p) h -> e p k h", p=128)
        up_r = wb["w_l0f_up"].rearrange("e (k p) h -> e p k h", p=128)
        down_r = wb["w_l0f_down"].rearrange("e (h p) c -> e p h c", p=128)
        sg_r = wb["w_l0f_sg"].rearrange("(k p) f -> p k f", p=128)
        su_r = wb["w_l0f_su"].rearrange("(k p) f -> p k f", p=128)
        sd_r = wb["w_l0f_sd"].rearrange("(f p) c -> p f c", p=128)
        wcA = st.cWCf.n
        wcB = st.cWCf.n
        NBLK4 = 4 * NGRP

        pa = contextlib.ExitStack()
        with pa:
            sba = lambda n, s, d: pa.enter_context(nc.sbuf_tensor(f"{n}_{tag}", s, d))
            mrow = sba("mrow", [1, 512], F32)
            trow = sba("trow", [1, 1024], F32)
            r0row = sba("r0row", [1, 512], F32)
            mrow_b = sba("mrow_b", [1, 512], BF16)
            r0row_b = sba("r0row_b", [1, 512], BF16)
            sco = sba("sco", [128, 16, 16], F32)
            cmb_blk = sba("cmb_blk", [128, 16, 16], BF16)
            sbb = sba("sbb", [128, 16], F32)
            msb = sba("msb", [128, 16], F32)
            gsc = sba("gsc", [128, 4], F32)
            madd = sba("madd", [128, 4], F32)
            t8 = sba("t8", [128, 8], F32)
            pad8 = sba("pad8", [128, 8], F32)
            wun = sba("wun", [128, 16], F32)
            wsum = sba("wsum", [128, 1], F32)
            wrt = sba("wrt", [128, NCH, 16], BF16)
            ps_st = pa.enter_context(nc.psum_tensor(f"psst_{tag}", [1, 1024], F32))
            ps_mr = pa.enter_context(nc.psum_tensor(f"psmr_{tag}", [128, 1024], F32))
            ps_lg = pa.enter_context(nc.psum_tensor(f"pslg_{tag}", [128, 2, 512], F32))
            psT = pa.enter_context(nc.psum_tensor(f"psT_{tag}", [128, 2, 128], BF16))

            @blk.sync
            def _(s):
                s.wait_ge(xr_h, xr_n)
                s.wait_ge(st.cWCf.h, wcA)
                s.dma_start(out=wrt[:], in_=wb["w_l0f_router"]
                            .rearrange("(k p) e -> p k e", p=128)).then_inc(cWS.h, 16)
                cWS.n += 16
            ws_rt = cWS.n

            @blk.vector
            def _(v):
                v.wait_ge(xr_h, xr_n)
                for g in range(NGRP):
                    gsl = slice(g * 512, (g + 1) * 512)
                    xsb = yb[:, :, gsl]
                    xqb = hsb[:, 0:NCH, :]
                    sA_sm.wait(v, g)
                    sA_q.wait(v, g)
                    for k in range(NCH):
                        v.tensor_copy(out=xsb[:, k, :], in_=x[:, k, gsl])
                    v.drain()
                    sA_x.bump(v)
                    sA_sm.wait(v, g + 1)
                    v.tensor_scalar_mul(out=mrow[:], in0=ps_st[0:1, 0:512],
                                        scalar1=1.0 / C)
                    v.tensor_scalar_mul(out=trow[0:1, 0:512],
                                        in0=ps_st[0:1, 512:1024], scalar1=1.0 / C)
                    v.drain()
                    v.tensor_mul(out=trow[0:1, 512:1024], in0=mrow[:], in1=mrow[:])
                    v.drain()
                    v.tensor_sub(out=trow[0:1, 0:512], in0=trow[0:1, 0:512],
                                 in1=trow[0:1, 512:1024])
                    v.drain()
                    v.tensor_scalar_add(out=trow[0:1, 512:1024],
                                        in0=trow[0:1, 0:512], scalar1=EPS)
                    v.drain()
                    sA_v.bump(v)
                    sA_sq.wait(v, g + 1)
                    v.reciprocal(out=r0row[:], in_=trow[0:1, 512:1024])
                    v.drain()
                    v.tensor_copy(out=mrow_b[:], in_=mrow[:])
                    v.tensor_copy(out=r0row_b[:], in_=r0row[:])
                    v.drain()
                    sA_r.bump(v)
                    sA_rep.wait(v, g + 1)
                    for k in range(NCH):
                        v.tensor_sub(out=hw_t[:], in0=x[:, k, gsl],
                                     in1=ps_mr[:, 0:512])
                        v.drain()
                        v.tensor_tensor(out=yb[:, k, gsl], in0=hw_t[:],
                                        in1=ps_mr[:, 512:1024], op=ALU.mult)
                        v.drain()
                    sA_y.bump(v)
                v.memset(pad8[:], -1e30)
                v.drain()
                for i in range(NBLK4):
                    sA_sg.wait(v, i + 1)
                    v.tensor_add(out=sbb[:], in0=sco[:, i, :], in1=consts["k_cb16"][:])
                    v.drain()
                    for gr in range(4):
                        v.tensor_copy(out=pad8[:, 0:4], in_=sbb[:, 4 * gr:4 * gr + 4])
                        v.drain()
                        v.max(out=t8[:], in_=pad8[:])
                        v.drain()
                        v.tensor_add(out=gsc[:, gr:gr + 1], in0=t8[:, 0:1],
                                     in1=t8[:, 1:2])
                        v.drain()
                    v.tensor_copy(out=pad8[:, 0:4], in_=gsc[:])
                    v.drain()
                    v.max(out=t8[:], in_=pad8[:])
                    v.drain()
                    v.tensor_scalar(out=madd[:], in0=gsc[:], scalar1=t8[:, 1:2],
                                    scalar2=None, op0=ALU.is_ge)
                    v.drain()
                    v.tensor_scalar(out=madd[:], in0=madd[:], scalar1=1e9,
                                    scalar2=1e9, op0=ALU.mult, op1=ALU.subtract)
                    v.drain()
                    for gr in range(4):
                        v.tensor_scalar_add(out=msb[:, 4 * gr:4 * gr + 4],
                                            in0=sbb[:, 4 * gr:4 * gr + 4],
                                            scalar1=madd[:, gr:gr + 1])
                    v.drain()
                    v.max(out=t8[:], in_=msb[:])
                    v.drain()
                    v.tensor_scalar(out=wun[:], in0=msb[:], scalar1=t8[:, 3:4],
                                    scalar2=None, op0=ALU.is_ge)
                    v.drain()
                    v.tensor_mul(out=wun[:], in0=wun[:], in1=sco[:, i, :])
                    v.drain()
                    v.reduce_sum(out=wsum[:], in_=wun[:], axis=AXT.X)
                    v.drain()
                    v.tensor_scalar_add(out=wsum[:], in0=wsum[:], scalar1=1e-20)
                    v.drain()
                    v.reciprocal(out=wsum[:], in_=wsum[:])
                    v.drain()
                    v.tensor_scalar(out=cmb_blk[:, i, :], in0=wun[:],
                                    scalar1=wsum[:, 0:1], scalar2=None,
                                    op0=ALU.mult)
                    v.drain()
                    sA_cmb.bump(v)
                    sA_tp.wait(v, i + 1)
                    v.tensor_copy(out=cmbT[0:16, i * 128:(i + 1) * 128],
                                  in_=psT[0:16, i % 2, :])
                    v.drain()
                    sA_ev.bump(v)

            @blk.scalar
            def _(s):
                for g in range(NGRP):
                    gsl = slice(g * 512, (g + 1) * 512)
                    xsb = yb[:, :, gsl]
                    xqb = hsb[:, 0:NCH, :]
                    sA_x.wait(s, g + 1)
                    for k in range(NCH):
                        s.activation(out=xqb[:, k, :], in_=xsb[:, k, :],
                                     func=AF.Square)
                    s.drain()
                    sA_q.bump(s)
                    sA_v.wait(s, g + 1)
                    s.activation(out=trow[0:1, 512:1024], in_=trow[0:1, 512:1024],
                                 func=AF.Sqrt)
                    s.drain()
                    sA_sq.bump(s)
                for i in range(NBLK4):
                    sA_lg.wait(s, i + 1)
                    s.activation(out=sco[:, i, :], in_=ps_lg[:, i % 2, 0:16],
                                 func=AF.Sigmoid)
                    s.drain()
                    sA_sg.bump(s)

            @blk.tensor
            def _(t):
                t.wait_ge(st.cKC.h, st.cKC.n)
                for g in range(NGRP):
                    gsl = slice(g * 512, (g + 1) * 512)
                    xsb = yb[:, :, gsl]
                    xqb = hsb[:, 0:NCH, :]
                    sA_x.wait(t, g + 1)
                    sA_q.wait(t, g + 1)
                    sA_r.wait(t, g)
                    for k in range(NCH):
                        t.matmul(ps_st[0:1, 0:512], st.onesc_b[:], xsb[:, k, :],
                                 start=(k == 0), stop=(k == NCH - 1))
                    for k in range(NCH):
                        last = (k == NCH - 1)
                        mm = t.matmul(ps_st[0:1, 512:1024], st.onesc_b[:],
                                      xqb[:, k, :], start=(k == 0), stop=last)
                        if last:
                            sA_sm.inc(mm)
                    sA_r.wait(t, g + 1)
                    sA_y.wait(t, g)
                    t.matmul(ps_mr[:, 0:512], st.onesr_b[0:1, :], mrow_b[:],
                             start=True, stop=True)
                    mm = t.matmul(ps_mr[:, 512:1024], st.onesr_b[0:1, :], r0row_b[:],
                                  start=True, stop=True)
                    sA_rep.inc(mm)
                t.wait_ge(cWS.h, ws_rt)
                for i in range(NBLK4):
                    g = i // 4
                    sA_y.wait(t, g + 1)
                    sA_sg.wait(t, max(0, i - 1))
                    bsl = slice(i * 128, (i + 1) * 128)
                    for k in range(NCH):
                        mm = t.matmul(ps_lg[:, i % 2, 0:16], yb[:, k, bsl],
                                      wrt[:, k, :], start=(k == 0),
                                      stop=(k == NCH - 1))
                    sA_lg.inc(mm)
                    sA_cmb.wait(t, i + 1)
                    sA_ev.wait(t, max(0, i - 1))
                    tp = t.transpose(psT[0:16, i % 2, :], cmb_blk[:, i, :],
                                     st.ident_b[:])
                    sA_tp.inc(tp)

        wdn = sb("wdn", [128, 2, 2, 1024], BF16)
        wsd = sb("wsd", [128, 2, 1024], BF16)
        @blk.gpsimd
        def _(g_):
            g_.wait_ge(xr_h, xr_n)
            g_.dma_start(out=sel16_b[:], in_=dr["k_sel16"]).then_inc(csel.h, 16)
            csel.note(16)
            g_.wait_ge(st.cWCf.h, wcB)
            for g in range(NGRP):
                for e in range(16):
                    q = g * 16 + e
                    if g == 0 and e < 2:
                        pass
                    else:
                        sB_eh.wait(g_, (q - 1) * 2)
                    g_.dma_start(out=wge[:, :, e % 2, :],
                                 in_=gate_r[e]).then_inc(sB_wge[q % 2].h, 16)
                    sB_wge[q % 2].note(16)
                    g_.dma_start(out=wup[:, :, e % 2, :],
                                 in_=up_r[e]).then_inc(sB_wge[q % 2].h, 16)
                    sB_wge[q % 2].note(16)

        @blk.sync
        def _(s):
            for g in range(NGRP):
                for j in range(8):
                    q = g * 8 + j
                    if q >= 2:
                        sB_fh.wait(s, (q - 2) * 2 + 2)
                    s.dma_start(out=wsg[:, :, j % 2, :],
                                in_=sg_r[:, :, j * 256:j * 256 + 256]
                                ).then_inc(sB_wgu[q % 2].h, 16)
                    sB_wgu[q % 2].note(16)
                    s.dma_start(out=wsu[:, :, j % 2, :],
                                in_=su_r[:, :, j * 256:j * 256 + 256]
                                ).then_inc(sB_wgu[q % 2].h, 16)
                    sB_wgu[q % 2].note(16)
                for f in range(16):
                    q = g * 16 + f
                    if q >= 2:
                        sB_dsf.wait(s, q - 1)
                    s.dma_start(out=wsd[:, f % 2, :],
                                in_=sd_r[:, f, :]).then_inc(sB_wsd[q % 2].h, 16)
                    sB_wsd[q % 2].note(16)
                for e in range(16):
                    q = g * 16 + e
                    if q >= 2:
                        sB_dde.wait(s, q - 1)
                    s.dma_start(out=wdn[:, :, e % 2, :],
                                in_=down_r[e]).then_inc(sB_wdn[q % 2].h, 16)
                    sB_wdn[q % 2].note(16)

        for g in range(NGRP):
            gsl = slice(g * 512, (g + 1) * 512)
            ph = contextlib.ExitStack()
            with ph:
                ps_g = ph.enter_context(
                    nc.psum_tensor(f"psg_{tag}{g}", [128, 2, 512], F32))
                ps_u = ph.enter_context(
                    nc.psum_tensor(f"psu_{tag}{g}", [128, 2, 512], F32))
                ps_cmb = ph.enter_context(
                    nc.psum_tensor(f"pscmb_{tag}{g}", [128, 512], F32))

                @blk.tensor
                def _(t):
                    if g == 0:
                        sA_ev.wait(t, NBLK4)
                    else:
                        sB_dv.wait(t, g)
                    for f in range(16):
                        q16 = g * 8 + f // 2
                        sB_wgu[q16 % 2].wait(t, 32 * (q16 // 2 + 1))
                        if f >= 2:
                            sB_hs.wait(t, g * 16 + f - 1)
                        wo = slice((f % 2) * 128, (f % 2) * 128 + 128)
                        for k in range(NCH):
                            t.matmul(ps_g[:, f % 2, :],
                                     wsg[:, k, (f // 2) % 2, wo],
                                     yb[:, k, gsl], start=(k == 0),
                                     stop=(k == NCH - 1))
                        for k in range(NCH):
                            last = (k == NCH - 1)
                            mm = t.matmul(ps_u[:, f % 2, :],
                                          wsu[:, k, (f // 2) % 2, wo],
                                          yb[:, k, gsl], start=(k == 0), stop=last)
                            if last:
                                sB_fh.inc(mm)
                    sB_hs.wait(t, g * 16 + 16)
                    for e in range(16):
                        q = g * 16 + e
                        sB_wge[q % 2].wait(t, 32 * (q // 2 + 1))
                        if e > 0 or g > 0:
                            sB_hw.wait(t, q * 2)
                        if gg_sel_w := (g == 0 and e == 0):
                            csel.wait(t)
                        mm = t.matmul(ps_cmb[:, :], sel16_b[:, e * 128:(e + 1) * 128],
                                      cmbT[0:16, gsl], start=True, stop=True)
                        sB_cr.inc(mm)
                        for hc in range(2):
                            qh = q * 2 + hc
                            if qh >= 2:
                                sB_hw.wait(t, qh - 1)
                            for k in range(NCH):
                                t.matmul(ps_g[:, hc, :],
                                         wge[:, k, e % 2, hc * 128:hc * 128 + 128],
                                         yb[:, k, gsl], start=(k == 0),
                                         stop=(k == NCH - 1))
                            for k in range(NCH):
                                last = (k == NCH - 1)
                                mm = t.matmul(ps_u[:, hc, :],
                                              wup[:, k, e % 2, hc * 128:hc * 128 + 128],
                                              yb[:, k, gsl], start=(k == 0),
                                              stop=last)
                                if last:
                                    sB_eh.inc(mm)

                @blk.scalar
                def _(s):
                    for f in range(16):
                        sB_fh.wait(s, g * 16 + f + 1)
                        if f >= 2:
                            sB_hs.wait(s, g * 16 + f - 1)
                        s.activation(out=silu_t[:, f % 2, :], in_=ps_g[:, f % 2, :],
                                     func=AF.Silu)
                        s.drain()
                        sB_sl.bump(s)
                    for e in range(16):
                        for hc in range(2):
                            qh = (g * 16 + e) * 2 + hc
                            sB_eh.wait(s, qh + 1)
                            if qh >= 2:
                                sB_hw.wait(s, qh - 1)
                            s.activation(out=silu_t[:, hc, :], in_=ps_g[:, hc, :],
                                         func=AF.Silu)
                            s.drain()
                            sB_esl.bump(s)

                @blk.vector
                def _(v):
                    for f in range(16):
                        sB_sl.wait(v, g * 16 + f + 1)
                        v.tensor_tensor(out=hsb[:, f, :], in0=silu_t[:, f % 2, :],
                                        in1=ps_u[:, f % 2, :], op=ALU.mult)
                        v.drain()
                        sB_hs.bump(v)
                    for e in range(16):
                        sB_cr.wait(v, g * 16 + e + 1)
                        for hc in range(2):
                            sB_esl.wait(v, (g * 16 + e) * 2 + hc + 1)
                            v.tensor_tensor(out=hw_t[:], in0=silu_t[:, hc, :],
                                            in1=ps_u[:, hc, :], op=ALU.mult)
                            v.drain()
                            v.tensor_tensor(out=hwb[:, e, hc, :], in0=hw_t[:],
                                            in1=ps_cmb[:, :], op=ALU.mult)
                            v.drain()
                            sB_hw.bump(v)

            pd = contextlib.ExitStack()
            with pd:
                ps_o = pd.enter_context(
                    nc.psum_tensor(f"pso_{tag}{g}", [128, NCH, 512], F32))

                @blk.tensor
                def _(t):
                    for f in range(16):
                        q = g * 16 + f
                        sB_wsd[q % 2].wait(t, 16 * (q // 2 + 1))
                        if f == 0:
                            sB_hs.wait(t, g * 16 + 16)
                            sB_hw.wait(t, (g + 1) * 32)
                        for cc in range(NCH):
                            mm = t.matmul(ps_o[:, cc, :],
                                          wsd[:, f % 2, cc * 128:cc * 128 + 128],
                                          hsb[:, f, :], start=(f == 0), stop=False)
                        sB_dsf.inc(mm)
                    for e in range(16):
                        q = g * 16 + e
                        sB_wdn[q % 2].wait(t, 16 * (q // 2 + 1))
                        if e == 0:
                            sB_hw.wait(t, (g * 16 + 16) * 2)
                        for hc in range(2):
                            for cc in range(NCH):
                                last = (e == 15 and hc == 1)
                                mm = t.matmul(ps_o[:, cc, :],
                                              wdn[:, hc, e % 2, cc * 128:cc * 128 + 128],
                                              hwb[:, e, hc, :],
                                              start=False, stop=last)
                        sB_dde.inc(mm)

                @blk.vector
                def _(v):
                    sB_dde.wait(v, 16 * (g + 1))
                    for cc in range(NCH):
                        v.scalar_tensor_tensor(out=x[:, cc, gsl],
                                               in0=ps_o[:, cc, :], scalar=0.5,
                                               in1=x[:, cc, gsl],
                                               op0=ALU.mult, op1=ALU.add)
                        v.drain()
                    sB_dv.bump(v)

        st.x_ready = (sB_dv.h, sB_dv.c.n)


def emit_l0attn(st, tag):
    nc, blk, x = st.nc, st.blk, st.x
    dr, wb, consts = st.dr, st.wb, st.consts
    es = st.es
    xr_h, xr_n = st.x_ready
    HTPC = TPC // 2
    HT = 8

    st.stage_reset()
    Ce, Ch, Cg = st.Ce, st.Ch, st.Cg
    c0s, c0v = Ce(), Ce()
    c1x, c1q, c1sm, c1v, c1sq, c1r, c1rep, c1h = (
        Ce(), Ce(), Ce(), Ce(), Ce(), Ce(), Ce(), Ce())
    cd1, cag1, cd1b, c15v, cdp1 = Cg(), Ce(), Cg(), Ce(), Ce()
    c2rep, c2m, c2kps, c2kev, c2vps, c2vev = Ce(), Ce(), Ce(), Ce(), Ce(), Ce()
    c2m2 = Ce()
    c2wk = [Ch(), Ch()]
    c2wv = [Ch(), Ch()]
    c3pk, c3f = Ce(), Ce()
    c3dn, c3iv, c3wv = Ce(), Ce(), Ce()
    cd3, cag3, cd3b, c3st, cdp3 = Cg(), Ce(), Cg(), Ce(), Ce()
    c4rep, c4m, c4rps, c4sg, c4rw, c4o, c4ev = (
        Ce(), Ce(), Ce(), Ce(), Ce(), Ce(), Ce())
    c4wr = [Ch(), Ch()]
    c4wo = [Ch(), Ch()]
    c5q, c5sm, c5v, c5sq, c5r, c5rep, c5h = (
        Ce(), Ce(), Ce(), Ce(), Ce(), Ce(), Ce())
    cd5, cag5, cd5b, c55v, cdp5 = Cg(), Ce(), Cg(), Ce(), Ce()
    c5m, c5rps, c5sg, c5kps, c5kr, c5kk, c5fv, c5ev = (
        Ce(), Ce(), Ce(), Ce(), Ce(), Ce(), Ce(), Ce())
    c5wr = [Ch(), Ch()]
    c5wk = [Ch(), Ch()]
    c5wv = [Ch(), Ch()]

    cin1 = st.dint(f"cin1_{tag}", [128, 128])
    cout1 = st.dint(f"cout1_{tag}", [256, 128])
    cins = [st.dint(f"cins{h}_{tag}", [128, 128]) for h in range(2)]
    couts = [st.dint(f"couts{h}_{tag}", [256, 128]) for h in range(2)]
    cin2 = st.dint(f"cin2_{tag}", [128, 128])
    cout2 = st.dint(f"cout2_{tag}", [256, 128])

    kw_r = wb["w_l0a_k"].rearrange("(k p) d -> p k d", p=128)
    vw_r = wb["w_l0a_v"].rearrange("(k p) d -> p k d", p=128)
    rw_r = wb["w_l0a_r"].rearrange("(k p) d -> p k d", p=128)
    ow_r = wb["w_l0a_o"].rearrange("(d p) c -> p d c", p=128)
    fk_r = wb["w_l0a_fk"].rearrange("(k p) f -> p k f", p=128)
    fr_r = wb["w_l0a_fr"].rearrange("(k p) d -> p k d", p=128)
    fv_r = wb["w_l0a_fv"].rearrange("(f p) c -> p f c", p=128)

    loc = contextlib.ExitStack()
    sb = lambda n, s, d: loc.enter_context(nc.sbuf_tensor(f"{n}_{tag}", s, d))
    with loc:
        xb = sb("xb", [128, NCH, TPC], BF16)
        rows = sb("rows", [96, TPC], BF16)
        dvec = sb("dvec", [128, 8], F32)
        euvec = sb("euvec", [128, 8], F32)
        d128 = sb("d128", [128, 8], F32)
        d128c = sb("d128c", [128, 8, 16], F32)
        dprof = sb("dprof", [128, NCH, 128], BF16)
        dcol = sb("dcol", [128, NCH, 128], BF16)
        zero8 = sb("zero8", [128, 8], F32)
        hlast = sb("hlast", [128, NCH, 16], BF16)
        agt1 = sb("agt1", [128, 2, NCH, 16], BF16)
        prevblk = sb("prevblk", [128, 2, NCH, 16], BF16)
        prevh = sb("prevh", [128, NCH, 16], BF16)
        carry = sb("carry", [128, 2, NCH], F32)
        rowf = sb("rowf", [1, 3 * 512], F32)

        @blk.scalar
        def _(s):
            s.wait_ge(st.cLD.h, st.ld_done)
            s.wait_ge(xr_h, xr_n)
            s.activation(out=dvec[:], in_=consts["c_decay"][:], func=AF.Exp)
            s.drain()
            s.activation(out=dvec[:], in_=dvec[:], func=AF.Exp, scale=-1.0)
            s.drain()
            s.activation(out=euvec[:], in_=consts["c_tfirst"][:], func=AF.Exp)
            s.drain()
            c0s.bump(s)

        @blk.vector
        def _(v):
            c0s.wait(v)
            p0 = contextlib.ExitStack()
            with p0:
                dpf = p0.enter_context(
                    nc.sbuf_tensor(f"dpf_{tag}", [128, 128], F32))
                for j in range(NCH):
                    v.memset(dpf[:, 0:1], 1.0)
                    v.drain()
                    ln = 1
                    while ln < 128:
                        step = min(ln, 128 - ln)
                        v.tensor_scalar_mul(out=dpf[:, ln:ln + step],
                                            in0=dpf[:, 0:step],
                                            scalar1=dpf[:, ln - 1:ln])
                        v.drain()
                        v.tensor_scalar_mul(out=dpf[:, ln:ln + step],
                                            in0=dpf[:, ln:ln + step],
                                            scalar1=dvec[:, j:j + 1])
                        v.drain()
                        ln += step
                    v.tensor_copy(out=dprof[:, j, :], in_=dpf[:])
                    v.drain()
                    v.tensor_scalar_mul(out=d128[:, j:j + 1],
                                        in0=dpf[:, 127:128],
                                        scalar1=dvec[:, j:j + 1])
                    v.drain()
                    v.memset(dcol[:, j, :], 1.0)
                    v.drain()
                    v.tensor_scalar_mul(out=dcol[:, j, :], in0=dcol[:, j, :],
                                        scalar1=dvec[:, j:j + 1])
                    v.drain()
                    v.memset(d128c[:, j, :], 1.0)
                    v.drain()
                    v.tensor_scalar_mul(out=d128c[:, j, :], in0=d128c[:, j, :],
                                        scalar1=d128[:, j:j + 1])
                    v.drain()
                v.memset(zero8[:], 0.0)
                v.drain()
            c0v.bump(v)

        p1 = contextlib.ExitStack()
        with p1:
            s1 = lambda n, s_, d: p1.enter_context(nc.sbuf_tensor(f"{n}_{tag}", s_, d))
            xcb = s1("xcb", [128, NCH, 512], BF16)
            xqb = s1("xqb", [128, NCH, 512], BF16)
            ps_st = p1.enter_context(nc.psum_tensor(f"ps1st_{tag}", [1, 1024], F32))
            ps_mr = p1.enter_context(nc.psum_tensor(f"ps1mr_{tag}", [128, 1024], F32))

            @blk.vector
            def _(v):
                v.wait_ge(xr_h, xr_n)
                for g in range(NGRP):
                    gsl = slice(g * 512, (g + 1) * 512)
                    c1sm.wait(v, g)
                    c1q.wait(v, g)
                    for k in range(NCH):
                        v.tensor_copy(out=xcb[:, k, :], in_=x[:, k, gsl])
                    v.drain()
                    c1x.bump(v)
                    c1sm.wait(v, g + 1)
                    v.tensor_scalar_mul(out=rowf[0:1, 0:512], in0=ps_st[0:1, 0:512],
                                        scalar1=1.0 / C)
                    v.drain()
                    v.tensor_mul(out=rowf[0:1, 1024:1536], in0=rowf[0:1, 0:512],
                                 in1=rowf[0:1, 0:512])
                    v.drain()
                    v.tensor_scalar_mul(out=rowf[0:1, 512:1024],
                                        in0=ps_st[0:1, 512:1024], scalar1=1.0 / C)
                    v.drain()
                    v.tensor_sub(out=rowf[0:1, 512:1024], in0=rowf[0:1, 512:1024],
                                 in1=rowf[0:1, 1024:1536])
                    v.drain()
                    v.tensor_scalar_add(out=rowf[0:1, 1024:1536],
                                        in0=rowf[0:1, 512:1024], scalar1=EPS)
                    v.drain()
                    v.tensor_scalar(out=rowf[0:1, 512:1024], in0=rowf[0:1, 512:1024],
                                    scalar1=1.0 + EPS, scalar2=EPS * EPS,
                                    op0=ALU.mult, op1=ALU.add)
                    v.drain()
                    c1v.bump(v)
                    c1sq.wait(v, g + 1)
                    v.tensor_copy(out=rows[0:1, gsl], in_=rowf[0:1, 0:512])
                    v.drain()
                    v.reciprocal(out=rowf[0:1, 0:512], in_=rowf[0:1, 512:1024])
                    v.drain()
                    v.tensor_copy(out=rows[32:33, gsl], in_=rowf[0:1, 0:512])
                    v.drain()
                    v.reciprocal(out=rowf[0:1, 0:512], in_=rowf[0:1, 1024:1536])
                    v.drain()
                    v.tensor_copy(out=rows[64:65, gsl], in_=rowf[0:1, 0:512])
                    v.drain()
                    c1r.bump(v)
                    c1rep.wait(v, g + 1)
                    for zz in range(4):
                        zt = g * 4 + zz
                        col = zz * 128 + 127
                        for k in range(NCH):
                            v.tensor_scalar(out=hlast[:, k, zt:zt + 1],
                                            in0=x[:, k, g * 512 + col:g * 512 + col + 1],
                                            scalar1=ps_mr[:, col:col + 1],
                                            scalar2=ps_mr[:, 512 + col:512 + col + 1],
                                            op0=ALU.subtract, op1=ALU.mult)
                        v.drain()
                        for k in range(NCH):
                            v.tensor_scalar(out=hlast[:, k, zt:zt + 1],
                                            in0=hlast[:, k, zt:zt + 1],
                                            scalar1=consts["c_ln1w"][:, k:k + 1],
                                            scalar2=consts["c_ln1b"][:, k:k + 1],
                                            op0=ALU.mult, op1=ALU.add)
                        v.drain()
                    c1h.bump(v)

            @blk.scalar
            def _(s):
                for g in range(NGRP):
                    c1x.wait(s, g + 1)
                    for k in range(NCH):
                        s.activation(out=xqb[:, k, :], in_=xcb[:, k, :],
                                     func=AF.Square)
                    s.drain()
                    c1q.bump(s)
                    c1v.wait(s, g + 1)
                    s.activation(out=rowf[0:1, 512:1024], in_=rowf[0:1, 512:1024],
                                 func=AF.Sqrt)
                    s.drain()
                    s.activation(out=rowf[0:1, 1024:1536], in_=rowf[0:1, 1024:1536],
                                 func=AF.Sqrt)
                    s.drain()
                    c1sq.bump(s)

            @blk.tensor
            def _(t):
                t.wait_ge(st.cKC.h, st.cKC.n)
                for g in range(NGRP):
                    c1x.wait(t, g + 1)
                    c1q.wait(t, g + 1)
                    c1r.wait(t, g)
                    for k in range(NCH):
                        t.matmul(ps_st[0:1, 0:512], st.onesc_b[:], xcb[:, k, :],
                                 start=(k == 0), stop=(k == NCH - 1))
                    for k in range(NCH):
                        last = (k == NCH - 1)
                        mm = t.matmul(ps_st[0:1, 512:1024], st.onesc_b[:],
                                      xqb[:, k, :], start=(k == 0), stop=last)
                        if last:
                            c1sm.inc(mm)
                    c1r.wait(t, g + 1)
                    c1h.wait(t, g)
                    t.matmul(ps_mr[:, 0:512], st.onesr_b[0:1, :], rows[0:1,
                             g * 512:(g + 1) * 512], start=True, stop=True)
                    mm = t.matmul(ps_mr[:, 512:1024], st.onesr_b[32:33, :],
                                  rows[32:33, g * 512:(g + 1) * 512],
                                  start=True, stop=True)
                    c1rep.inc(mm)

            @blk.gpsimd
            def _(g_):
                c1h.wait(g_, NGRP)
                g_.dma_start(out=cin1, in_=hlast.ap().rearrange("p k z -> p (k z)")
                             ).then_inc(cd1.h, 16)
                cd1.note(16)
                cd1.wait(g_)
                cc = g_.collective_compute(
                    "AllGather", ALU.bypass,
                    replica_groups=[[0, 1], [2, 3], [4, 5], [6, 7]],
                    ins=[cin1], outs=[cout1])
                cag1.inc(cc)
                cag1.wait(g_)
                g_.dma_start(out=agt1.ap().rearrange("p r k z -> p r (k z)"),
                             in_=cout1.rearrange("(r p) n -> p r n", p=128)
                             ).then_inc(cd1b.h, 16)
                cd1b.note(16)


            @blk.vector
            def _(v):
                cd1b.wait(v)
                for k in range(NCH):
                    v.memset(prevblk[:, 0, k, 0:1], 0.0)
                v.drain()
                v.tensor_copy(out=prevblk[:, 0, :, 1:16], in_=agt1[:, 1, :, 0:15])
                v.drain()
                v.tensor_copy(out=prevblk[:, 1, :, :], in_=agt1[:, 0, :, :])
                v.drain()
                v.tensor_scalar_mul(
                    out=prevh.ap().rearrange("p k z -> p (k z)"),
                    in0=prevblk.ap().rearrange("p q k z -> p (q k z)")[:, 0:NCH * 16],
                    scalar1=st.nparv[:, 0:1])
                v.drain()
                v.scalar_tensor_tensor(
                    out=prevh.ap().rearrange("p k z -> p (k z)"),
                    in0=prevblk.ap().rearrange("p q k z -> p (q k z)")
                    [:, NCH * 16:2 * NCH * 16],
                    scalar=st.parv[:, 0:1],
                    in1=prevh.ap().rearrange("p k z -> p (k z)"),
                    op0=ALU.mult, op1=ALU.add)
                v.drain()
                c15v.bump(v)
                cdp1.bump(v, 16)

        p24 = contextlib.ExitStack()
        with p24:
            s2 = lambda n, s_, d: p24.enter_context(nc.sbuf_tensor(f"{n}_{tag}", s_, d))
            kb = s2("kb", [128, NCH, HTPC], BF16)
            vb = s2("vb", [128, NCH, HTPC], BF16)
            wkv = s2("wkv", [128, NCH, HTPC], BF16)
            h_b = s2("h_b", [128, NCH, 512], BF16)
            t1b = s2("t1b", [128, 512], BF16)
            xkb = s2("xkb", [128, NCH, 512], BF16)
            srb = s2("srb", [128, 2, 512], BF16)
            wst1 = s2("wst1", [128, NCH, 2, 128], BF16)
            wst2 = s2("wst2", [128, NCH, 2, 128], BF16)
            pkf = s2("pkf", [128, HTPC], F32)
            pkv = s2("pkv", [128, HTPC], BF16)
            numb = s2("numb", [128, HTPC], BF16)
            denb = s2("denb", [128, HTPC], BF16)
            aext = s2("aext", [128, HT, 132], BF16)
            bext = s2("bext", [128, HT, 132], BF16)
            afin = s2("afin", [128, NCH, HT, 2], F32)
            agts = s2("agts", [128, 2, NCH, HT, 2], F32)
            gext = s2("gext", [128, 2, NCH, 17], F32)
            ablk = s2("ablk", [128, 2, 2, NCH, 8], F32)
            ain = s2("ain", [128, 2, NCH, 8], F32)
            ps_mr2 = p24.enter_context(nc.psum_tensor(f"ps2mr_{tag}", [128, 1536], F32))
            ps_k = p24.enter_context(nc.psum_tensor(f"ps2k_{tag}", [128, 2, 512], F32))
            ps_v = p24.enter_context(nc.psum_tensor(f"ps2v_{tag}", [128, 2, 512], F32))

            for hf in range(2):
                for g2 in range(2):
                    gg = hf * 2 + g2
                    gsl = slice(gg * 512, (gg + 1) * 512)
                    hsl = slice(g2 * 512, (g2 + 1) * 512)

                    @blk.sync
                    def _(s):
                        if gg == 0:
                            c1sm.wait(s, NGRP)
                            s.wait_ge(st.cWCa.h, st.cWCa.n)
                        for oc in range(8):
                            qi = gg * 8 + oc
                            if qi >= 2:
                                c2kps.wait(s, qi - 1)
                            s.dma_start(out=wst1[:, :, oc % 2, :],
                                        in_=kw_r[:, :, oc * 128:(oc + 1) * 128]
                                        ).then_inc(c2wk[qi % 2].h, 16)
                            c2wk[qi % 2].note(16)
                        for oc in range(8):
                            qi = gg * 8 + oc
                            if qi >= 2:
                                c2vps.wait(s, qi - 1)
                            s.dma_start(out=wst2[:, :, oc % 2, :],
                                        in_=vw_r[:, :, oc * 128:(oc + 1) * 128]
                                        ).then_inc(c2wv[qi % 2].h, 16)
                            c2wv[qi % 2].note(16)

                    @blk.tensor
                    def _(t):
                        if gg == 0:
                            t.wait_ge(st.cWCa.h, st.cWCa.n)
                        c1r.wait(t, gg + 1)
                        c2m.wait(t, gg)
                        if hf > 0 and g2 == 0:
                            c4ev.wait(t, hf * 16)
                        t.matmul(ps_mr2[:, 0:512], st.onesr_b[0:1, :], rows[0:1, gsl],
                                 start=True, stop=True)
                        mm = t.matmul(ps_mr2[:, 512:1024], st.onesr_b[32:33, :],
                                      rows[32:33, gsl], start=True, stop=True)
                        c2rep.inc(mm)
                        c2m.wait(t, gg + 1)
                        for oc in range(8):
                            qi = gg * 8 + oc
                            c2wk[qi % 2].wait(t, 16 * (qi // 2 + 1))
                            if qi >= 2:
                                c2kev.wait(t, qi - 1)
                            for ic in range(NCH):
                                last = ic == NCH - 1
                                mm = t.matmul(ps_k[:, oc % 2, :],
                                              wst1[:, ic, oc % 2, :],
                                              xkb[:, ic, :],
                                              start=(ic == 0), stop=last)
                            c2kps.inc(mm)
                        c2m2.wait(t, gg + 1)
                        for oc in range(8):
                            qi = gg * 8 + oc
                            c2wv[qi % 2].wait(t, 16 * (qi // 2 + 1))
                            if qi >= 2:
                                c2vev.wait(t, qi - 1)
                            for ic in range(NCH):
                                last = ic == NCH - 1
                                mm = t.matmul(ps_v[:, oc % 2, :],
                                              wst2[:, ic, oc % 2, :],
                                              xkb[:, ic, :],
                                              start=(ic == 0), stop=last)
                            c2vps.inc(mm)

                    @blk.vector
                    def _(v):
                        if gg == 0:
                            c1sm.wait(v, NGRP)
                        c2rep.wait(v, gg + 1)
                        cdp1.wait(v)
                        for k in range(NCH):
                            v.tensor_sub(out=t1b[:], in0=x[:, k, gsl],
                                         in1=ps_mr2[:, 0:512])
                            v.drain()
                            v.tensor_tensor(out=h_b[:, k, :], in0=t1b[:],
                                            in1=ps_mr2[:, 512:1024], op=ALU.mult)
                            v.drain()
                            v.tensor_scalar(out=h_b[:, k, :], in0=h_b[:, k, :],
                                            scalar1=consts["c_ln1w"][:, k:k + 1],
                                            scalar2=consts["c_ln1b"][:, k:k + 1],
                                            op0=ALU.mult, op1=ALU.add)
                            v.drain()
                            v.tensor_sub(out=t1b[:, 1:512], in0=h_b[:, k, 1:512],
                                         in1=h_b[:, k, 0:511])
                            v.drain()
                            for zz in range(4):
                                zt = hf * 8 + g2 * 4 + zz
                                cc0 = zz * 128
                                v.tensor_sub(out=t1b[:, cc0:cc0 + 1],
                                             in0=h_b[:, k, cc0:cc0 + 1],
                                             in1=prevh[:, k, zt:zt + 1])
                            v.drain()
                            v.scalar_tensor_tensor(
                                out=xkb[:, k, 1:512], in0=t1b[:, 1:512],
                                scalar=consts["c_tmk"][:, k:k + 1],
                                in1=h_b[:, k, 0:511],
                                op0=ALU.mult, op1=ALU.add)
                            v.drain()
                            for zz in range(4):
                                zt = hf * 8 + g2 * 4 + zz
                                cc0 = zz * 128
                                v.scalar_tensor_tensor(
                                    out=xkb[:, k, cc0:cc0 + 1],
                                    in0=t1b[:, cc0:cc0 + 1],
                                    scalar=consts["c_tmk"][:, k:k + 1],
                                    in1=prevh[:, k, zt:zt + 1],
                                    op0=ALU.mult, op1=ALU.add)
                            v.drain()
                        c2m.bump(v)
                        for oc in range(8):
                            qi = gg * 8 + oc
                            c2kps.wait(v, qi + 1)
                            v.tensor_copy(out=kb[:, oc, hsl], in_=ps_k[:, oc % 2, :])
                            v.drain()
                            c2kev.bump(v)
                        for k in range(NCH):
                            v.tensor_sub(out=t1b[:, 1:512], in0=h_b[:, k, 1:512],
                                         in1=h_b[:, k, 0:511])
                            v.drain()
                            for zz in range(4):
                                zt = hf * 8 + g2 * 4 + zz
                                cc0 = zz * 128
                                v.tensor_sub(out=t1b[:, cc0:cc0 + 1],
                                             in0=h_b[:, k, cc0:cc0 + 1],
                                             in1=prevh[:, k, zt:zt + 1])
                            v.drain()
                            v.scalar_tensor_tensor(
                                out=xkb[:, k, 1:512], in0=t1b[:, 1:512],
                                scalar=consts["c_tmv"][:, k:k + 1],
                                in1=h_b[:, k, 0:511],
                                op0=ALU.mult, op1=ALU.add)
                            v.drain()
                            for zz in range(4):
                                zt = hf * 8 + g2 * 4 + zz
                                cc0 = zz * 128
                                v.scalar_tensor_tensor(
                                    out=xkb[:, k, cc0:cc0 + 1],
                                    in0=t1b[:, cc0:cc0 + 1],
                                    scalar=consts["c_tmv"][:, k:k + 1],
                                    in1=prevh[:, k, zt:zt + 1],
                                    op0=ALU.mult, op1=ALU.add)
                            v.drain()
                        c2m2.bump(v)

                    @blk.scalar
                    def _(s):
                        for oc in range(8):
                            qi = gg * 8 + oc
                            c2vps.wait(s, qi + 1)
                            s.copy(out=vb[:, oc, hsl], in_=ps_v[:, oc % 2, :])
                            s.drain()
                            c2vev.bump(s)

                @blk.scalar
                def _(s):
                    for pas in range(2):
                        for j in range(NCH):
                            jj = (hf * 2 + pas) * 8 + j
                            c2kev.wait(s, 16 * (hf + 1))
                            c2vev.wait(s, 16 * (hf + 1))
                            if pas == 0:
                                if hf * 8 + j > 0:
                                    c3f.wait(s, hf * 8 + j)
                            else:
                                c3st.wait(s, 8 * (hf + 1))
                                if j > 0:
                                    c3wv.wait(s, hf * 8 + j)
                            s.activation(out=pkf[:], in_=kb[:, j, :], func=AF.Exp)
                            s.drain()
                            c3pk.bump(s)
                            if pas == 1:
                                c3dn.wait(s, hf * 8 + j + 1)
                                s.activation(out=pkv[:], in_=denb[:], func=AF.Ln)
                                s.drain()
                                s.activation(out=pkv[:], in_=pkv[:], func=AF.Exp,
                                             scale=-1.0)
                                s.drain()
                                c3iv.bump(s)

                @blk.vector
                def _(v):
                    def chunk_scans(pas, j):
                        jj = (hf * 2 + pas) * 8 + j
                        c3pk.wait(v, jj + 1)
                        v.tensor_tensor(out=pkv[:], in0=pkf[:],
                                        in1=vb[:, j, :], op=ALU.mult)
                        v.drain()
                        v.memset(aext[:, :, 0:1], 0.0)
                        v.memset(bext[:, :, 0:1], 0.0)
                        v.drain()
                        for z in range(HT):
                            zsl = slice(z * 128, (z + 1) * 128)
                            v.tensor_tensor_scan(
                                out=aext[:, z, 1:129], data0=dcol[:, j, :],
                                data1=pkv[:, zsl], initial=0.0,
                                op0=ALU.mult, op1=ALU.add)
                            v.tensor_tensor_scan(
                                out=bext[:, z, 1:129], data0=dcol[:, j, :],
                                data1=pkf[:, zsl], initial=0.0,
                                op0=ALU.mult, op1=ALU.add)
                        v.drain()

                    for j in range(NCH):
                        chunk_scans(0, j)
                        v.tensor_copy(out=afin[:, j, :, 0], in_=aext[:, :, 128])
                        v.tensor_copy(out=afin[:, j, :, 1], in_=bext[:, :, 128])
                        v.drain()
                        c3f.bump(v)
                    cd3b.wait(v, 16 * (hf + 1))
                    for j in range(NCH):
                        for ab in range(2):
                            v.tensor_copy(out=numb[:, 0:16:2],
                                          in_=agts[:, 0, j, :, ab])
                            v.tensor_copy(out=numb[:, 1:16:2],
                                          in_=agts[:, 1, j, :, ab])
                            v.drain()
                            ini = (zero8[:, j:j + 1] if hf == 0
                                   else carry[:, ab, j:j + 1])
                            v.tensor_tensor_scan(
                                out=gext[:, ab, j, 1:17], data0=d128c[:, j, :],
                                data1=numb[:, 0:16], initial=ini,
                                op0=ALU.mult, op1=ALU.add)
                            v.drain()
                            v.tensor_copy(out=gext[:, ab, j, 0:1], in_=ini)
                            v.drain()
                            v.tensor_copy(out=carry[:, ab, j:j + 1],
                                          in_=gext[:, ab, j, 16:17])
                            v.drain()
                            for p_ in range(2):
                                v.tensor_copy(out=ablk[:, p_, ab, j, :],
                                              in_=gext[:, ab, j, p_:16:2])
                            v.drain()
                    v.tensor_scalar_mul(
                        out=ain.ap().rearrange("p a k z -> p (a k z)"),
                        in0=ablk.ap().rearrange("p q a k z -> p (q a k z)")
                        [:, 0:2 * NCH * 8],
                        scalar1=st.nparv[:, 0:1])
                    v.drain()
                    v.scalar_tensor_tensor(
                        out=ain.ap().rearrange("p a k z -> p (a k z)"),
                        in0=ablk.ap().rearrange("p q a k z -> p (q a k z)")
                        [:, 2 * NCH * 8:4 * NCH * 8],
                        scalar=st.parv[:, 0:1],
                        in1=ain.ap().rearrange("p a k z -> p (a k z)"),
                        op0=ALU.mult, op1=ALU.add)
                    v.drain()
                    c3st.bump(v, 8)
                    for j in range(NCH):
                        chunk_scans(1, j)
                        for z in range(HT):
                            zsl = slice(z * 128, (z + 1) * 128)
                            v.scalar_tensor_tensor(
                                out=numb[:, zsl], in0=dprof[:, j, :],
                                scalar=ain[:, 0, j, z:z + 1],
                                in1=aext[:, z, 0:128],
                                op0=ALU.mult, op1=ALU.add)
                            v.scalar_tensor_tensor(
                                out=denb[:, zsl], in0=dprof[:, j, :],
                                scalar=ain[:, 1, j, z:z + 1],
                                in1=bext[:, z, 0:128],
                                op0=ALU.mult, op1=ALU.add)
                        v.drain()
                        v.tensor_scalar_mul(out=pkf[:], in0=pkf[:],
                                            scalar1=euvec[:, j:j + 1])
                        v.drain()
                        v.tensor_tensor(out=pkv[:], in0=pkf[:],
                                        in1=vb[:, j, :], op=ALU.mult)
                        v.drain()
                        v.tensor_add(out=numb[:], in0=numb[:], in1=pkv[:])
                        v.drain()
                        v.tensor_add(out=denb[:], in0=denb[:], in1=pkf[:])
                        v.drain()
                        c3dn.bump(v)
                        c3iv.wait(v, hf * 8 + j + 1)
                        v.tensor_tensor(out=wkv[:, j, :], in0=numb[:],
                                        in1=pkv[:], op=ALU.mult)
                        v.drain()
                        c3wv.bump(v)

                @blk.gpsimd
                def _(g_):
                    c3f.wait(g_, 8 * (hf + 1))
                    g_.dma_start(out=cins[hf],
                                 in_=afin.ap().rearrange("p k z a -> p (k z a)")
                                 ).then_inc(cd3.h, 16)
                    cd3.note(16)
                    cd3.wait(g_)
                    cc = g_.collective_compute(
                        "AllGather", ALU.bypass,
                        replica_groups=[[0, 1], [2, 3], [4, 5], [6, 7]],
                        ins=[cins[hf]], outs=[couts[hf]])
                    cag3.inc(cc)
                    cag3.wait(g_)
                    g_.dma_start(out=agts.ap().rearrange("p r k z a -> p r (k z a)"),
                                 in_=couts[hf].rearrange("(r p) n -> p r n", p=128)
                                 ).then_inc(cd3b.h, 16)
                    cd3b.note(16)


                for g2 in range(2):
                    gg = hf * 2 + g2
                    gsl = slice(gg * 512, (gg + 1) * 512)
                    hsl = slice(g2 * 512, (g2 + 1) * 512)

                    @blk.sync
                    def _(s):
                        for oc in range(8):
                            qi = gg * 8 + oc
                            if qi >= 2:
                                c4rps.wait(s, qi - 1)
                            s.dma_start(out=wst1[:, :, oc % 2, :],
                                        in_=rw_r[:, :, oc * 128:(oc + 1) * 128]
                                        ).then_inc(c4wr[qi % 2].h, 16)
                            c4wr[qi % 2].note(16)
                        for cc_ in range(8):
                            qi = gg * 8 + cc_
                            if qi >= 2:
                                c4o.wait(s, qi - 1)
                            s.dma_start(out=wst2[:, :, cc_ % 2, :],
                                        in_=ow_r[:, :, cc_ * 128:(cc_ + 1) * 128]
                                        ).then_inc(c4wo[qi % 2].h, 16)
                            c4wo[qi % 2].note(16)

                    @blk.tensor
                    def _(t):
                        c4m.wait(t, gg)
                        c4ev.wait(t, gg * 8)
                        if g2 == 0:
                            c2m.wait(t, (hf + 1) * 2)
                            c3wv.wait(t, 8 * (hf + 1))
                        t.matmul(ps_mr2[:, 0:512], st.onesr_b[0:1, :], rows[0:1, gsl],
                                 start=True, stop=True)
                        t.matmul(ps_mr2[:, 512:1024], st.onesr_b[32:33, :],
                                 rows[32:33, gsl], start=True, stop=True)
                        mm = t.matmul(ps_mr2[:, 1024:1536], st.onesr_b[64:65, :],
                                      rows[64:65, gsl], start=True, stop=True)
                        c4rep.inc(mm)
                        c4m.wait(t, gg + 1)
                        for oc in range(8):
                            qi = gg * 8 + oc
                            c4wr[qi % 2].wait(t, 16 * (qi // 2 + 1))
                            if qi >= 2:
                                c4sg.wait(t, qi - 1)
                            for ic in range(NCH):
                                mm = t.matmul(ps_k[:, oc % 2, :],
                                              wst1[:, ic, oc % 2, :],
                                              xkb[:, ic, :], start=(ic == 0),
                                              stop=(ic == NCH - 1))
                            c4rps.inc(mm)
                        for cc_ in range(8):
                            qi = gg * 8 + cc_
                            c4wo[qi % 2].wait(t, 16 * (qi // 2 + 1))
                            c4rw.wait(t, gg * 8 + 8)
                            if qi >= 2:
                                c4ev.wait(t, qi - 1)
                            for ic in range(NCH):
                                mm = t.matmul(ps_v[:, cc_ % 2, :],
                                              wst2[:, ic, cc_ % 2, :],
                                              wkv[:, ic, hsl], start=(ic == 0),
                                              stop=(ic == NCH - 1))
                            c4o.inc(mm)

                    @blk.vector
                    def _(v):
                        c4rep.wait(v, gg + 1)
                        for k in range(NCH):
                            v.tensor_sub(out=t1b[:], in0=x[:, k, gsl],
                                         in1=ps_mr2[:, 0:512])
                            v.drain()
                            v.tensor_tensor(out=h_b[:, k, :], in0=t1b[:],
                                            in1=ps_mr2[:, 512:1024], op=ALU.mult)
                            v.drain()
                            v.tensor_scalar(out=h_b[:, k, :], in0=h_b[:, k, :],
                                            scalar1=consts["c_ln1w"][:, k:k + 1],
                                            scalar2=consts["c_ln1b"][:, k:k + 1],
                                            op0=ALU.mult, op1=ALU.add)
                            v.drain()
                            v.tensor_sub(out=t1b[:, 1:512], in0=h_b[:, k, 1:512],
                                         in1=h_b[:, k, 0:511])
                            v.drain()
                            for zz in range(4):
                                zt = hf * 8 + g2 * 4 + zz
                                cc0 = zz * 128
                                v.tensor_sub(out=t1b[:, cc0:cc0 + 1],
                                             in0=h_b[:, k, cc0:cc0 + 1],
                                             in1=prevh[:, k, zt:zt + 1])
                            v.drain()
                            v.scalar_tensor_tensor(
                                out=xkb[:, k, 1:512], in0=t1b[:, 1:512],
                                scalar=consts["c_tmr"][:, k:k + 1],
                                in1=h_b[:, k, 0:511], op0=ALU.mult, op1=ALU.add)
                            v.drain()
                            for zz in range(4):
                                zt = hf * 8 + g2 * 4 + zz
                                cc0 = zz * 128
                                v.scalar_tensor_tensor(
                                    out=xkb[:, k, cc0:cc0 + 1],
                                    in0=t1b[:, cc0:cc0 + 1],
                                    scalar=consts["c_tmr"][:, k:k + 1],
                                    in1=prevh[:, k, zt:zt + 1],
                                    op0=ALU.mult, op1=ALU.add)
                            v.drain()
                        c4m.bump(v)
                        for oc in range(8):
                            qi = gg * 8 + oc
                            c4sg.wait(v, qi + 1)
                            v.tensor_tensor(out=wkv[:, oc, hsl],
                                            in0=wkv[:, oc, hsl],
                                            in1=srb[:, oc % 2, :], op=ALU.mult)
                            v.drain()
                            c4rw.bump(v)
                        for cc_ in range(8):
                            qi = gg * 8 + cc_
                            c4o.wait(v, qi + 1)
                            v.tensor_sub(out=t1b[:], in0=x[:, cc_, gsl],
                                         in1=ps_mr2[:, 0:512])
                            v.drain()
                            v.tensor_tensor(out=t1b[:], in0=t1b[:],
                                            in1=ps_mr2[:, 1024:1536], op=ALU.mult)
                            v.drain()
                            v.tensor_tensor(out=xb[:, cc_, gsl], in0=t1b[:],
                                            in1=ps_v[:, cc_ % 2, :], op=ALU.add)
                            v.drain()
                            v.tensor_tensor(out=x[:, cc_, gsl], in0=x[:, cc_, gsl],
                                            in1=xb[:, cc_, gsl], op=ALU.add)
                            v.drain()
                            c4ev.bump(v)

                    @blk.scalar
                    def _(s):
                        for oc in range(8):
                            qi = gg * 8 + oc
                            c4rps.wait(s, qi + 1)
                            if qi >= 2:
                                c4rw.wait(s, qi - 1)
                            s.activation(out=srb[:, oc % 2, :], in_=ps_k[:, oc % 2, :],
                                         func=AF.Sigmoid)
                            s.drain()
                            c4sg.bump(s)

        p5 = contextlib.ExitStack()
        with p5:
            s5 = lambda n, s_, d: p5.enter_context(nc.sbuf_tensor(f"{n}_{tag}", s_, d))
            xqb5 = s5("xqb5", [128, NCH, 512], BF16)
            kk_b = s5("kk_b", [128, 32, 512], BF16)
            sr2b = s5("sr2b", [128, NCH, 512], BF16)
            h2b = s5("h2b", [128, NCH, 512], BF16)
            t1b5 = s5("t1b5", [128, 512], BF16)
            tf5 = s5("tf5", [128, 512], F32)
            xk2b = s5("xk2b", [128, NCH, 512], BF16)
            xr2b = s5("xr2b", [128, NCH, 512], BF16)
            wfr = s5("wfr", [128, NCH, 2, 128], BF16)
            wfk = s5("wfk", [128, NCH, 2, 128], BF16)
            wfv = s5("wfv", [128, 2, 512], BF16)
            ps_mr5 = p5.enter_context(nc.psum_tensor(f"ps5mr_{tag}", [128, 1024], F32))
            p5s = contextlib.ExitStack()
            ps_st5 = p5s.enter_context(nc.psum_tensor(f"ps5st_{tag}", [1, 1024], F32))

            @blk.vector
            def _(v):
                c4ev.wait(v)
                for g in range(NGRP):
                    gsl = slice(g * 512, (g + 1) * 512)
                    c5sm.wait(v, g + 1)
                    v.tensor_scalar_mul(out=rowf[0:1, 0:512], in0=ps_st5[0:1, 0:512],
                                        scalar1=1.0 / C)
                    v.drain()
                    v.tensor_mul(out=rowf[0:1, 1024:1536], in0=rowf[0:1, 0:512],
                                 in1=rowf[0:1, 0:512])
                    v.drain()
                    v.tensor_scalar_mul(out=rowf[0:1, 512:1024],
                                        in0=ps_st5[0:1, 512:1024], scalar1=1.0 / C)
                    v.drain()
                    v.tensor_sub(out=rowf[0:1, 512:1024], in0=rowf[0:1, 512:1024],
                                 in1=rowf[0:1, 1024:1536])
                    v.drain()
                    v.tensor_scalar_add(out=rowf[0:1, 512:1024],
                                        in0=rowf[0:1, 512:1024], scalar1=EPS)
                    v.drain()
                    c5v.bump(v)
                    c5sq.wait(v, g + 1)
                    v.tensor_copy(out=rows[0:1, gsl], in_=rowf[0:1, 0:512])
                    v.drain()
                    v.reciprocal(out=rowf[0:1, 0:512], in_=rowf[0:1, 512:1024])
                    v.drain()
                    v.tensor_copy(out=rows[32:33, gsl], in_=rowf[0:1, 0:512])
                    v.drain()
                    c5r.bump(v)
                    c5rep.wait(v, g + 1)
                    for zz in range(4):
                        zt = g * 4 + zz
                        col = zz * 128 + 127
                        for k in range(NCH):
                            v.tensor_scalar(out=hlast[:, k, zt:zt + 1],
                                            in0=xb[:, k, g * 512 + col:g * 512 + col + 1],
                                            scalar1=ps_mr5[:, col:col + 1],
                                            scalar2=ps_mr5[:, 512 + col:512 + col + 1],
                                            op0=ALU.subtract, op1=ALU.mult)
                        v.drain()
                        for k in range(NCH):
                            v.tensor_scalar(out=hlast[:, k, zt:zt + 1],
                                            in0=hlast[:, k, zt:zt + 1],
                                            scalar1=consts["c_ln2w"][:, k:k + 1],
                                            scalar2=consts["c_ln2b"][:, k:k + 1],
                                            op0=ALU.mult, op1=ALU.add)
                        v.drain()
                    c5h.bump(v)

            @blk.scalar
            def _(s):
                c4ev.wait(s)
                for g in range(NGRP):
                    gsl = slice(g * 512, (g + 1) * 512)
                    if g > 0:
                        c5sm.wait(s, g)
                    for k in range(NCH):
                        s.activation(out=xqb5[:, k, :], in_=xb[:, k, gsl],
                                     func=AF.Square)
                    s.drain()
                    c5q.bump(s)
                    c5v.wait(s, g + 1)
                    s.activation(out=rowf[0:1, 512:1024], in_=rowf[0:1, 512:1024],
                                 func=AF.Sqrt)
                    s.drain()
                    c5sq.bump(s)

            @blk.tensor
            def _(t):
                c4ev.wait(t)
                for g in range(NGRP):
                    gsl = slice(g * 512, (g + 1) * 512)
                    c5q.wait(t, g + 1)
                    c5r.wait(t, g)
                    for k in range(NCH):
                        t.matmul(ps_st5[0:1, 0:512], st.onesc_b[:], xb[:, k, gsl],
                                 start=(k == 0), stop=(k == NCH - 1))
                    for k in range(NCH):
                        last = k == NCH - 1
                        mm = t.matmul(ps_st5[0:1, 512:1024], st.onesc_b[:],
                                      xqb5[:, k, :], start=(k == 0), stop=last)
                        if last:
                            c5sm.inc(mm)
                    c5r.wait(t, g + 1)
                    c5h.wait(t, g)
                    t.matmul(ps_mr5[:, 0:512], st.onesr_b[0:1, :], rows[0:1, gsl],
                             start=True, stop=True)
                    mm = t.matmul(ps_mr5[:, 512:1024], st.onesr_b[32:33, :],
                                  rows[32:33, gsl], start=True, stop=True)
                    c5rep.inc(mm)

            @blk.gpsimd
            def _(g_):
                c5h.wait(g_, NGRP)
                g_.dma_start(out=cin2, in_=hlast.ap().rearrange("p k z -> p (k z)")
                             ).then_inc(cd5.h, 16)
                cd5.note(16)
                cd5.wait(g_)
                cc = g_.collective_compute(
                    "AllGather", ALU.bypass,
                    replica_groups=[[0, 1], [2, 3], [4, 5], [6, 7]],
                    ins=[cin2], outs=[cout2])
                cag5.inc(cc)
                cag5.wait(g_)
                g_.dma_start(out=agt1.ap().rearrange("p r k z -> p r (k z)"),
                             in_=cout2.rearrange("(r p) n -> p r n", p=128)
                             ).then_inc(cd5b.h, 16)
                cd5b.note(16)


            @blk.vector
            def _(v):
                cd5b.wait(v)
                for k in range(NCH):
                    v.memset(prevblk[:, 0, k, 0:1], 0.0)
                v.drain()
                v.tensor_copy(out=prevblk[:, 0, :, 1:16], in_=agt1[:, 1, :, 0:15])
                v.drain()
                v.tensor_copy(out=prevblk[:, 1, :, :], in_=agt1[:, 0, :, :])
                v.drain()
                v.tensor_scalar_mul(
                    out=prevh.ap().rearrange("p k z -> p (k z)"),
                    in0=prevblk.ap().rearrange("p q k z -> p (q k z)")[:, 0:NCH * 16],
                    scalar1=st.nparv[:, 0:1])
                v.drain()
                v.scalar_tensor_tensor(
                    out=prevh.ap().rearrange("p k z -> p (k z)"),
                    in0=prevblk.ap().rearrange("p q k z -> p (q k z)")
                    [:, NCH * 16:2 * NCH * 16],
                    scalar=st.parv[:, 0:1],
                    in1=prevh.ap().rearrange("p k z -> p (k z)"),
                    op0=ALU.mult, op1=ALU.add)
                v.drain()
                c55v.bump(v)
                cdp5.bump(v, 16)

            p5s.close()
            for g in range(NGRP):
                gsl = slice(g * 512, (g + 1) * 512)
                pg = contextlib.ExitStack()
                ps_r5 = pg.enter_context(
                    nc.psum_tensor(f"ps5r_{tag}{g}", [128, 2, 512], F32))
                ps_kk = pg.enter_context(
                    nc.psum_tensor(f"ps5kk_{tag}{g}", [128, 2, 512], F32))

                @blk.sync
                def _(s):
                    if g == 0:
                        c4ev.wait(s, 32)
                    for oc in range(8):
                        qi = g * 8 + oc
                        if qi >= 2:
                            c5rps.wait(s, qi - 1)
                        s.dma_start(out=wfr[:, :, oc % 2, :],
                                    in_=fr_r[:, :, oc * 128:(oc + 1) * 128]
                                    ).then_inc(c5wr[qi % 2].h, 16)
                        c5wr[qi % 2].note(16)
                    for f in range(32):
                        qi = g * 32 + f
                        if qi >= 2:
                            c5kps.wait(s, qi - 1)
                        s.dma_start(out=wfk[:, :, f % 2, :],
                                    in_=fk_r[:, :, f * 128:(f + 1) * 128]
                                    ).then_inc(c5wk[qi % 2].h, 16)
                        c5wk[qi % 2].note(16)

                @blk.tensor
                def _(t):
                    c5r.wait(t, NGRP)
                    c5h.wait(t, NGRP)
                    c5m.wait(t, g)
                    if g == 0:
                        c5ev.wait(t, 0)
                    else:
                        c5ev.wait(t, (g - 1) * 8 + 8)
                    t.matmul(ps_mr5[:, 0:512], st.onesr_b[0:1, :], rows[0:1, gsl],
                             start=True, stop=True)
                    mm = t.matmul(ps_mr5[:, 512:1024], st.onesr_b[32:33, :],
                                  rows[32:33, gsl], start=True, stop=True)
                    c5rep.inc(mm)
                    c5m.wait(t, g + 1)
                    for oc in range(8):
                        qi = g * 8 + oc
                        c5wr[qi % 2].wait(t, 16 * (qi // 2 + 1))
                        if qi >= 2:
                            c5sg.wait(t, qi - 1)
                        for ic in range(NCH):
                            mm = t.matmul(ps_r5[:, oc % 2, :],
                                          wfr[:, ic, oc % 2, :],
                                          xr2b[:, ic, :], start=(ic == 0),
                                          stop=(ic == NCH - 1))
                        c5rps.inc(mm)
                    for f in range(32):
                        qi = g * 32 + f
                        c5wk[qi % 2].wait(t, 16 * (qi // 2 + 1))
                        if qi >= 2:
                            c5kr.wait(t, qi - 1)
                        for ic in range(NCH):
                            mm = t.matmul(ps_kk[:, f % 2, :],
                                          wfk[:, ic, f % 2, :],
                                          xk2b[:, ic, :], start=(ic == 0),
                                          stop=(ic == NCH - 1))
                        c5kps.inc(mm)

                @blk.vector
                def _(v):
                    c5rep.wait(v, NGRP + g + 1)
                    cdp5.wait(v)
                    for k in range(NCH):
                        v.tensor_tensor(out=tf5[:], in0=xb[:, k, gsl],
                                        in1=ps_mr5[:, 0:512], op=ALU.subtract)
                        v.drain()
                        v.tensor_tensor(out=h2b[:, k, :], in0=tf5[:],
                                        in1=ps_mr5[:, 512:1024], op=ALU.mult)
                        v.drain()
                        v.tensor_scalar(out=h2b[:, k, :], in0=h2b[:, k, :],
                                        scalar1=consts["c_ln2w"][:, k:k + 1],
                                        scalar2=consts["c_ln2b"][:, k:k + 1],
                                        op0=ALU.mult, op1=ALU.add)
                        v.drain()
                        v.tensor_sub(out=t1b5[:, 1:512], in0=h2b[:, k, 1:512],
                                     in1=h2b[:, k, 0:511])
                        v.drain()
                        for zz in range(4):
                            zt = g * 4 + zz
                            cc0 = zz * 128
                            v.tensor_sub(out=t1b5[:, cc0:cc0 + 1],
                                         in0=h2b[:, k, cc0:cc0 + 1],
                                         in1=prevh[:, k, zt:zt + 1])
                        v.drain()
                        for dst, tm in ((xk2b, "c_ftmk"), (xr2b, "c_ftmr")):
                            v.scalar_tensor_tensor(
                                out=dst[:, k, 1:512], in0=t1b5[:, 1:512],
                                scalar=consts[tm][:, k:k + 1],
                                in1=h2b[:, k, 0:511], op0=ALU.mult, op1=ALU.add)
                            v.drain()
                            for zz in range(4):
                                zt = g * 4 + zz
                                cc0 = zz * 128
                                v.scalar_tensor_tensor(
                                    out=dst[:, k, cc0:cc0 + 1],
                                    in0=t1b5[:, cc0:cc0 + 1],
                                    scalar=consts[tm][:, k:k + 1],
                                    in1=prevh[:, k, zt:zt + 1],
                                    op0=ALU.mult, op1=ALU.add)
                            v.drain()
                    c5m.bump(v)
                    for f in range(32):
                        qi = g * 32 + f
                        c5kps.wait(v, qi + 1)
                        if qi > 0:
                            c5kk.wait(v, qi)
                        v.tensor_scalar_max(out=tf5[:], in0=ps_kk[:, f % 2, :],
                                            scalar1=0.0)
                        v.drain()
                        c5kr.bump(v)

                @blk.scalar
                def _(s):
                    for oc in range(8):
                        qi = g * 8 + oc
                        c5rps.wait(s, qi + 1)
                        s.activation(out=sr2b[:, oc, :], in_=ps_r5[:, oc % 2, :],
                                     func=AF.Sigmoid)
                        s.drain()
                        c5sg.bump(s)
                    for f in range(32):
                        qi = g * 32 + f
                        c5kr.wait(s, qi + 1)
                        s.activation(out=kk_b[:, f, :], in_=tf5[:], func=AF.Square)
                        s.drain()
                        c5kk.bump(s)

                pg.close()
                pf = contextlib.ExitStack()
                with pf:
                    ps_fv = pf.enter_context(
                        nc.psum_tensor(f"ps5fv_{tag}{g}", [128, 4, 512], F32))

                    @blk.sync
                    def _(s):
                        for pas in range(2):
                            for f in range(32):
                                qi = (g * 2 + pas) * 32 + f
                                if qi >= 2:
                                    c5fv.wait(s, qi - 1)
                                s.dma_start(
                                    out=wfv[:, f % 2, :],
                                    in_=fv_r[:, f, pas * 512:(pas + 1) * 512]
                                    ).then_inc(c5wv[qi % 2].h, 16)
                                c5wv[qi % 2].note(16)

                    @blk.tensor
                    def _(t):
                        c5kk.wait(t, (g + 1) * 32)
                        for pas in range(2):
                            if g > 0 or pas > 0:
                                c5ev.wait(t, g * 8 + pas * 4)
                            for f in range(32):
                                qi = (g * 2 + pas) * 32 + f
                                c5wv[qi % 2].wait(t, 16 * (qi // 2 + 1))
                                for oc4 in range(4):
                                    last = f == 31
                                    mm = t.matmul(
                                        ps_fv[:, oc4, :],
                                        wfv[:, f % 2, oc4 * 128:(oc4 + 1) * 128],
                                        kk_b[:, f, :], start=(f == 0), stop=last)
                                c5fv.inc(mm)

                    @blk.vector
                    def _(v):
                        for pas in range(2):
                            c5fv.wait(v, (g * 2 + pas) * 32 + 32)
                            for oc4 in range(4):
                                cc_ = pas * 4 + oc4
                                c5sg.wait(v, g * 8 + 8)
                                v.tensor_tensor(out=tf5[:], in0=sr2b[:, cc_, :],
                                                in1=ps_fv[:, oc4, :], op=ALU.mult)
                                v.drain()
                                v.tensor_tensor(out=x[:, cc_, gsl],
                                                in0=x[:, cc_, gsl], in1=tf5[:],
                                                op=ALU.add)
                                v.drain()
                                c5ev.bump(v)

    st.x_ready = (c5ev.h, c5ev.c.n)


def emit_l1ffn(st, tag):
    nc, blk, x = st.nc, st.blk, st.x
    dr, wb, consts = st.dr, st.wb, st.consts
    CCAP = float(int(1.25 * (B * T) / 8))
    xr_h, xr_n = st.x_ready
    wcB = st.cWC1.n

    st.stage_reset()
    Ce, Ch, Cg = st.Ce, st.Ch, st.Cg
    fA_x, fA_q, fA_sm, fA_v, fA_sq, fA_r, fA_rep, fA_y = (
        Ce(), Ce(), Ce(), Ce(), Ce(), Ce(), Ce(), Ce())
    fLG, fVm, fEX, fRT, fCNT = Ce(), Ce(), Ce(), Ce(), Ce()
    fVcnt, fAG, fVag, fBase, fThr = Ce(), Ce(), Ce(), Ce(), Ce()
    fCum, fGb, fTp, fGT = Ce(), Ce(), Ce(), Ce()
    fDcnt, fDag, fSel = Cg(), Cg(), Cg()
    fLB, fB2w, fWgt = Ch(), Ch(), Ch()
    fB2, fAcc, fUp, fRelu, fGrpP, fGrpV, fDn, fEvac, fXup = (
        Ce(), Ce(), Ce(), Ce(), Ce(), Ce(), Ce(), Ce(), Ce())
    fW1 = [Ch(), Ch()]
    fW2 = [Cg(), Cg()]

    cin_f = st.dint(f"cinf_{tag}", [1, 128])
    cout_f = st.dint(f"coutf_{tag}", [8, 128])

    loc = contextlib.ExitStack()
    sb = lambda n, s, d: loc.enter_context(nc.sbuf_tensor(f"{n}_{tag}", s, d))
    with loc:
        yn = sb("yn", [128, NCH, TPC], BF16)
        gT_b = sb("gT", [8, TPC], BF16)
        h_b = sb("hb1", [128, 32, 512], BF16)
        acc = sb("acc1", [128, NCH, 512], F32)
        grep_s = sb("grep", [128, 512], F32)
        sel16_b = sb("sel16f", [16, 16 * 128], BF16)
        wgt = sb("wgt1", [128, NCH, 8], BF16)
        b1_s = sb("b1s", [128, 8, 32], F32)
        b2w = sb("b2w", [8, C], BF16)
        w1s = sb("w1s", [128, NCH, 2, 128], BF16)
        w2s = sb("w2s", [128, 2, C], BF16)
        wte_all = sb("wte", [128, NTILE, 8], F32)
        mask_b = sb("mskb", [128, NTILE, 8], BF16)
        agcnt_s = sb("agc", [128, 8], F32)
        agcnt_b = sb("agcb", [128, 8], BF16)
        thr_s = sb("thr", [16, 8], F32)
        thr_b = sb("thrb", [16, 8], BF16)
        cntrow = sb("cntr", [1, 128], F32)
        lgf = sb("lgf", [128, 2, 8], F32)
        expd = sb("expd", [128, 2, 8], F32)
        nmax = sb("nmax", [128, 2, 1], F32)
        prb = sb("prb", [128, 8], F32)
        t8f = sb("t8f", [128, 8], F32)
        sel_f = sb("self", [128, 8], F32)
        wsm = sb("wsm", [128, 1], F32)
        g_b = sb("gb1", [128, 2, 8], BF16)
        tfv = sb("tfv1", [128, 512], F32)

        w1_r = wb["w_l1f_w1"].rearrange("e (k p) h -> e p k h", p=128)
        w2_r = wb["w_l1f_w2"].rearrange("e (h p) c -> e p h c", p=128)

        @blk.sync
        def _(s):
            s.wait_ge(xr_h, xr_n)
            s.wait_ge(st.cWC1.h, wcB)
            s.dma_start(out=wgt[:], in_=wb["w_l1f_gate"]
                        .rearrange("(k p) e -> p k e", p=128)).then_inc(fWgt.h, 16)
            fWgt.note(16)
            s.dma_start(out=b1_s[:], in_=dr["c_l1b1"]
                        .rearrange("e p h -> p e h")).then_inc(fLB.h, 16)
            fLB.note(16)
            s.dma_start(out=b2w[:], in_=wb["r_l1b2"]).then_inc(fB2w.h, 16)
            fB2w.note(16)

        @blk.gpsimd
        def _(g_):
            g_.wait_ge(xr_h, xr_n)
            g_.dma_start(out=sel16_b[:], in_=dr["k_sel16"]).then_inc(fSel.h, 16)
            fSel.note(16)

        pa = contextlib.ExitStack()
        with pa:
            sba = lambda n, s, d: pa.enter_context(
                nc.sbuf_tensor(f"{n}_{tag}", s, d))
            mrow = sba("mrowf", [1, 512], F32)
            trow = sba("trowf", [1, 1024], F32)
            r0row = sba("r0rowf", [1, 512], F32)
            mrow_b = sba("mrowfb", [1, 512], BF16)
            r0row_b = sba("r0rowfb", [1, 512], BF16)
            xq = sba("xqf", [128, NCH, 512], BF16)
            ps_st = pa.enter_context(nc.psum_tensor(f"fpst_{tag}", [1, 1024], F32))
            ps_mr = pa.enter_context(nc.psum_tensor(f"fpmr_{tag}", [128, 1024], F32))

            @blk.vector
            def _(v):
                v.wait_ge(xr_h, xr_n)
                for g in range(NGRP):
                    gsl = slice(g * 512, (g + 1) * 512)
                    xsb = h_b[:, 0:NCH, :]
                    fA_sm.wait(v, g)
                    for k in range(NCH):
                        v.tensor_copy(out=xsb[:, k, :], in_=x[:, k, gsl])
                    v.drain()
                    fA_x.bump(v)
                    fA_sm.wait(v, g + 1)
                    v.tensor_scalar_mul(out=mrow[:], in0=ps_st[0:1, 0:512],
                                        scalar1=1.0 / C)
                    v.tensor_scalar_mul(out=trow[0:1, 0:512],
                                        in0=ps_st[0:1, 512:1024], scalar1=1.0 / C)
                    v.drain()
                    v.tensor_mul(out=trow[0:1, 512:1024], in0=mrow[:], in1=mrow[:])
                    v.drain()
                    v.tensor_sub(out=trow[0:1, 0:512], in0=trow[0:1, 0:512],
                                 in1=trow[0:1, 512:1024])
                    v.drain()
                    v.tensor_scalar_add(out=trow[0:1, 512:1024],
                                        in0=trow[0:1, 0:512], scalar1=EPS)
                    v.drain()
                    fA_v.bump(v)
                    fA_sq.wait(v, g + 1)
                    v.reciprocal(out=r0row[:], in_=trow[0:1, 512:1024])
                    v.drain()
                    v.tensor_copy(out=mrow_b[:], in_=mrow[:])
                    v.tensor_copy(out=r0row_b[:], in_=r0row[:])
                    v.drain()
                    fA_r.bump(v)
                    fA_rep.wait(v, g + 1)
                    for k in range(NCH):
                        v.tensor_sub(out=tfv[:], in0=x[:, k, gsl],
                                     in1=ps_mr[:, 0:512])
                        v.drain()
                        v.tensor_tensor(out=yn[:, k, gsl], in0=tfv[:],
                                        in1=ps_mr[:, 512:1024], op=ALU.mult)
                        v.drain()
                    fA_y.bump(v)

            @blk.scalar
            def _(s):
                for g in range(NGRP):
                    fA_x.wait(s, g + 1)
                    xsb = h_b[:, 0:NCH, :]
                    fA_sm.wait(s, g)
                    for k in range(NCH):
                        s.activation(out=xq[:, k, :], in_=xsb[:, k, :],
                                     func=AF.Square)
                    s.drain()
                    fA_q.bump(s)
                    fA_v.wait(s, g + 1)
                    s.activation(out=trow[0:1, 512:1024], in_=trow[0:1, 512:1024],
                                 func=AF.Sqrt)
                    s.drain()
                    fA_sq.bump(s)

            @blk.tensor
            def _(t):
                t.wait_ge(st.cKC.h, st.cKC.n)
                for g in range(NGRP):
                    gsl = slice(g * 512, (g + 1) * 512)
                    xsb = h_b[:, 0:NCH, :]
                    fA_x.wait(t, g + 1)
                    fA_q.wait(t, g + 1)
                    fA_r.wait(t, g)
                    for k in range(NCH):
                        t.matmul(ps_st[0:1, 0:512], st.onesc_b[:], xsb[:, k, :],
                                 start=(k == 0), stop=(k == NCH - 1))
                    for k in range(NCH):
                        last = (k == NCH - 1)
                        mm = t.matmul(ps_st[0:1, 512:1024], st.onesc_b[:],
                                      xq[:, k, :], start=(k == 0), stop=last)
                        if last:
                            fA_sm.inc(mm)
                    fA_r.wait(t, g + 1)
                    fA_y.wait(t, g)
                    t.matmul(ps_mr[:, 0:512], st.onesr_b[0:1, :], mrow_b[:],
                             start=True, stop=True)
                    mm = t.matmul(ps_mr[:, 512:1024], st.onesr_b[0:1, :],
                                  r0row_b[:], start=True, stop=True)
                    fA_rep.inc(mm)

        pr = contextlib.ExitStack()
        with pr:
            ps_lg = pr.enter_context(nc.psum_tensor(f"fplg_{tag}", [128, 2, 8], F32))
            ps_cnt = pr.enter_context(nc.psum_tensor(f"fpcnt_{tag}", [1, 128], F32))
            ps_cum = pr.enter_context(nc.psum_tensor(f"fpcum_{tag}", [128, 2, 8], F32))
            ps_thr = pr.enter_context(nc.psum_tensor(f"fpthr_{tag}", [128, 2, 8], F32))
            ps_base = pr.enter_context(nc.psum_tensor(f"fpbase_{tag}", [16, 8], F32))
            psT_g = pr.enter_context(nc.psum_tensor(f"fptg_{tag}", [8, 2, 128], BF16))

            @blk.tensor
            def _(t):
                fWgt.wait(t)
                fA_y.wait(t, NGRP)
                for z in range(NTILE):
                    bsl = slice(z * 128, (z + 1) * 128)
                    if z >= 2:
                        fVm.wait(t, z - 1)
                    for k in range(NCH):
                        mm = t.matmul(ps_lg[:, z % 2, :], yn[:, k, bsl],
                                      wgt[:, k, :], start=(k == 0),
                                      stop=(k == NCH - 1))
                    fLG.inc(mm)
                    fRT.wait(t, z + 1)
                    mm = t.matmul(ps_cnt[0:1, z * 8:(z + 1) * 8], st.onesc_b[:],
                                  mask_b[:, z, :], start=True, stop=True)
                    fCNT.inc(mm)
                fVag.wait(t, 1)
                mm = t.matmul(ps_base[:], st.pfx_b[:], agcnt_b[:],
                              start=True, stop=True)
                fBase.inc(mm)
                fThr.wait(t, 1)
                fSel.wait(t)
                for z in range(NTILE):
                    if z >= 2:
                        fGb.wait(t, z - 1)
                    t.matmul(ps_cum[:, z % 2, :], st.tri_b[:], mask_b[:, z, :],
                             start=True, stop=True)
                    mm = t.matmul(ps_thr[:, z % 2, :],
                                  sel16_b[0:16, z * 128:(z + 1) * 128],
                                  thr_b[:], start=True, stop=True)
                    fCum.inc(mm)
                    fGb.wait(t, z + 1)
                    if z >= 2:
                        fGT.wait(t, z - 1)
                    tp = t.transpose(psT_g[0:8, z % 2, :], g_b[:, z % 2, :],
                                     st.ident_b[:])
                    fTp.inc(tp)

            @blk.vector
            def _(v):
                for z in range(NTILE):
                    fLG.wait(v, z + 1)
                    v.tensor_tensor(out=lgf[:, z % 2, :], in0=ps_lg[:, z % 2, :],
                                    in1=consts["k_gb8"][:], op=ALU.add)
                    v.drain()
                    v.max(out=t8f[:], in_=lgf[:, z % 2, :])
                    v.drain()
                    v.tensor_scalar_mul(out=nmax[:, z % 2, :], in0=t8f[:, 0:1],
                                        scalar1=-1.0)
                    v.drain()
                    fVm.bump(v)
                    fEX.wait(v, z + 1)
                    v.reduce_sum(out=wsm[:], in_=expd[:, z % 2, :], axis=AXT.X)
                    v.drain()
                    v.reciprocal(out=wsm[:], in_=wsm[:])
                    v.drain()
                    v.tensor_scalar(out=prb[:], in0=expd[:, z % 2, :],
                                    scalar1=wsm[:, 0:1], scalar2=None,
                                    op0=ALU.mult)
                    v.drain()
                    v.max(out=t8f[:], in_=prb[:])
                    v.drain()
                    v.tensor_scalar(out=sel_f[:], in0=prb[:],
                                    scalar1=t8f[:, 1:2], scalar2=None,
                                    op0=ALU.is_ge)
                    v.drain()
                    v.tensor_add(out=wsm[:], in0=t8f[:, 0:1], in1=t8f[:, 1:2])
                    v.drain()
                    v.reciprocal(out=wsm[:], in_=wsm[:])
                    v.drain()
                    v.tensor_scalar(out=prb[:], in0=prb[:], scalar1=wsm[:, 0:1],
                                    scalar2=None, op0=ALU.mult)
                    v.drain()
                    v.tensor_tensor(out=wte_all[:, z, :], in0=prb[:],
                                    in1=sel_f[:], op=ALU.mult)
                    v.drain()
                    v.tensor_copy(out=mask_b[:, z, :], in_=sel_f[:])
                    v.drain()
                    fRT.bump(v)
                fCNT.wait(v, NTILE)
                v.tensor_copy(out=cntrow[:], in_=ps_cnt[0:1, :])
                v.drain()
                fVcnt.bump(v)
                fDag.wait(v, 16)
                v.tensor_copy(out=agcnt_b[:], in_=agcnt_s[:])
                v.drain()
                fVag.bump(v)
                fBase.wait(v, 1)
                v.tensor_scalar(out=thr_s[:], in0=ps_base[:], scalar1=-1.0,
                                scalar2=CCAP, op0=ALU.mult, op1=ALU.add)
                v.drain()
                v.tensor_scalar(out=thr_s[:], in0=thr_s[:], scalar1=129.0,
                                scalar2=-1.0, op0=ALU.min, op1=ALU.max)
                v.drain()
                v.tensor_copy(out=thr_b[:], in_=thr_s[:])
                v.drain()
                fThr.bump(v)
                for z in range(NTILE):
                    fCum.wait(v, z + 1)
                    v.tensor_copy(out=t8f[:], in_=ps_cum[:, z % 2, :])
                    v.drain()
                    v.tensor_tensor(out=prb[:], in0=ps_thr[:, z % 2, :],
                                    in1=t8f[:], op=ALU.is_ge)
                    v.drain()
                    v.tensor_tensor(out=sel_f[:], in0=prb[:],
                                    in1=wte_all[:, z, :], op=ALU.mult)
                    v.drain()
                    v.tensor_copy(out=g_b[:, z % 2, :], in_=sel_f[:])
                    v.drain()
                    fGb.bump(v)
                    fTp.wait(v, z + 1)
                    v.tensor_copy(out=gT_b[0:8, z * 128:(z + 1) * 128],
                                  in_=psT_g[0:8, z % 2, :])
                    v.drain()
                    fGT.bump(v)

            @blk.scalar
            def _(s):
                for z in range(NTILE):
                    fVm.wait(s, z + 1)
                    s.activation(out=expd[:, z % 2, :], in_=lgf[:, z % 2, :],
                                 func=AF.Exp, bias=nmax[:, z % 2, 0:1])
                    s.drain()
                    fEX.bump(s)

            @blk.gpsimd
            def _(g_):
                fVcnt.wait(g_, 1)
                g_.dma_start(out=cin_f, in_=cntrow[:]).then_inc(fDcnt.h, 16)
                fDcnt.note(16)
                fDcnt.wait(g_)
                cc = g_.collective_compute(
                    "AllGather", ALU.bypass,
                    replica_groups=[[0, 1, 2, 3, 4, 5, 6, 7]],
                    ins=[cin_f], outs=[cout_f])
                fAG.inc(cc)
                fAG.wait(g_)
                g_.dma_start(out=agcnt_s[:],
                             in_=cout_f.rearrange("r (z e) -> (r z) e", e=8)
                             ).then_inc(fDag.h, 16)
                fDag.note(16)

        ps_g = loc.enter_context(nc.psum_tensor(f"fpg_{tag}", [128, 8, 512], F32))

        @blk.sync
        def _(s):
            for g in range(NGRP):
                for e in range(8):
                    q = g * 8 + e
                    for hb in range(32):
                        qh = q * 32 + hb
                        if qh >= 2:
                            fUp.wait(s, qh - 1)
                        s.dma_start(out=w1s[:, :, qh % 2, :],
                                    in_=w1_r[e][:, :, hb * 128:(hb + 1) * 128]
                                    ).then_inc(fW1[qh % 2].h, 16)
                        fW1[qh % 2].note(16)

        @blk.gpsimd
        def _(g_):
            for g in range(NGRP):
                for e in range(8):
                    q = g * 8 + e
                    for hb in range(32):
                        qh = q * 32 + hb
                        if qh >= 2:
                            fDn.wait(g_, qh - 1)
                        g_.dma_start(out=w2s[:, qh % 2, :],
                                     in_=w2_r[e][:, hb, :]
                                     ).then_inc(fW2[qh % 2].h, 16)
                        fW2[qh % 2].note(16)

        @blk.tensor
        def _(t):
            fGT.wait(t, NTILE)
            fB2w.wait(t)
            for g in range(NGRP):
                gsl = slice(g * 512, (g + 1) * 512)
                fGrpV.wait(t, g * 8)
                fEvac.wait(t, g * 64)
                for ck in range(NCH):
                    mm = t.matmul(ps_g[:, ck, :],
                                  b2w[0:8, ck * 128:(ck + 1) * 128],
                                  gT_b[0:8, gsl], start=True, stop=True)
                    fB2.inc(mm)
                for e in range(8):
                    q = g * 8 + e
                    if e == 0:
                        fAcc.wait(t, g * 8 + 2)
                    else:
                        fEvac.wait(t, (q - 1) * 8 + 2)
                    for hb in range(32):
                        qh = q * 32 + hb
                        fW1[qh % 2].wait(t, 16 * (qh // 2 + 1))
                        if qh >= 2:
                            fRelu.wait(t, qh - 1)
                        for k in range(NCH):
                            mm = t.matmul(ps_g[:, qh % 2, :],
                                          w1s[:, k, qh % 2, :], yn[:, k, gsl],
                                          start=(k == 0), stop=(k == NCH - 1))
                        fUp.inc(mm)
                    if e == 0:
                        fAcc.wait(t, g * 8 + 8)
                    else:
                        fEvac.wait(t, q * 8)
                    mm = t.matmul(ps_g[:, 7, :],
                                  sel16_b[0:8, e * 128:(e + 1) * 128],
                                  gT_b[0:8, gsl], start=True, stop=True)
                    fGrpP.inc(mm)
                    fRelu.wait(t, (q + 1) * 32)
                    fGrpV.wait(t, q + 1)
                    for hb in range(32):
                        qh = q * 32 + hb
                        fW2[qh % 2].wait(t, 16 * (qh // 2 + 1))
                        for ck in range(NCH):
                            mm = t.matmul(
                                ps_g[:, ck, :],
                                w2s[:, qh % 2, ck * 128:(ck + 1) * 128],
                                h_b[:, hb, :], start=(hb == 0), stop=(hb == 31))
                        fDn.inc(mm)

        @blk.scalar
        def _(s):
            fLB.wait(s)
            for g in range(NGRP):
                for e in range(8):
                    q = g * 8 + e
                    for hb in range(32):
                        qh = q * 32 + hb
                        fUp.wait(s, qh + 1)
                        if q > 0:
                            fDn.wait(s, (q - 1) * 32 + hb + 1)
                        s.activation(out=h_b[:, hb, :], in_=ps_g[:, qh % 2, :],
                                     func=AF.Relu, bias=b1_s[:, e, hb:hb + 1])
                        s.drain()
                        fRelu.bump(s)

        @blk.vector
        def _(v):
            for g in range(NGRP):
                gsl = slice(g * 512, (g + 1) * 512)
                for ck in range(NCH):
                    fB2.wait(v, g * 8 + ck + 1)
                    v.tensor_copy(out=acc[:, ck, :], in_=ps_g[:, ck, :])
                    v.drain()
                    fAcc.bump(v)
                for e in range(8):
                    q = g * 8 + e
                    fGrpP.wait(v, q + 1)
                    v.tensor_copy(out=grep_s[:], in_=ps_g[:, 7, :])
                    v.drain()
                    fGrpV.bump(v)
                    fDn.wait(v, (q + 1) * 32)
                    for ck in range(NCH):
                        v.tensor_tensor(out=tfv[:], in0=grep_s[:],
                                        in1=ps_g[:, ck, :], op=ALU.mult)
                        v.drain()
                        v.tensor_tensor(out=acc[:, ck, :], in0=acc[:, ck, :],
                                        in1=tfv[:], op=ALU.add)
                        v.drain()
                        fEvac.bump(v)
                for ck in range(NCH):
                    v.scalar_tensor_tensor(out=x[:, ck, gsl], in0=acc[:, ck, :],
                                           scalar=0.5, in1=x[:, ck, gsl],
                                           op0=ALU.mult, op1=ALU.add)
                    v.drain()
                    fXup.bump(v)

    st.x_ready = (fXup.h, fXup.c.n)


def emit_l1attn(st, tag):
    nc, blk, x = st.nc, st.blk, st.x
    dr, wb, consts = st.dr, st.wb, st.consts
    par_known = None
    xr_h, xr_n = st.x_ready
    wcB = st.cWC1.n

    st.stage_reset()
    Ce, Ch, Cg = st.Ce, st.Ch, st.Cg
    aA_x, aA_q, aA_sm, aA_v, aA_sq, aA_r, aA_rep, aA_y = (
        Ce(), Ce(), Ce(), Ce(), Ce(), Ce(), Ce(), Ce())
    aWL, aBV = Ch(), Cg()
    aQKP, aQKE, aVP, aVE = Ce(), Ce(), Ce(), Ce()
    aKDs = [Ch(), Ch()]
    aVDs = [Ch(), Ch()]
    cagK, cagV = Ce(), Ce()
    aSC, aSE, aMX, aEXP, aRS = Ce(), Ce(), Ce(), Ce(), Ce()
    aPT, aPTE, aPV, aYB, aYT, aXE = Ce(), Ce(), Ce(), Ce(), Ce(), Ce()
    aKS = [Ch(), Ch()]
    aVS = [Ch(), Ch()]

    cin_k = st.dint(f"cink_{tag}", [1024, TPC], BF16)
    cout_k = st.dint(f"coutk_{tag}", [2048, TPC], BF16)
    cin_v = st.dint(f"cinv_{tag}", [TPC, 1024], BF16)
    cout_v = st.dint(f"coutv_{tag}", [2 * TPC, 1024], BF16)
    cinkr = cin_k.rearrange("(k p) t -> p k t", p=128)
    cinvr = cin_v.rearrange("(z p) c -> p z c", p=128)

    loc = contextlib.ExitStack()
    sb = lambda n, s, d: loc.enter_context(nc.sbuf_tensor(f"{n}_{tag}", s, d))
    with loc:
        qfm = sb("qfm", [128, NCH, TPC], BF16)
        bvrow = sb("bvrow", [1, 1024], BF16)

        pp = contextlib.ExitStack()
        with pp:
            sbp = lambda n, s, d: pp.enter_context(
                nc.sbuf_tensor(f"{n}_{tag}", s, d))
            xn = sbp("xn", [128, NCH, TPC], BF16)
            wqk = sbp("wqk", [128, NCH, 2048], BF16)
            wv = sbp("wv", [128, NCH, 1024], BF16)
            kq_s = sbp("kqs", [128, 2, 512], BF16)
            vt_s = sbp("vts", [128, 2, 512], BF16)
            xq = sbp("xqa", [128, NCH, 512], BF16)
            mrow = sbp("mrowa", [1, 512], F32)
            trow = sbp("trowa", [1, 1024], F32)
            r0row = sbp("r0rowa", [1, 512], F32)
            mrow_b = sbp("mrowab", [1, 512], BF16)
            r0row_b = sbp("r0rowab", [1, 512], BF16)
            ps_st = pp.enter_context(nc.psum_tensor(f"apst_{tag}", [1, 1024], F32))
            ps_mr = pp.enter_context(nc.psum_tensor(f"apmr_{tag}", [128, 1024], F32))
            ps_p = pp.enter_context(nc.psum_tensor(f"app_{tag}", [128, 2, 512], F32))
            ps_v = pp.enter_context(nc.psum_tensor(f"apv_{tag}", [128, 2, 512], F32))

            qkv_r = wb["w_l1a_qkv"].rearrange("(k p) f -> p k f", p=128)

            @blk.sync
            def _(s):
                s.wait_ge(xr_h, xr_n)
                s.wait_ge(st.cWC1.h, wcB)
                s.dma_start(out=wqk[:], in_=qkv_r[:, :, 0:2048]
                            ).then_inc(aWL.h, 16)
                aWL.note(16)
                s.dma_start(out=wv[:], in_=qkv_r[:, :, 2048:3072]
                            ).then_inc(aWL.h, 16)
                aWL.note(16)
                for kk in range(32):
                    fk, g = kk // 4, kk % 4
                    gsl = slice(g * 512, (g + 1) * 512)
                    aQKE.wait(s, 32 + kk + 1)
                    s.dma_start(out=cinkr[:, fk, gsl], in_=kq_s[:, kk % 2, :]
                                ).then_inc(aKDs[kk % 2].h, 16)
                    aKDs[kk % 2].note(16)
                for vz in range(32):
                    z, h = vz // 2, vz % 2
                    aVE.wait(s, vz + 1)
                    s.dma_start(out=cinvr[:, z, h * 512:(h + 1) * 512],
                                in_=vt_s[:, vz % 2, :]).then_inc(aVDs[vz % 2].h, 16)
                    aVDs[vz % 2].note(16)

            @blk.gpsimd
            def _(g_):
                g_.wait_ge(xr_h, xr_n)
                g_.dma_start(out=bvrow[:], in_=dr["r_bv"]).then_inc(aBV.h, 16)
                aBV.note(16)
                for c_ in aKDs:
                    c_.wait(g_)
                for c_ in aVDs:
                    c_.wait(g_)
                cc = g_.collective_compute(
                    "AllGather", ALU.bypass,
                    replica_groups=[[0, 1], [2, 3], [4, 5], [6, 7]],
                    ins=[cin_k], outs=[cout_k])
                cagK.inc(cc)
                cc = g_.collective_compute(
                    "AllGather", ALU.bypass,
                    replica_groups=[[0, 1], [2, 3], [4, 5], [6, 7]],
                    ins=[cin_v], outs=[cout_v])
                cagV.inc(cc)

            @blk.vector
            def _(v):
                v.wait_ge(xr_h, xr_n)
                for g in range(NGRP):
                    gsl = slice(g * 512, (g + 1) * 512)
                    xsb = xn[:, :, gsl]
                    aA_sm.wait(v, g)
                    for k in range(NCH):
                        v.tensor_copy(out=xsb[:, k, :], in_=x[:, k, gsl])
                    v.drain()
                    aA_x.bump(v)
                    aA_sm.wait(v, g + 1)
                    v.tensor_scalar_mul(out=mrow[:], in0=ps_st[0:1, 0:512],
                                        scalar1=1.0 / C)
                    v.tensor_scalar_mul(out=trow[0:1, 0:512],
                                        in0=ps_st[0:1, 512:1024], scalar1=1.0 / C)
                    v.drain()
                    v.tensor_mul(out=trow[0:1, 512:1024], in0=mrow[:], in1=mrow[:])
                    v.drain()
                    v.tensor_sub(out=trow[0:1, 0:512], in0=trow[0:1, 0:512],
                                 in1=trow[0:1, 512:1024])
                    v.drain()
                    v.tensor_scalar_add(out=trow[0:1, 512:1024],
                                        in0=trow[0:1, 0:512], scalar1=EPS)
                    v.drain()
                    aA_v.bump(v)
                    aA_sq.wait(v, g + 1)
                    v.reciprocal(out=r0row[:], in_=trow[0:1, 512:1024])
                    v.drain()
                    v.tensor_copy(out=mrow_b[:], in_=mrow[:])
                    v.tensor_copy(out=r0row_b[:], in_=r0row[:])
                    v.drain()
                    aA_r.bump(v)
                    aA_rep.wait(v, g + 1)
                    for k in range(NCH):
                        v.tensor_tensor(out=xsb[:, k, :], in0=x[:, k, gsl],
                                        in1=ps_mr[:, 0:512], op=ALU.subtract)
                        v.drain()
                        v.tensor_tensor(out=xsb[:, k, :], in0=xsb[:, k, :],
                                        in1=ps_mr[:, 512:1024], op=ALU.mult)
                        v.drain()
                    aA_y.bump(v)

            @blk.scalar
            def _(s):
                for g in range(NGRP):
                    aA_x.wait(s, g + 1)
                    aA_sm.wait(s, g)
                    xsb = xn[:, :, slice(g * 512, (g + 1) * 512)]
                    for k in range(NCH):
                        s.activation(out=xq[:, k, :], in_=xsb[:, k, :],
                                     func=AF.Square)
                    s.drain()
                    aA_q.bump(s)
                    aA_v.wait(s, g + 1)
                    s.activation(out=trow[0:1, 512:1024], in_=trow[0:1, 512:1024],
                                 func=AF.Sqrt)
                    s.drain()
                    aA_sq.bump(s)
                for pj in range(64):
                    j, g = pj // 4, pj % 4
                    gsl = slice(g * 512, (g + 1) * 512)
                    aQKP.wait(s, pj + 1)
                    if j < 8:
                        s.add(out=qfm[:, j, gsl], in_=ps_p[:, pj % 2, :],
                              add=consts["c_bqkv"][:, j:j + 1])
                        s.drain()
                    else:
                        kk = pj - 32
                        if kk >= 2:
                            aKDs[kk % 2].wait(s, 16 * (kk // 2))
                        s.add(out=kq_s[:, kk % 2, :], in_=ps_p[:, pj % 2, :],
                              add=consts["c_bqkv"][:, j:j + 1])
                        s.drain()
                    aQKE.bump(s)
                for vz in range(32):
                    aVP.wait(s, vz + 1)
                    if vz >= 2:
                        aVDs[vz % 2].wait(s, 16 * (vz // 2))
                    s.copy(out=vt_s[:, vz % 2, :], in_=ps_v[:, vz % 2, :])
                    s.drain()
                    aVE.bump(s)

            @blk.tensor
            def _(t):
                t.wait_ge(st.cKC.h, st.cKC.n)
                for g in range(NGRP):
                    gsl = slice(g * 512, (g + 1) * 512)
                    xsb = xn[:, :, gsl]
                    aA_x.wait(t, g + 1)
                    aA_q.wait(t, g + 1)
                    aA_r.wait(t, g)
                    for k in range(NCH):
                        t.matmul(ps_st[0:1, 0:512], st.onesc_b[:], xsb[:, k, :],
                                 start=(k == 0), stop=(k == NCH - 1))
                    for k in range(NCH):
                        last = (k == NCH - 1)
                        mm = t.matmul(ps_st[0:1, 512:1024], st.onesc_b[:],
                                      xq[:, k, :], start=(k == 0), stop=last)
                        if last:
                            aA_sm.inc(mm)
                    aA_r.wait(t, g + 1)
                    aA_y.wait(t, g)
                    t.matmul(ps_mr[:, 0:512], st.onesr_b[0:1, :], mrow_b[:],
                             start=True, stop=True)
                    mm = t.matmul(ps_mr[:, 512:1024], st.onesr_b[0:1, :],
                                  r0row_b[:], start=True, stop=True)
                    aA_rep.inc(mm)
                aWL.wait(t, 32)
                for pj in range(64):
                    j, g = pj // 4, pj % 4
                    gsl = slice(g * 512, (g + 1) * 512)
                    aA_y.wait(t, g + 1)
                    if pj >= 2:
                        aQKE.wait(t, pj - 1)
                    for k in range(NCH):
                        mm = t.matmul(ps_p[:, pj % 2, :],
                                      wqk[:, k, j * 128:(j + 1) * 128],
                                      xn[:, k, gsl], start=(k == 0),
                                      stop=(k == NCH - 1))
                    aQKP.inc(mm)
                aWL.wait(t, 32)
                aBV.wait(t)
                for vz in range(32):
                    z, h = vz // 2, vz % 2
                    bsl = slice(z * 128, (z + 1) * 128)
                    aA_y.wait(t, z // 4 + 1)
                    if vz >= 2:
                        aVE.wait(t, vz - 1)
                    for k in range(NCH):
                        t.matmul(ps_v[:, vz % 2, :], xn[:, k, bsl],
                                 wv[:, k, h * 512:(h + 1) * 512],
                                 start=(k == 0), stop=False)
                    mm = t.matmul(ps_v[:, vz % 2, :], st.onesr_b[0:1, :],
                                  bvrow[0:1, h * 512:(h + 1) * 512],
                                  start=False, stop=True)
                    aVP.inc(mm)

        pt = contextlib.ExitStack()
        with pt:
            sbt = lambda n, s, d: pt.enter_context(
                nc.sbuf_tensor(f"{n}_{tag}", s, d))
            sc_s = sbt("scs", [128, 4096], F32)
            p_b = sbt("pb", [128, 4096], BF16)
            kst = sbt("kst", [128, NCH, 2, 512], BF16)
            vst = sbt("vst", [128, 2, 1024], BF16)
            pT_b = sbt("pTb", [128, 2, 128], BF16)
            yt_b = sbt("ytb", [128, 1024], BF16)
            mx = sbt("mx", [128, 1], F32)
            nm32 = sbt("nm32", [128, 1], F32)
            rs = sbt("rs", [128, 1], F32)
            rsv = sbt("rsv", [128, 1], F32)
            npv9 = sbt("npv9", [128, 1], F32)
            ps_sc = pt.enter_context(nc.psum_tensor(f"apsc_{tag}", [128, 2, 512], F32))
            ps_y = pt.enter_context(nc.psum_tensor(f"apy_{tag}", [128, 2, 512], F32))
            psT_p = pt.enter_context(nc.psum_tensor(f"apTp_{tag}", [128, 2, 128], BF16))
            psT_y = pt.enter_context(nc.psum_tensor(f"apTy_{tag}", [128, 1024], BF16))

            CH = []
            VT = []
            ks_tot = [0, 0]
            vs_tot = [0, 0]
            scq = 0
            ktg = 0
            for z in range(NTILE):
                ntk = 2 * z + 2
                nch512 = (ntk + 3) // 4
                for kc in range(nch512):
                    tiles = min(4, ntk - kc * 4)
                    ks_tot[scq % 2] += 16 * tiles
                    CH.append((z, kc, scq, tiles, ks_tot[scq % 2]))
                    scq += 1
                for kt in range(ntk):
                    vs_tot[ktg % 2] += 16
                    VT.append((z, kt, ktg, vs_tot[ktg % 2]))
                    ktg += 1
            KTEND = {}
            SCEND = {}
            a = 0
            b_ = 0
            for z in range(NTILE):
                a += 2 * z + 2
                b_ += (2 * z + 2 + 3) // 4
                KTEND[z] = a
                SCEND[z] = b_

            @blk.sync
            def _(s):
                cagK.wait(s, 1)
                cagV.wait(s, 1)
                ckr = cout_k.rearrange("(r k p) t -> p r k t", p=128, r=2)
                cvr = cout_v.rearrange("(r z p) c -> p r z c", p=128, r=2)
                chz = {}
                vtz = {}
                for item in CH:
                    chz.setdefault(item[0], []).append(item)
                for item in VT:
                    vtz.setdefault(item[0], []).append(item)
                for z in range(NTILE):
                    for (_z, kc, sq, tiles, _tot) in chz[z]:
                        if sq >= 2:
                            aSE.wait(s, sq - 1)
                        for tt in range(tiles):
                            kt = kc * 4 + tt
                            s.dma_start(
                                out=kst[:, :, sq % 2, tt * 128:(tt + 1) * 128],
                                in_=ckr[:, kt % 2, :,
                                        (kt // 2) * 128:(kt // 2 + 1) * 128]
                                ).then_inc(aKS[sq % 2].h, 16)
                            aKS[sq % 2].note(16)
                    for (_z, kt, kg, _tot) in vtz[z]:
                        if kg >= 2:
                            aPV.wait(s, kg - 1)
                        s.dma_start(out=vst[:, kg % 2, :],
                                    in_=cvr[:, kt % 2, kt // 2, :]
                                    ).then_inc(aVS[kg % 2].h, 16)
                        aVS[kg % 2].note(16)

            @blk.tensor
            def _(t):
                aQKE.wait(t, 32)
                ci = 0
                vi = 0
                for z in range(NTILE):
                    ntk = 2 * z + 2
                    nch512 = (ntk + 3) // 4
                    bsl = slice(z * 128, (z + 1) * 128)
                    for kc in range(nch512):
                        (_z, _kc, sq, tiles, tot) = CH[ci]
                        ci += 1
                        nkc = tiles * 128
                        aKS[sq % 2].wait(t, tot)
                        if sq >= 2:
                            aSE.wait(t, sq - 1)
                        for k in range(NCH):
                            mm = t.matmul(ps_sc[:, sq % 2, 0:nkc],
                                          qfm[:, k, bsl],
                                          kst[:, k, sq % 2, 0:nkc],
                                          start=(k == 0), stop=(k == NCH - 1))
                        aSC.inc(mm)
                    aEXP.wait(t, z + 1)
                    if z >= 1:
                        aYB.wait(t, z)
                    for kt in range(ntk):
                        (_z, _kt, kg, vtot) = VT[vi]
                        vi += 1
                        if kg >= 2:
                            aPTE.wait(t, kg - 1)
                        tp = t.transpose(psT_p[:, kg % 2, :],
                                         p_b[:, kt * 128:(kt + 1) * 128],
                                         st.ident_b[:])
                        aPT.inc(tp)
                        aPTE.wait(t, kg + 1)
                        aVS[kg % 2].wait(t, vtot)
                        for h in range(2):
                            mm = t.matmul(ps_y[:, h, :], pT_b[:, kg % 2, :],
                                          vst[:, kg % 2, h * 512:(h + 1) * 512],
                                          start=(kt == 0), stop=(kt == ntk - 1))
                        aPV.inc(mm)
                    aYB.wait(t, z + 1)
                    if z >= 1:
                        aXE.wait(t, z)
                    for ck in range(NCH):
                        tp = t.transpose(psT_y[:, ck * 128:(ck + 1) * 128],
                                         yt_b[:, ck * 128:(ck + 1) * 128],
                                         st.ident_b[:])
                        aYT.inc(tp)

            @blk.vector
            def _(v):
                cagK.wait(v, 1)
                cagV.wait(v, 1)
                v.tensor_scalar_mul(out=npv9[:], in0=st.nparv[:], scalar1=1e9)
                v.drain()
                ci = 0
                ktg_v = 0
                for z in range(NTILE):
                    ntk = 2 * z + 2
                    nch512 = (ntk + 3) // 4
                    nk = ntk * 128
                    for kc in range(nch512):
                        (_z, _kc, sq, tiles, _tot) = CH[ci]
                        ci += 1
                        nkc = tiles * 128
                        aSC.wait(v, sq + 1)
                        v.tensor_copy(out=sc_s[:, kc * 512:kc * 512 + nkc],
                                      in_=ps_sc[:, sq % 2, 0:nkc])
                        v.drain()
                        aSE.bump(v)
                    d0 = slice(2 * z * 128, 2 * z * 128 + 128)
                    d1 = slice((2 * z + 1) * 128, (2 * z + 1) * 128 + 128)
                    v.scalar_tensor_tensor(out=sc_s[:, d0],
                                           in0=consts["k_causal"][:],
                                           scalar=st.nparv[:, 0:1],
                                           in1=sc_s[:, d0],
                                           op0=ALU.mult, op1=ALU.add)
                    v.drain()
                    v.scalar_tensor_tensor(out=sc_s[:, d1],
                                           in0=consts["k_causal"][:],
                                           scalar=st.parv[:, 0:1],
                                           in1=sc_s[:, d1],
                                           op0=ALU.mult, op1=ALU.add)
                    v.drain()
                    v.tensor_scalar(out=sc_s[:, d1], in0=sc_s[:, d1],
                                    scalar1=npv9[:, 0:1], scalar2=None,
                                    op0=ALU.subtract)
                    v.drain()
                    v.reduce_max(out=mx[:], in_=sc_s[:, 0:nk], axis=AXT.X)
                    v.drain()
                    v.tensor_scalar_mul(out=nm32[:], in0=mx[:], scalar1=-1.0 / 32)
                    v.drain()
                    aMX.bump(v)
                    aEXP.wait(v, z + 1)
                    v.reciprocal(out=rsv[:], in_=rs[:])
                    v.drain()
                    aRS.bump(v)
                    for kt in range(ntk):
                        kg = ktg_v
                        ktg_v += 1
                        aPT.wait(v, kg + 1)
                        if kg >= 2:
                            aPV.wait(v, kg - 1)
                        v.tensor_copy(out=pT_b[:, kg % 2, :],
                                      in_=psT_p[:, kg % 2, :])
                        v.drain()
                        aPTE.bump(v)
                    aPV.wait(v, KTEND[z])
                    for h in range(2):
                        v.tensor_scalar(out=yt_b[:, h * 512:(h + 1) * 512],
                                        in0=ps_y[:, h, :], scalar1=rsv[:, 0:1],
                                        scalar2=None, op0=ALU.mult)
                        v.drain()
                    aYB.bump(v)
                    aYT.wait(v, (z + 1) * NCH)
                    for ck in range(NCH):
                        v.tensor_tensor(out=x[:, ck, z * 128:(z + 1) * 128],
                                        in0=x[:, ck, z * 128:(z + 1) * 128],
                                        in1=psT_y[:, ck * 128:(ck + 1) * 128],
                                        op=ALU.add)
                        v.drain()
                    aXE.bump(v)

            @blk.scalar
            def _(s):
                for z in range(NTILE):
                    ntk = 2 * z + 2
                    nk = ntk * 128
                    aMX.wait(s, z + 1)
                    s.activation(out=p_b[:, 0:nk], in_=sc_s[:, 0:nk],
                                 func=AF.Exp, scale=1.0 / 32,
                                 bias=nm32[:, 0:1], accum_out=rs[:])
                    s.drain()
                    aEXP.bump(s)

    st.x_ready = (aXE.h, aXE.c.n)




def kernel(**inputs):
    cores = _prep_inputs(inputs)
    nc = build_program()
    res = run_bass_kernel_spmd(nc, cores, core_ids=list(range(8)))
    return _gather_output(res.results)


# version 03
# speedup vs baseline: 1.0111x; 1.0111x over previous
import contextlib
import numpy as np

import concourse.bass as bass
import concourse.mybir as mybir
from concourse.bass_utils import run_bass_kernel_spmd

F32 = mybir.dt.float32
BF16 = mybir.dt.bfloat16
I16 = mybir.dt.int16
I32 = mybir.dt.int32
U32 = mybir.dt.uint32
AF = mybir.ActivationFunctionType
ALU = mybir.AluOpType
AXT = mybir.AxisListType

B, T, C = 4, 4096, 1024
NCH = C // 128
TPC = 2048
NTILE = TPC // 128
NGRP = TPC // 512
EPS = 1e-5

N_STAGES = 6



def _prep_inputs(inp):
    pm = {}

    def T2(a):
        return np.ascontiguousarray(np.asarray(a).T)

    pm["w_l0f_router"] = T2(inp["L0_ffn_router_w"])
    pm["w_l0f_gate"] = np.ascontiguousarray(inp["L0_ffn_gate_w"])
    pm["w_l0f_up"] = np.ascontiguousarray(inp["L0_ffn_up_w"])
    pm["w_l0f_down"] = np.ascontiguousarray(inp["L0_ffn_down_w"])
    pm["w_l0f_sg"] = T2(inp["L0_ffn_shared_gate_w"])
    pm["w_l0f_su"] = T2(inp["L0_ffn_shared_up_w"])
    pm["w_l0f_sd"] = T2(inp["L0_ffn_shared_down_w"])
    pm["w_l0a_k"] = T2(inp["L0_attn_att_kw"])
    pm["w_l0a_v"] = T2(inp["L0_attn_att_vw"])
    pm["w_l0a_r"] = T2(inp["L0_attn_att_rw"])
    pm["w_l0a_o"] = T2(inp["L0_attn_att_ow"])
    pm["w_l0a_fk"] = T2(inp["L0_attn_f_kw"])
    pm["w_l0a_fr"] = T2(inp["L0_attn_f_rw"])
    pm["w_l0a_fv"] = T2(inp["L0_attn_f_vw"])
    pm["w_l1a_qkv"] = T2(inp["L1_attn_W_qkv"])
    pm["w_l1f_gate"] = np.ascontiguousarray(inp["L1_ffn_gate_w"])
    pm["w_l1f_w1"] = np.ascontiguousarray(inp["L1_ffn_w1"])
    pm["w_l1f_w2"] = np.ascontiguousarray(inp["L1_ffn_w2"])

    def chanvec(a):
        return np.ascontiguousarray(np.asarray(a).reshape(NCH, 128).T)

    for nm, src in [("c_ln1w", "L0_attn_ln1_w"), ("c_ln1b", "L0_attn_ln1_b"),
                    ("c_ln2w", "L0_attn_ln2_w"), ("c_ln2b", "L0_attn_ln2_b"),
                    ("c_decay", "L0_attn_time_decay"), ("c_tfirst", "L0_attn_time_first"),
                    ("c_tmk", "L0_attn_tmk"), ("c_tmv", "L0_attn_tmv"),
                    ("c_tmr", "L0_attn_tmr"), ("c_ftmk", "L0_attn_f_tmk"),
                    ("c_ftmr", "L0_attn_f_tmr")]:
        pm[nm] = chanvec(inp[src])
    pm["c_bqkv"] = np.ascontiguousarray(
        np.asarray(inp["L1_attn_b_qkv"]).reshape(24, 128).T)
    pm["r_bv"] = np.ascontiguousarray(
        np.asarray(inp["L1_attn_b_qkv"])[None, 2048:3072]).astype(np.float32)
    pm["c_l1b1"] = np.ascontiguousarray(
        np.asarray(inp["L1_ffn_b1"]).reshape(8, 32, 128).transpose(0, 2, 1))
    pm["r_l1b2"] = np.ascontiguousarray(np.asarray(inp["L1_ffn_b2"]))
    pm["k_cb16"] = np.ascontiguousarray(
        np.tile(np.asarray(inp["L0_ffn_correction_bias"])[None, :], (128, 1))
    ).astype(np.float32)
    pm["k_gb8"] = np.ascontiguousarray(
        np.tile(np.asarray(inp["L1_ffn_gate_b"])[None, :], (128, 1))
    ).astype(np.float32)
    pm["k_iota16"] = np.tile(np.arange(16, dtype=np.float32)[None, :], (128, 1))
    pm["k_iota8"] = np.tile(np.arange(8, dtype=np.float32)[None, :], (128, 1))
    sel = np.zeros((16, 16 * 128), np.float32)
    for e in range(16):
        sel[e, e * 128:(e + 1) * 128] = 1.0
    pm["k_sel16"] = sel
    pm["k_onesr"] = np.ones((65, 128), np.float32)
    pm["k_onesc"] = np.ones((128, 1), np.float32)
    pm["k_tri"] = np.ascontiguousarray(np.triu(np.ones((128, 128), np.float32)))
    pm["k_causal"] = np.where(np.tril(np.ones((128, 128), np.bool_)), 0.0,
                              -1e9).astype(np.float32)
    pm["k_ident"] = np.eye(128, dtype=np.float32)

    gt_src = np.empty(128, np.int64)
    for r in range(8):
        for zz in range(16):
            gt_src[r * 16 + zz] = (r // 2) * 32 + 2 * zz + (r % 2)

    x = np.asarray(inp["L0_attn_x"])
    cores = []
    for core in range(8):
        b, par = core // 2, core % 2
        xt = x[b].reshape(32, 128, C)[par::2].reshape(TPC, C)
        gt_dst = b * 32 + 2 * np.arange(16) + par
        pfx = (gt_src[:, None] < gt_dst[None, :]).astype(np.float32)
        d = {"x_in": np.ascontiguousarray(xt.T),
             "c_parv": np.full((128, 1), float(par), np.float32),
             "c_nparv": np.full((128, 1), 1.0 - par, np.float32),
             "k_pfx": pfx}
        d.update(pm)
        cores.append(d)
    return cores


def _gather_output(res):
    out = np.empty((B, T, C), np.float32)
    for core in range(8):
        b, par = core // 2, core % 2
        xo = res[core]["x_out"]
        out[b].reshape(32, 128, C)[par::2] = xo.T.reshape(NTILE, 128, C)
    return out



class Ctr:
    def __init__(self, nc, es, name):
        self.h = es.enter_context(nc.semaphore(name))
        self.n = 0

    def inc(self, instr, k=1):
        instr.then_inc(self.h, k)
        self.n += k
        return self.n

    def incd(self, instr, k=16):
        return self.inc(instr, k)

    def bump(self, eng, k=1):
        eng.sem_inc(self.h, k)
        self.n += k
        return self.n

    def note(self, k=16):
        self.n += k

    def wait(self, eng, n=None):
        eng.wait_ge(self.h, self.n if n is None else n)


class CtrView:

    def __init__(self, ctr):
        self.c = ctr
        self.base = ctr.n

    @property
    def h(self):
        return self.c.h

    @property
    def n(self):
        return self.c.n - self.base

    def inc(self, instr, k=1):
        self.c.inc(instr, k)
        return self.n

    def incd(self, instr, k=16):
        return self.inc(instr, k)

    def bump(self, eng, k=1):
        self.c.bump(eng, k)
        return self.n

    def note(self, k=16):
        self.c.n += k

    def wait(self, eng, n=None):
        eng.wait_ge(self.c.h, self.c.n if n is None else self.base + n)


class St:
    pass



def build_program(n_stages=N_STAGES, debug_taps=(), first_stage=1):
    nc = bass.Bass("TRN2", target_bir_lowering=False)
    es = contextlib.ExitStack()
    st = St()
    st.nc, st.es = nc, es
    st.debug_taps = tuple(debug_taps)
    st.uid = 0

    dr = {}

    def din(name, shape, dtype=F32):
        dr[name] = nc.dram_tensor(name, shape, dtype, kind="ExternalInput").ap()

    def dint(name, shape, dtype=F32):
        dr[name] = nc.dram_tensor(name, shape, dtype, kind="Internal").ap()
        return dr[name]

    din("x_in", [C, TPC])
    din("c_parv", [128, 1]); din("c_nparv", [128, 1])
    x_out = nc.dram_tensor("x_out", [C, TPC], F32, kind="ExternalOutput").ap()
    st.dbg = {}
    for tap in st.debug_taps:
        st.dbg[tap] = nc.dram_tensor(f"dbg{tap}", [C, TPC], F32,
                                     kind="ExternalOutput").ap()

    din("w_l0f_router", [C, 16]); din("w_l0f_gate", [16, C, 256])
    din("w_l0f_up", [16, C, 256]); din("w_l0f_down", [16, 256, C])
    din("w_l0f_sg", [C, 2048]); din("w_l0f_su", [C, 2048]); din("w_l0f_sd", [2048, C])
    din("w_l0a_k", [C, C]); din("w_l0a_v", [C, C]); din("w_l0a_r", [C, C])
    din("w_l0a_o", [C, C]); din("w_l0a_fk", [C, 4096]); din("w_l0a_fr", [C, C])
    din("w_l0a_fv", [4096, C]); din("w_l1a_qkv", [C, 3072])
    din("w_l1f_gate", [C, 8]); din("w_l1f_w1", [8, C, 4096]); din("w_l1f_w2", [8, 4096, C])
    CONSTS = [("c_ln1w", [128, 8]), ("c_ln1b", [128, 8]), ("c_ln2w", [128, 8]),
              ("c_ln2b", [128, 8]), ("c_decay", [128, 8]), ("c_tfirst", [128, 8]),
              ("c_tmk", [128, 8]), ("c_tmv", [128, 8]), ("c_tmr", [128, 8]),
              ("c_ftmk", [128, 8]), ("c_ftmr", [128, 8]), ("c_bqkv", [128, 24]),
              ("k_cb16", [128, 16]), ("k_gb8", [128, 8]), ("k_iota16", [128, 16]),
              ("k_iota8", [128, 8]), ("k_causal", [128, 128])]
    for nm, sh in CONSTS:
        din(nm, sh)
    din("c_l1b1", [8, 128, 32]); din("r_l1b2", [8, C])
    din("k_sel16", [16, 16 * 128]); din("k_onesr", [65, 128]); din("k_onesc", [128, 1])
    din("k_tri", [128, 128]); din("k_ident", [128, 128]); din("k_pfx", [128, 16])
    din("r_bv", [1, 1024])

    wb = {}
    W_ORDER = ["w_l0f_router", "w_l0f_sg", "w_l0f_su", "w_l0f_sd", "w_l0f_gate",
               "w_l0f_up", "w_l0f_down", "w_l0a_k", "w_l0a_v", "w_l0a_r",
               "w_l0a_o", "w_l0a_fk", "w_l0a_fr", "w_l0a_fv", "w_l1f_gate",
               "w_l1f_w1", "w_l1f_w2", "w_l1a_qkv", "r_l1b2"]
    for nm in W_ORDER:
        wb[nm] = dint(nm + "_b", list(dr[nm].shape), BF16)
    st.dr, st.wb, st.dint = dr, wb, dint
    st.wc_done = {}

    with es:
        st.x = es.enter_context(nc.sbuf_tensor("x", [128, NCH, TPC], F32))
        st.parv = es.enter_context(nc.sbuf_tensor("parv", [128, 1], F32))
        st.nparv = es.enter_context(nc.sbuf_tensor("nparv", [128, 1], F32))
        st.consts = {}
        for nm, sh in CONSTS:
            st.consts[nm] = es.enter_context(nc.sbuf_tensor(nm + "_s", sh, F32))
        st.tri_b = es.enter_context(nc.sbuf_tensor("tri_b", [128, 128], BF16))
        st.ident_b = es.enter_context(nc.sbuf_tensor("ident_b", [128, 128], BF16))
        st.onesr_b = es.enter_context(nc.sbuf_tensor("onesr_b", [65, 128], BF16))
        st.onesc_b = es.enter_context(nc.sbuf_tensor("onesc_b", [128, 1], BF16))
        st.pfx_b = es.enter_context(nc.sbuf_tensor("pfx_b", [128, 16], BF16))

        st.cLD = Ctr(nc, es, "cLD")
        st.cKC = Ctr(nc, es, "cKC")
        st.cWCf = Ctr(nc, es, "cWCf")
        st.cWCa = Ctr(nc, es, "cWCa")
        st.cWC1 = Ctr(nc, es, "cWC1")
        st.cWS = Ctr(nc, es, "cWS")
        st.cTV = Ctr(nc, es, "cTV")
        st.cVT = Ctr(nc, es, "cVT")
        st.cSC = Ctr(nc, es, "cSC")
        st.cGP = Ctr(nc, es, "cGP")
        st.cDM = Ctr(nc, es, "cDM")
        st.cOUT = Ctr(nc, es, "cOUT")
        st._pool = {}
        st._stageidx = {"E": 0, "H": 0, "G": 0}

        def _generic(cls):
            i = st._stageidx[cls]
            st._stageidx[cls] = i + 1
            name = f"{cls}{i}"
            if name not in st._pool:
                st._pool[name] = Ctr(nc, es, name)
            return CtrView(st._pool[name])

        st.Ce = lambda name=None: _generic("E")
        st.Ch = lambda name=None: _generic("H")
        st.Cg = lambda name=None: _generic("G")

        def stage_reset():
            st._stageidx = {"E": 0, "H": 0, "G": 0}
        st.stage_reset = stage_reset

        blk = es.enter_context(nc.Block())
        st.blk = blk

        _emit_preload(st)
        EMITTERS = [(1, emit_l0ffn, "s1"), (2, emit_l0attn, "s2"),
                    (3, emit_l0ffn, "s3"), (4, emit_l1ffn, "s4"),
                    (5, emit_l1attn, "s5"), (6, emit_l1ffn, "s6")]
        for i, fn, tg in EMITTERS:
            if first_stage <= i <= n_stages:
                with nc.named_scope(tg):
                    fn(st, tag=tg)
                if i in st.debug_taps:
                    _emit_tap(st, i)

        @blk.scalar
        def _(s):
            xh, xn = st.x_ready
            s.wait_ge(xh, xn)
            s.dma_start(out=x_out.rearrange("(k p) t -> p k t", p=128),
                        in_=st.x[:, :, :]).then_inc(st.cOUT.h, 16)
            st.cOUT.n += 16
            st.cOUT.wait(s)

        @blk.sync
        def _(s):
            st.cOUT.wait(s)

        @blk.gpsimd
        def _(g):
            st.cOUT.wait(g)

        @blk.vector
        def _(v):
            st.cOUT.wait(v)

        @blk.tensor
        def _(t):
            st.cOUT.wait(t)

    return nc


def _emit_tap(st, tap):
    blk, cOUT = st.blk, st.cOUT

    @blk.scalar
    def _(s):
        xh, xn = st.x_ready
        s.wait_ge(xh, xn)
        s.dma_start(out=st.dbg[tap].rearrange("(k p) t -> p k t", p=128),
                    in_=st.x[:, :, :]).then_inc(cOUT.h, 16)
        cOUT.n += 16
        cOUT.wait(s)


def _emit_preload(st):
    nc, blk, dr, wb = st.nc, st.blk, st.dr, st.wb
    cLD, cSC, cVT = st.cLD, st.cSC, st.cVT

    @blk.sync
    def _(s):
        s.dma_start(out=st.x[:, :, :],
                    in_=dr["x_in"].rearrange("(k p) t -> p k t", p=128)
                    ).then_inc(cLD.h, 16)
        cLD.n += 16
        s.dma_start(out=st.parv[:], in_=dr["c_parv"]).then_inc(cLD.h, 16)
        cLD.n += 16
        s.dma_start(out=st.nparv[:], in_=dr["c_nparv"]).then_inc(cLD.h, 16)
        cLD.n += 16
        for nm in st.consts:
            s.dma_start(out=st.consts[nm][:], in_=dr[nm]).then_inc(cLD.h, 16)
            cLD.n += 16

    @blk.gpsimd
    def _(g):
        for dst, src in [(st.tri_b, "k_tri"), (st.ident_b, "k_ident"),
                         (st.onesr_b, "k_onesr"), (st.onesc_b, "k_onesc"),
                         (st.pfx_b, "k_pfx")]:
            g.dma_start(out=dst[:], in_=dr[src]).then_inc(st.cKC.h, 16)
            st.cKC.n += 16
        GROUPS = [(st.cWCf, ["w_l0f_router", "w_l0f_sg", "w_l0f_su", "w_l0f_sd",
                             "w_l0f_gate", "w_l0f_up", "w_l0f_down"]),
                  (st.cWCa, ["w_l0a_k", "w_l0a_v", "w_l0a_r", "w_l0a_o",
                             "w_l0a_fk", "w_l0a_fr", "w_l0a_fv"]),
                  (st.cWC1, ["w_l1f_gate", "w_l1f_w1", "w_l1f_w2", "w_l1a_qkv",
                             "r_l1b2"])]
        for ctr, names_ in GROUPS:
            for nm in names_:
                src, dst = dr[nm], wb[nm]
                n = int(np.prod(src.shape))
                names = " ".join("abcdefg"[: len(src.shape)])
                pat = f"{names} -> ({names})"
                fs, fd = src.rearrange(pat), dst.rearrange(pat)
                step = 4 * 1024 * 1024
                for off in range(0, n, step):
                    ln = min(step, n - off)
                    g.dma_start(out=fd[off:off + ln],
                                in_=fs[off:off + ln]).then_inc(ctr.h, 16)
                    ctr.n += 16

    ld_done = cLD.n
    st.ld_done = ld_done
    st.x_ready = (st.cLD.h, ld_done)



def emit_l0ffn(st, tag):
    nc, blk, x = st.nc, st.blk, st.x
    dr, wb, consts = st.dr, st.wb, st.consts
    es = st.es
    cLD, cWS = st.cLD, st.cWS
    xr_h, xr_n = st.x_ready

    st.stage_reset()
    Ce, Ch, Cg = st.Ce, st.Ch, st.Cg
    sA_x, sA_q, sA_sm, sA_v, sA_sq, sA_r, sA_rep, sA_y = (
        Ce(), Ce(), Ce(), Ce(), Ce(), Ce(), Ce(), Ce())
    sA_lg, sA_sg, sA_cmb, sA_tp, sA_ev = Ce(), Ce(), Ce(), Ce(), Ce()
    sB_wgu = [Ch(), Ch()]
    sB_wge = [Ch(), Ch()]
    sB_wsd = [Ch(), Ch()]
    sB_wdn = [Ch(), Ch()]
    sB_fh, sB_sl, sB_hs = Ce(), Ce(), Ce()
    sB_dsf, sB_dde = Ce(), Ce()
    sB_cr, sB_eh, sB_esl, sB_hw = Ce(), Ce(), Ce(), Ce()
    sB_dn, sB_dv = Ce(), Ce()

    loc = contextlib.ExitStack()
    sb = lambda n, s, d: loc.enter_context(nc.sbuf_tensor(f"{n}_{tag}", s, d))
    with loc:
        yb = sb("yb", [128, NCH, TPC], BF16)
        cmbT = sb("cmbT", [16, TPC], BF16)
        hsb = sb("hsb", [128, 16, 512], BF16)
        hwb = sb("hwb", [128, 16, 2, 512], BF16)
        silu_t = sb("silu_t", [128, 2, 512], F32)
        hw_t = sb("hw_t", [128, 512], F32)
        sel16_b = sb("sel16_b", [16, 16 * 128], BF16)
        csel = st.Cg()
        wsg = sb("wsg", [128, NCH, 2, 256], BF16)
        wsu = sb("wsu", [128, NCH, 2, 256], BF16)
        wge = sb("wge", [128, NCH, 2, 256], BF16)
        wup = sb("wup", [128, NCH, 2, 256], BF16)

        gate_r = wb["w_l0f_gate"].rearrange("e (k p) h -> e p k h", p=128)
        up_r = wb["w_l0f_up"].rearrange("e (k p) h -> e p k h", p=128)
        down_r = wb["w_l0f_down"].rearrange("e (h p) c -> e p h c", p=128)
        sg_r = wb["w_l0f_sg"].rearrange("(k p) f -> p k f", p=128)
        su_r = wb["w_l0f_su"].rearrange("(k p) f -> p k f", p=128)
        sd_r = wb["w_l0f_sd"].rearrange("(f p) c -> p f c", p=128)
        wcA = st.cWCf.n
        wcB = st.cWCf.n
        NBLK4 = 4 * NGRP

        pa = contextlib.ExitStack()
        with pa:
            sba = lambda n, s, d: pa.enter_context(nc.sbuf_tensor(f"{n}_{tag}", s, d))
            mrow = sba("mrow", [1, 512], F32)
            trow = sba("trow", [1, 1024], F32)
            r0row = sba("r0row", [1, 512], F32)
            mrow_b = sba("mrow_b", [1, 512], BF16)
            r0row_b = sba("r0row_b", [1, 512], BF16)
            sco = sba("sco", [128, 16, 16], F32)
            cmb_blk = sba("cmb_blk", [128, 16, 16], BF16)
            sbb = sba("sbb", [128, 16], F32)
            msb = sba("msb", [128, 16], F32)
            gsc = sba("gsc", [128, 4], F32)
            madd = sba("madd", [128, 4], F32)
            t8 = sba("t8", [128, 8], F32)
            pad8 = sba("pad8", [128, 8], F32)
            wun = sba("wun", [128, 16], F32)
            wsum = sba("wsum", [128, 1], F32)
            wrt = sba("wrt", [128, NCH, 16], BF16)
            ps_st = pa.enter_context(nc.psum_tensor(f"psst_{tag}", [1, 1024], F32))
            ps_mr = pa.enter_context(nc.psum_tensor(f"psmr_{tag}", [128, 1024], F32))
            ps_lg = pa.enter_context(nc.psum_tensor(f"pslg_{tag}", [128, 2, 512], F32))
            psT = pa.enter_context(nc.psum_tensor(f"psT_{tag}", [128, 2, 128], BF16))

            @blk.sync
            def _(s):
                s.wait_ge(xr_h, xr_n)
                s.wait_ge(st.cWCf.h, wcA)
                s.dma_start(out=wrt[:], in_=wb["w_l0f_router"]
                            .rearrange("(k p) e -> p k e", p=128)).then_inc(cWS.h, 16)
                cWS.n += 16
            ws_rt = cWS.n

            @blk.vector
            def _(v):
                v.wait_ge(xr_h, xr_n)
                for g in range(NGRP):
                    gsl = slice(g * 512, (g + 1) * 512)
                    xsb = yb[:, :, gsl]
                    xqb = hsb[:, 0:NCH, :]
                    sA_sm.wait(v, g)
                    sA_q.wait(v, g)
                    for k in range(NCH):
                        v.tensor_copy(out=xsb[:, k, :], in_=x[:, k, gsl])
                    v.drain()
                    sA_x.bump(v)
                    sA_sm.wait(v, g + 1)
                    v.tensor_scalar_mul(out=mrow[:], in0=ps_st[0:1, 0:512],
                                        scalar1=1.0 / C)
                    v.tensor_scalar_mul(out=trow[0:1, 0:512],
                                        in0=ps_st[0:1, 512:1024], scalar1=1.0 / C)
                    v.drain()
                    v.tensor_mul(out=trow[0:1, 512:1024], in0=mrow[:], in1=mrow[:])
                    v.drain()
                    v.tensor_sub(out=trow[0:1, 0:512], in0=trow[0:1, 0:512],
                                 in1=trow[0:1, 512:1024])
                    v.drain()
                    v.tensor_scalar_add(out=trow[0:1, 512:1024],
                                        in0=trow[0:1, 0:512], scalar1=EPS)
                    v.drain()
                    sA_v.bump(v)
                    sA_sq.wait(v, g + 1)
                    v.reciprocal(out=r0row[:], in_=trow[0:1, 512:1024])
                    v.drain()
                    v.tensor_copy(out=mrow_b[:], in_=mrow[:])
                    v.tensor_copy(out=r0row_b[:], in_=r0row[:])
                    v.drain()
                    sA_r.bump(v)
                    sA_rep.wait(v, g + 1)
                    for k in range(NCH):
                        v.tensor_sub(out=hw_t[:], in0=x[:, k, gsl],
                                     in1=ps_mr[:, 0:512])
                        v.drain()
                        v.tensor_tensor(out=yb[:, k, gsl], in0=hw_t[:],
                                        in1=ps_mr[:, 512:1024], op=ALU.mult)
                        v.drain()
                    sA_y.bump(v)
                v.memset(pad8[:], -1e30)
                v.drain()
                for i in range(NBLK4):
                    sA_sg.wait(v, i + 1)
                    v.tensor_add(out=sbb[:], in0=sco[:, i, :], in1=consts["k_cb16"][:])
                    v.drain()
                    for gr in range(4):
                        v.tensor_copy(out=pad8[:, 0:4], in_=sbb[:, 4 * gr:4 * gr + 4])
                        v.drain()
                        v.max(out=t8[:], in_=pad8[:])
                        v.drain()
                        v.tensor_add(out=gsc[:, gr:gr + 1], in0=t8[:, 0:1],
                                     in1=t8[:, 1:2])
                        v.drain()
                    v.tensor_copy(out=pad8[:, 0:4], in_=gsc[:])
                    v.drain()
                    v.max(out=t8[:], in_=pad8[:])
                    v.drain()
                    v.tensor_scalar(out=madd[:], in0=gsc[:], scalar1=t8[:, 1:2],
                                    scalar2=None, op0=ALU.is_ge)
                    v.drain()
                    v.tensor_scalar(out=madd[:], in0=madd[:], scalar1=1e9,
                                    scalar2=1e9, op0=ALU.mult, op1=ALU.subtract)
                    v.drain()
                    for gr in range(4):
                        v.tensor_scalar_add(out=msb[:, 4 * gr:4 * gr + 4],
                                            in0=sbb[:, 4 * gr:4 * gr + 4],
                                            scalar1=madd[:, gr:gr + 1])
                    v.drain()
                    v.max(out=t8[:], in_=msb[:])
                    v.drain()
                    v.tensor_scalar(out=wun[:], in0=msb[:], scalar1=t8[:, 3:4],
                                    scalar2=None, op0=ALU.is_ge)
                    v.drain()
                    v.tensor_mul(out=wun[:], in0=wun[:], in1=sco[:, i, :])
                    v.drain()
                    v.reduce_sum(out=wsum[:], in_=wun[:], axis=AXT.X)
                    v.drain()
                    v.tensor_scalar_add(out=wsum[:], in0=wsum[:], scalar1=1e-20)
                    v.drain()
                    v.reciprocal(out=wsum[:], in_=wsum[:])
                    v.drain()
                    v.tensor_scalar(out=cmb_blk[:, i, :], in0=wun[:],
                                    scalar1=wsum[:, 0:1], scalar2=None,
                                    op0=ALU.mult)
                    v.drain()
                    sA_cmb.bump(v)
                    sA_tp.wait(v, i + 1)
                    v.tensor_copy(out=cmbT[0:16, i * 128:(i + 1) * 128],
                                  in_=psT[0:16, i % 2, :])
                    v.drain()
                    sA_ev.bump(v)

            @blk.scalar
            def _(s):
                for g in range(NGRP):
                    gsl = slice(g * 512, (g + 1) * 512)
                    xsb = yb[:, :, gsl]
                    xqb = hsb[:, 0:NCH, :]
                    sA_x.wait(s, g + 1)
                    for k in range(NCH):
                        s.activation(out=xqb[:, k, :], in_=xsb[:, k, :],
                                     func=AF.Square)
                    s.drain()
                    sA_q.bump(s)
                    sA_v.wait(s, g + 1)
                    s.activation(out=trow[0:1, 512:1024], in_=trow[0:1, 512:1024],
                                 func=AF.Sqrt)
                    s.drain()
                    sA_sq.bump(s)
                for i in range(NBLK4):
                    sA_lg.wait(s, i + 1)
                    s.activation(out=sco[:, i, :], in_=ps_lg[:, i % 2, 0:16],
                                 func=AF.Sigmoid)
                    s.drain()
                    sA_sg.bump(s)

            @blk.tensor
            def _(t):
                t.wait_ge(st.cKC.h, st.cKC.n)
                for g in range(NGRP):
                    gsl = slice(g * 512, (g + 1) * 512)
                    xsb = yb[:, :, gsl]
                    xqb = hsb[:, 0:NCH, :]
                    sA_x.wait(t, g + 1)
                    sA_q.wait(t, g + 1)
                    sA_r.wait(t, g)
                    for k in range(NCH):
                        t.matmul(ps_st[0:1, 0:512], st.onesc_b[:], xsb[:, k, :],
                                 start=(k == 0), stop=(k == NCH - 1))
                    for k in range(NCH):
                        last = (k == NCH - 1)
                        mm = t.matmul(ps_st[0:1, 512:1024], st.onesc_b[:],
                                      xqb[:, k, :], start=(k == 0), stop=last)
                        if last:
                            sA_sm.inc(mm)
                    sA_r.wait(t, g + 1)
                    sA_y.wait(t, g)
                    t.matmul(ps_mr[:, 0:512], st.onesr_b[0:1, :], mrow_b[:],
                             start=True, stop=True)
                    mm = t.matmul(ps_mr[:, 512:1024], st.onesr_b[0:1, :], r0row_b[:],
                                  start=True, stop=True)
                    sA_rep.inc(mm)
                t.wait_ge(cWS.h, ws_rt)
                for i in range(NBLK4):
                    g = i // 4
                    sA_y.wait(t, g + 1)
                    sA_sg.wait(t, max(0, i - 1))
                    bsl = slice(i * 128, (i + 1) * 128)
                    for k in range(NCH):
                        mm = t.matmul(ps_lg[:, i % 2, 0:16], yb[:, k, bsl],
                                      wrt[:, k, :], start=(k == 0),
                                      stop=(k == NCH - 1))
                    sA_lg.inc(mm)
                    sA_cmb.wait(t, i + 1)
                    sA_ev.wait(t, max(0, i - 1))
                    tp = t.transpose(psT[0:16, i % 2, :], cmb_blk[:, i, :],
                                     st.ident_b[:])
                    sA_tp.inc(tp)

        wdn = sb("wdn", [128, 2, 2, 1024], BF16)
        wsd = sb("wsd", [128, 2, 1024], BF16)
        @blk.gpsimd
        def _(g_):
            g_.wait_ge(xr_h, xr_n)
            g_.dma_start(out=sel16_b[:], in_=dr["k_sel16"]).then_inc(csel.h, 16)
            csel.note(16)

        def _ld_expert(s_, qt):
            if qt >= NGRP * 16:
                return
            et = qt % 16
            s_.dma_start(out=wge[:, :, et % 2, :],
                         in_=gate_r[et]).then_inc(sB_wge[qt % 2].h, 16)
            sB_wge[qt % 2].note(16)
            s_.dma_start(out=wup[:, :, et % 2, :],
                         in_=up_r[et]).then_inc(sB_wge[qt % 2].h, 16)
            sB_wge[qt % 2].note(16)

        st.ld_expert = _ld_expert

        @blk.sync
        def _(s):
            for g in range(NGRP):
                for j in range(8):
                    q = g * 8 + j
                    if q >= 2:
                        sB_fh.wait(s, (q - 2) * 2 + 2)
                    s.dma_start(out=wsg[:, :, j % 2, :],
                                in_=sg_r[:, :, j * 256:j * 256 + 256]
                                ).then_inc(sB_wgu[q % 2].h, 16)
                    sB_wgu[q % 2].note(16)
                    s.dma_start(out=wsu[:, :, j % 2, :],
                                in_=su_r[:, :, j * 256:j * 256 + 256]
                                ).then_inc(sB_wgu[q % 2].h, 16)
                    sB_wgu[q % 2].note(16)
                for f in range(16):
                    q = g * 16 + f
                    if q >= 2:
                        sB_dsf.wait(s, q - 1)
                    s.dma_start(out=wsd[:, f % 2, :],
                                in_=sd_r[:, f, :]).then_inc(sB_wsd[q % 2].h, 16)
                    sB_wsd[q % 2].note(16)
                for e in range(16):
                    q = g * 16 + e
                    if q >= 2:
                        sB_dde.wait(s, q - 1)
                    s.dma_start(out=wdn[:, :, e % 2, :],
                                in_=down_r[e]).then_inc(sB_wdn[q % 2].h, 16)
                    sB_wdn[q % 2].note(16)

        for g in range(NGRP):
            gsl = slice(g * 512, (g + 1) * 512)
            ph = contextlib.ExitStack()
            with ph:
                ps_g = ph.enter_context(
                    nc.psum_tensor(f"psg_{tag}{g}", [128, 2, 512], F32))
                ps_u = ph.enter_context(
                    nc.psum_tensor(f"psu_{tag}{g}", [128, 2, 512], F32))
                ps_cmb = ph.enter_context(
                    nc.psum_tensor(f"pscmb_{tag}{g}", [128, 512], F32))

                @blk.tensor
                def _(t):
                    if g == 0:
                        sA_ev.wait(t, NBLK4)
                    else:
                        sB_dv.wait(t, g)
                    for f in range(16):
                        q16 = g * 8 + f // 2
                        sB_wgu[q16 % 2].wait(t, 32 * (q16 // 2 + 1))
                        if f >= 2:
                            sB_hs.wait(t, g * 16 + f - 1)
                        wo = slice((f % 2) * 128, (f % 2) * 128 + 128)
                        for k in range(NCH):
                            t.matmul(ps_g[:, f % 2, :],
                                     wsg[:, k, (f // 2) % 2, wo],
                                     yb[:, k, gsl], start=(k == 0),
                                     stop=(k == NCH - 1))
                        for k in range(NCH):
                            last = (k == NCH - 1)
                            mm = t.matmul(ps_u[:, f % 2, :],
                                          wsu[:, k, (f // 2) % 2, wo],
                                          yb[:, k, gsl], start=(k == 0), stop=last)
                            if last:
                                sB_fh.inc(mm)
                    sB_hs.wait(t, g * 16 + 16)
                    for e in range(16):
                        q = g * 16 + e
                        sB_wge[q % 2].wait(t, 32 * (q // 2 + 1))
                        if e > 0 or g > 0:
                            sB_hw.wait(t, q * 2)
                        if gg_sel_w := (g == 0 and e == 0):
                            csel.wait(t)
                        mm = t.matmul(ps_cmb[:, :], sel16_b[:, e * 128:(e + 1) * 128],
                                      cmbT[0:16, gsl], start=True, stop=True)
                        sB_cr.inc(mm)
                        for hc in range(2):
                            qh = q * 2 + hc
                            if qh >= 2:
                                sB_hw.wait(t, qh - 1)
                            for k in range(NCH):
                                t.matmul(ps_g[:, hc, :],
                                         wge[:, k, e % 2, hc * 128:hc * 128 + 128],
                                         yb[:, k, gsl], start=(k == 0),
                                         stop=(k == NCH - 1))
                            for k in range(NCH):
                                last = (k == NCH - 1)
                                mm = t.matmul(ps_u[:, hc, :],
                                              wup[:, k, e % 2, hc * 128:hc * 128 + 128],
                                              yb[:, k, gsl], start=(k == 0),
                                              stop=last)
                                if last:
                                    sB_eh.inc(mm)

                @blk.scalar
                def _(s):
                    if g == 0:
                        s.wait_ge(st.cWCf.h, wcB)
                        _ld_expert(s, 0)
                        _ld_expert(s, 1)
                    for f in range(16):
                        sB_fh.wait(s, g * 16 + f + 1)
                        if f >= 2:
                            sB_hs.wait(s, g * 16 + f - 1)
                        s.activation(out=silu_t[:, f % 2, :], in_=ps_g[:, f % 2, :],
                                     func=AF.Silu)
                        s.drain()
                        sB_sl.bump(s)
                    for e in range(16):
                        for hc in range(2):
                            qh = (g * 16 + e) * 2 + hc
                            sB_eh.wait(s, qh + 1)
                            if qh >= 2:
                                sB_hw.wait(s, qh - 1)
                            s.activation(out=silu_t[:, hc, :], in_=ps_g[:, hc, :],
                                         func=AF.Silu)
                            s.drain()
                            sB_esl.bump(s)
                        _ld_expert(s, g * 16 + e + 2)

                @blk.vector
                def _(v):
                    for f in range(16):
                        sB_sl.wait(v, g * 16 + f + 1)
                        v.tensor_tensor(out=hsb[:, f, :], in0=silu_t[:, f % 2, :],
                                        in1=ps_u[:, f % 2, :], op=ALU.mult)
                        v.drain()
                        sB_hs.bump(v)
                    for e in range(16):
                        sB_cr.wait(v, g * 16 + e + 1)
                        for hc in range(2):
                            sB_esl.wait(v, (g * 16 + e) * 2 + hc + 1)
                            v.tensor_tensor(out=hw_t[:], in0=silu_t[:, hc, :],
                                            in1=ps_u[:, hc, :], op=ALU.mult)
                            v.drain()
                            v.tensor_tensor(out=hwb[:, e, hc, :], in0=hw_t[:],
                                            in1=ps_cmb[:, :], op=ALU.mult)
                            v.drain()
                            sB_hw.bump(v)

            pd = contextlib.ExitStack()
            with pd:
                ps_o = pd.enter_context(
                    nc.psum_tensor(f"pso_{tag}{g}", [128, NCH, 512], F32))

                @blk.tensor
                def _(t):
                    for f in range(16):
                        q = g * 16 + f
                        sB_wsd[q % 2].wait(t, 16 * (q // 2 + 1))
                        if f == 0:
                            sB_hs.wait(t, g * 16 + 16)
                            sB_hw.wait(t, (g + 1) * 32)
                        for cc in range(NCH):
                            mm = t.matmul(ps_o[:, cc, :],
                                          wsd[:, f % 2, cc * 128:cc * 128 + 128],
                                          hsb[:, f, :], start=(f == 0), stop=False)
                        sB_dsf.inc(mm)
                    for e in range(16):
                        q = g * 16 + e
                        sB_wdn[q % 2].wait(t, 16 * (q // 2 + 1))
                        if e == 0:
                            sB_hw.wait(t, (g * 16 + 16) * 2)
                        for hc in range(2):
                            for cc in range(NCH):
                                last = (e == 15 and hc == 1)
                                mm = t.matmul(ps_o[:, cc, :],
                                              wdn[:, hc, e % 2, cc * 128:cc * 128 + 128],
                                              hwb[:, e, hc, :],
                                              start=False, stop=last)
                        sB_dde.inc(mm)

                @blk.vector
                def _(v):
                    sB_dde.wait(v, 16 * (g + 1))
                    for cc in range(NCH):
                        v.scalar_tensor_tensor(out=x[:, cc, gsl],
                                               in0=ps_o[:, cc, :], scalar=0.5,
                                               in1=x[:, cc, gsl],
                                               op0=ALU.mult, op1=ALU.add)
                        v.drain()
                    sB_dv.bump(v)

        st.x_ready = (sB_dv.h, sB_dv.c.n)


def emit_l0attn(st, tag):
    nc, blk, x = st.nc, st.blk, st.x
    dr, wb, consts = st.dr, st.wb, st.consts
    es = st.es
    xr_h, xr_n = st.x_ready
    HTPC = TPC // 2
    HT = 8

    st.stage_reset()
    Ce, Ch, Cg = st.Ce, st.Ch, st.Cg
    c0s, c0v = Ce(), Ce()
    c1x, c1q, c1sm, c1v, c1sq, c1r, c1rep, c1h = (
        Ce(), Ce(), Ce(), Ce(), Ce(), Ce(), Ce(), Ce())
    cd1, cag1, cd1b, c15v, cdp1 = Cg(), Ce(), Cg(), Ce(), Ce()
    c2rep, c2m, c2kps, c2kev, c2vps, c2vev = Ce(), Ce(), Ce(), Ce(), Ce(), Ce()
    c2m2 = Ce()
    c2wk = [Ch(), Ch()]
    c2wv = [Ch(), Ch()]
    c3pk, c3f = Ce(), Ce()
    c3dn, c3iv, c3wv = Ce(), Ce(), Ce()
    cd3, cag3, cd3b, c3st, cdp3 = Cg(), Ce(), Cg(), Ce(), Ce()
    c4rep, c4m, c4rps, c4sg, c4rw, c4o, c4ev = (
        Ce(), Ce(), Ce(), Ce(), Ce(), Ce(), Ce())
    c4wr = [Ch(), Ch()]
    c4wo = [Ch(), Ch()]
    c5q, c5sm, c5v, c5sq, c5r, c5rep, c5h = (
        Ce(), Ce(), Ce(), Ce(), Ce(), Ce(), Ce())
    cd5, cag5, cd5b, c55v, cdp5 = Cg(), Ce(), Cg(), Ce(), Ce()
    c5m, c5rps, c5sg, c5kps, c5kr, c5kk, c5fv, c5ev = (
        Ce(), Ce(), Ce(), Ce(), Ce(), Ce(), Ce(), Ce())
    c5wr = [Ch(), Ch()]
    c5wk = [Ch(), Ch()]
    c5wv = [Ch(), Ch()]

    cin1 = st.dint(f"cin1_{tag}", [128, 128])
    cout1 = st.dint(f"cout1_{tag}", [256, 128])
    cins = [st.dint(f"cins{h}_{tag}", [128, 128]) for h in range(2)]
    couts = [st.dint(f"couts{h}_{tag}", [256, 128]) for h in range(2)]
    cin2 = st.dint(f"cin2_{tag}", [128, 128])
    cout2 = st.dint(f"cout2_{tag}", [256, 128])

    kw_r = wb["w_l0a_k"].rearrange("(k p) d -> p k d", p=128)
    vw_r = wb["w_l0a_v"].rearrange("(k p) d -> p k d", p=128)
    rw_r = wb["w_l0a_r"].rearrange("(k p) d -> p k d", p=128)
    ow_r = wb["w_l0a_o"].rearrange("(d p) c -> p d c", p=128)
    fk_r = wb["w_l0a_fk"].rearrange("(k p) f -> p k f", p=128)
    fr_r = wb["w_l0a_fr"].rearrange("(k p) d -> p k d", p=128)
    fv_r = wb["w_l0a_fv"].rearrange("(f p) c -> p f c", p=128)

    loc = contextlib.ExitStack()
    sb = lambda n, s, d: loc.enter_context(nc.sbuf_tensor(f"{n}_{tag}", s, d))
    with loc:
        xb = sb("xb", [128, NCH, TPC], BF16)
        rows = sb("rows", [96, TPC], BF16)
        dvec = sb("dvec", [128, 8], F32)
        euvec = sb("euvec", [128, 8], F32)
        d128 = sb("d128", [128, 8], F32)
        d128c = sb("d128c", [128, 8, 16], F32)
        dprof = sb("dprof", [128, NCH, 128], BF16)
        dcol = sb("dcol", [128, NCH, 128], BF16)
        zero8 = sb("zero8", [128, 8], F32)
        hlast = sb("hlast", [128, NCH, 16], BF16)
        agt1 = sb("agt1", [128, 2, NCH, 16], BF16)
        prevblk = sb("prevblk", [128, 2, NCH, 16], BF16)
        prevh = sb("prevh", [128, NCH, 16], BF16)
        carry = sb("carry", [128, 2, NCH], F32)
        rowf = sb("rowf", [1, 3 * 512], F32)

        @blk.scalar
        def _(s):
            s.wait_ge(st.cLD.h, st.ld_done)
            s.wait_ge(xr_h, xr_n)
            s.activation(out=dvec[:], in_=consts["c_decay"][:], func=AF.Exp)
            s.drain()
            s.activation(out=dvec[:], in_=dvec[:], func=AF.Exp, scale=-1.0)
            s.drain()
            s.activation(out=euvec[:], in_=consts["c_tfirst"][:], func=AF.Exp)
            s.drain()
            c0s.bump(s)

        @blk.vector
        def _(v):
            c0s.wait(v)
            p0 = contextlib.ExitStack()
            with p0:
                dpf = p0.enter_context(
                    nc.sbuf_tensor(f"dpf_{tag}", [128, 128], F32))
                for j in range(NCH):
                    v.memset(dpf[:, 0:1], 1.0)
                    v.drain()
                    ln = 1
                    while ln < 128:
                        step = min(ln, 128 - ln)
                        v.tensor_scalar_mul(out=dpf[:, ln:ln + step],
                                            in0=dpf[:, 0:step],
                                            scalar1=dpf[:, ln - 1:ln])
                        v.drain()
                        v.tensor_scalar_mul(out=dpf[:, ln:ln + step],
                                            in0=dpf[:, ln:ln + step],
                                            scalar1=dvec[:, j:j + 1])
                        v.drain()
                        ln += step
                    v.tensor_copy(out=dprof[:, j, :], in_=dpf[:])
                    v.drain()
                    v.tensor_scalar_mul(out=d128[:, j:j + 1],
                                        in0=dpf[:, 127:128],
                                        scalar1=dvec[:, j:j + 1])
                    v.drain()
                    v.memset(dcol[:, j, :], 1.0)
                    v.drain()
                    v.tensor_scalar_mul(out=dcol[:, j, :], in0=dcol[:, j, :],
                                        scalar1=dvec[:, j:j + 1])
                    v.drain()
                    v.memset(d128c[:, j, :], 1.0)
                    v.drain()
                    v.tensor_scalar_mul(out=d128c[:, j, :], in0=d128c[:, j, :],
                                        scalar1=d128[:, j:j + 1])
                    v.drain()
                v.memset(zero8[:], 0.0)
                v.drain()
            c0v.bump(v)

        p1 = contextlib.ExitStack()
        with p1:
            s1 = lambda n, s_, d: p1.enter_context(nc.sbuf_tensor(f"{n}_{tag}", s_, d))
            xcb = s1("xcb", [128, NCH, 512], BF16)
            xqb = s1("xqb", [128, NCH, 512], BF16)
            ps_st = p1.enter_context(nc.psum_tensor(f"ps1st_{tag}", [1, 1024], F32))
            ps_mr = p1.enter_context(nc.psum_tensor(f"ps1mr_{tag}", [128, 1024], F32))

            @blk.vector
            def _(v):
                v.wait_ge(xr_h, xr_n)
                for g in range(NGRP):
                    gsl = slice(g * 512, (g + 1) * 512)
                    c1sm.wait(v, g)
                    c1q.wait(v, g)
                    for k in range(NCH):
                        v.tensor_copy(out=xcb[:, k, :], in_=x[:, k, gsl])
                    v.drain()
                    c1x.bump(v)
                    c1sm.wait(v, g + 1)
                    v.tensor_scalar_mul(out=rowf[0:1, 0:512], in0=ps_st[0:1, 0:512],
                                        scalar1=1.0 / C)
                    v.drain()
                    v.tensor_mul(out=rowf[0:1, 1024:1536], in0=rowf[0:1, 0:512],
                                 in1=rowf[0:1, 0:512])
                    v.drain()
                    v.tensor_scalar_mul(out=rowf[0:1, 512:1024],
                                        in0=ps_st[0:1, 512:1024], scalar1=1.0 / C)
                    v.drain()
                    v.tensor_sub(out=rowf[0:1, 512:1024], in0=rowf[0:1, 512:1024],
                                 in1=rowf[0:1, 1024:1536])
                    v.drain()
                    v.tensor_scalar_add(out=rowf[0:1, 1024:1536],
                                        in0=rowf[0:1, 512:1024], scalar1=EPS)
                    v.drain()
                    v.tensor_scalar(out=rowf[0:1, 512:1024], in0=rowf[0:1, 512:1024],
                                    scalar1=1.0 + EPS, scalar2=EPS * EPS,
                                    op0=ALU.mult, op1=ALU.add)
                    v.drain()
                    c1v.bump(v)
                    c1sq.wait(v, g + 1)
                    v.tensor_copy(out=rows[0:1, gsl], in_=rowf[0:1, 0:512])
                    v.drain()
                    v.reciprocal(out=rowf[0:1, 0:512], in_=rowf[0:1, 512:1024])
                    v.drain()
                    v.tensor_copy(out=rows[32:33, gsl], in_=rowf[0:1, 0:512])
                    v.drain()
                    v.reciprocal(out=rowf[0:1, 0:512], in_=rowf[0:1, 1024:1536])
                    v.drain()
                    v.tensor_copy(out=rows[64:65, gsl], in_=rowf[0:1, 0:512])
                    v.drain()
                    c1r.bump(v)
                    c1rep.wait(v, g + 1)
                    for zz in range(4):
                        zt = g * 4 + zz
                        col = zz * 128 + 127
                        for k in range(NCH):
                            v.tensor_scalar(out=hlast[:, k, zt:zt + 1],
                                            in0=x[:, k, g * 512 + col:g * 512 + col + 1],
                                            scalar1=ps_mr[:, col:col + 1],
                                            scalar2=ps_mr[:, 512 + col:512 + col + 1],
                                            op0=ALU.subtract, op1=ALU.mult)
                        v.drain()
                        for k in range(NCH):
                            v.tensor_scalar(out=hlast[:, k, zt:zt + 1],
                                            in0=hlast[:, k, zt:zt + 1],
                                            scalar1=consts["c_ln1w"][:, k:k + 1],
                                            scalar2=consts["c_ln1b"][:, k:k + 1],
                                            op0=ALU.mult, op1=ALU.add)
                        v.drain()
                    c1h.bump(v)

            @blk.scalar
            def _(s):
                for g in range(NGRP):
                    c1x.wait(s, g + 1)
                    for k in range(NCH):
                        s.activation(out=xqb[:, k, :], in_=xcb[:, k, :],
                                     func=AF.Square)
                    s.drain()
                    c1q.bump(s)
                    c1v.wait(s, g + 1)
                    s.activation(out=rowf[0:1, 512:1024], in_=rowf[0:1, 512:1024],
                                 func=AF.Sqrt)
                    s.drain()
                    s.activation(out=rowf[0:1, 1024:1536], in_=rowf[0:1, 1024:1536],
                                 func=AF.Sqrt)
                    s.drain()
                    c1sq.bump(s)

            @blk.tensor
            def _(t):
                t.wait_ge(st.cKC.h, st.cKC.n)
                for g in range(NGRP):
                    c1x.wait(t, g + 1)
                    c1q.wait(t, g + 1)
                    c1r.wait(t, g)
                    for k in range(NCH):
                        t.matmul(ps_st[0:1, 0:512], st.onesc_b[:], xcb[:, k, :],
                                 start=(k == 0), stop=(k == NCH - 1))
                    for k in range(NCH):
                        last = (k == NCH - 1)
                        mm = t.matmul(ps_st[0:1, 512:1024], st.onesc_b[:],
                                      xqb[:, k, :], start=(k == 0), stop=last)
                        if last:
                            c1sm.inc(mm)
                    c1r.wait(t, g + 1)
                    c1h.wait(t, g)
                    t.matmul(ps_mr[:, 0:512], st.onesr_b[0:1, :], rows[0:1,
                             g * 512:(g + 1) * 512], start=True, stop=True)
                    mm = t.matmul(ps_mr[:, 512:1024], st.onesr_b[32:33, :],
                                  rows[32:33, g * 512:(g + 1) * 512],
                                  start=True, stop=True)
                    c1rep.inc(mm)

            @blk.gpsimd
            def _(g_):
                c1h.wait(g_, NGRP)
                g_.dma_start(out=cin1, in_=hlast.ap().rearrange("p k z -> p (k z)")
                             ).then_inc(cd1.h, 16)
                cd1.note(16)
                cd1.wait(g_)
                cc = g_.collective_compute(
                    "AllGather", ALU.bypass,
                    replica_groups=[[0, 1], [2, 3], [4, 5], [6, 7]],
                    ins=[cin1], outs=[cout1])
                cag1.inc(cc)
                cag1.wait(g_)
                g_.dma_start(out=agt1.ap().rearrange("p r k z -> p r (k z)"),
                             in_=cout1.rearrange("(r p) n -> p r n", p=128)
                             ).then_inc(cd1b.h, 16)
                cd1b.note(16)


            @blk.vector
            def _(v):
                cd1b.wait(v)
                for k in range(NCH):
                    v.memset(prevblk[:, 0, k, 0:1], 0.0)
                v.drain()
                v.tensor_copy(out=prevblk[:, 0, :, 1:16], in_=agt1[:, 1, :, 0:15])
                v.drain()
                v.tensor_copy(out=prevblk[:, 1, :, :], in_=agt1[:, 0, :, :])
                v.drain()
                v.tensor_scalar_mul(
                    out=prevh.ap().rearrange("p k z -> p (k z)"),
                    in0=prevblk.ap().rearrange("p q k z -> p (q k z)")[:, 0:NCH * 16],
                    scalar1=st.nparv[:, 0:1])
                v.drain()
                v.scalar_tensor_tensor(
                    out=prevh.ap().rearrange("p k z -> p (k z)"),
                    in0=prevblk.ap().rearrange("p q k z -> p (q k z)")
                    [:, NCH * 16:2 * NCH * 16],
                    scalar=st.parv[:, 0:1],
                    in1=prevh.ap().rearrange("p k z -> p (k z)"),
                    op0=ALU.mult, op1=ALU.add)
                v.drain()
                c15v.bump(v)
                cdp1.bump(v, 16)

        p24 = contextlib.ExitStack()
        with p24:
            s2 = lambda n, s_, d: p24.enter_context(nc.sbuf_tensor(f"{n}_{tag}", s_, d))
            kb = s2("kb", [128, NCH, HTPC], BF16)
            vb = s2("vb", [128, NCH, HTPC], BF16)
            wkv = s2("wkv", [128, NCH, HTPC], BF16)
            h_b = s2("h_b", [128, NCH, 512], BF16)
            t1b = s2("t1b", [128, 512], BF16)
            xkb = s2("xkb", [128, NCH, 512], BF16)
            srb = s2("srb", [128, 2, 512], BF16)
            wst1 = s2("wst1", [128, NCH, 2, 128], BF16)
            wst2 = s2("wst2", [128, NCH, 2, 128], BF16)
            pkf = s2("pkf", [128, HTPC], F32)
            pkv = s2("pkv", [128, HTPC], BF16)
            numb = s2("numb", [128, HTPC], BF16)
            denb = s2("denb", [128, HTPC], BF16)
            aext = s2("aext", [128, HT, 132], BF16)
            bext = s2("bext", [128, HT, 132], BF16)
            afin = s2("afin", [128, NCH, HT, 2], F32)
            agts = s2("agts", [128, 2, NCH, HT, 2], F32)
            gext = s2("gext", [128, 2, NCH, 17], F32)
            ablk = s2("ablk", [128, 2, 2, NCH, 8], F32)
            ain = s2("ain", [128, 2, NCH, 8], F32)
            ps_mr2 = p24.enter_context(nc.psum_tensor(f"ps2mr_{tag}", [128, 1536], F32))
            ps_k = p24.enter_context(nc.psum_tensor(f"ps2k_{tag}", [128, 2, 512], F32))
            ps_v = p24.enter_context(nc.psum_tensor(f"ps2v_{tag}", [128, 2, 512], F32))

            for hf in range(2):
                for g2 in range(2):
                    gg = hf * 2 + g2
                    gsl = slice(gg * 512, (gg + 1) * 512)
                    hsl = slice(g2 * 512, (g2 + 1) * 512)

                    @blk.sync
                    def _(s):
                        if gg == 0:
                            c1sm.wait(s, NGRP)
                            s.wait_ge(st.cWCa.h, st.cWCa.n)
                        for oc in range(8):
                            qi = gg * 8 + oc
                            if qi >= 2:
                                c2kps.wait(s, qi - 1)
                            s.dma_start(out=wst1[:, :, oc % 2, :],
                                        in_=kw_r[:, :, oc * 128:(oc + 1) * 128]
                                        ).then_inc(c2wk[qi % 2].h, 16)
                            c2wk[qi % 2].note(16)
                        for oc in range(8):
                            qi = gg * 8 + oc
                            if qi >= 2:
                                c2vps.wait(s, qi - 1)
                            s.dma_start(out=wst2[:, :, oc % 2, :],
                                        in_=vw_r[:, :, oc * 128:(oc + 1) * 128]
                                        ).then_inc(c2wv[qi % 2].h, 16)
                            c2wv[qi % 2].note(16)

                    @blk.tensor
                    def _(t):
                        if gg == 0:
                            t.wait_ge(st.cWCa.h, st.cWCa.n)
                        c1r.wait(t, gg + 1)
                        c2m.wait(t, gg)
                        if hf > 0 and g2 == 0:
                            c4ev.wait(t, hf * 16)
                        t.matmul(ps_mr2[:, 0:512], st.onesr_b[0:1, :], rows[0:1, gsl],
                                 start=True, stop=True)
                        mm = t.matmul(ps_mr2[:, 512:1024], st.onesr_b[32:33, :],
                                      rows[32:33, gsl], start=True, stop=True)
                        c2rep.inc(mm)
                        c2m.wait(t, gg + 1)
                        for oc in range(8):
                            qi = gg * 8 + oc
                            c2wk[qi % 2].wait(t, 16 * (qi // 2 + 1))
                            if qi >= 2:
                                c2kev.wait(t, qi - 1)
                            for ic in range(NCH):
                                last = ic == NCH - 1
                                mm = t.matmul(ps_k[:, oc % 2, :],
                                              wst1[:, ic, oc % 2, :],
                                              xkb[:, ic, :],
                                              start=(ic == 0), stop=last)
                            c2kps.inc(mm)
                        c2m2.wait(t, gg + 1)
                        for oc in range(8):
                            qi = gg * 8 + oc
                            c2wv[qi % 2].wait(t, 16 * (qi // 2 + 1))
                            if qi >= 2:
                                c2vev.wait(t, qi - 1)
                            for ic in range(NCH):
                                last = ic == NCH - 1
                                mm = t.matmul(ps_v[:, oc % 2, :],
                                              wst2[:, ic, oc % 2, :],
                                              xkb[:, ic, :],
                                              start=(ic == 0), stop=last)
                            c2vps.inc(mm)

                    @blk.vector
                    def _(v):
                        if gg == 0:
                            c1sm.wait(v, NGRP)
                        c2rep.wait(v, gg + 1)
                        cdp1.wait(v)
                        for k in range(NCH):
                            v.tensor_sub(out=t1b[:], in0=x[:, k, gsl],
                                         in1=ps_mr2[:, 0:512])
                            v.drain()
                            v.tensor_tensor(out=h_b[:, k, :], in0=t1b[:],
                                            in1=ps_mr2[:, 512:1024], op=ALU.mult)
                            v.drain()
                            v.tensor_scalar(out=h_b[:, k, :], in0=h_b[:, k, :],
                                            scalar1=consts["c_ln1w"][:, k:k + 1],
                                            scalar2=consts["c_ln1b"][:, k:k + 1],
                                            op0=ALU.mult, op1=ALU.add)
                            v.drain()
                            v.tensor_sub(out=t1b[:, 1:512], in0=h_b[:, k, 1:512],
                                         in1=h_b[:, k, 0:511])
                            v.drain()
                            for zz in range(4):
                                zt = hf * 8 + g2 * 4 + zz
                                cc0 = zz * 128
                                v.tensor_sub(out=t1b[:, cc0:cc0 + 1],
                                             in0=h_b[:, k, cc0:cc0 + 1],
                                             in1=prevh[:, k, zt:zt + 1])
                            v.drain()
                            v.scalar_tensor_tensor(
                                out=xkb[:, k, 1:512], in0=t1b[:, 1:512],
                                scalar=consts["c_tmk"][:, k:k + 1],
                                in1=h_b[:, k, 0:511],
                                op0=ALU.mult, op1=ALU.add)
                            v.drain()
                            for zz in range(4):
                                zt = hf * 8 + g2 * 4 + zz
                                cc0 = zz * 128
                                v.scalar_tensor_tensor(
                                    out=xkb[:, k, cc0:cc0 + 1],
                                    in0=t1b[:, cc0:cc0 + 1],
                                    scalar=consts["c_tmk"][:, k:k + 1],
                                    in1=prevh[:, k, zt:zt + 1],
                                    op0=ALU.mult, op1=ALU.add)
                            v.drain()
                        c2m.bump(v)
                        for oc in range(8):
                            qi = gg * 8 + oc
                            c2kps.wait(v, qi + 1)
                            v.tensor_copy(out=kb[:, oc, hsl], in_=ps_k[:, oc % 2, :])
                            v.drain()
                            c2kev.bump(v)
                        for k in range(NCH):
                            v.tensor_sub(out=t1b[:, 1:512], in0=h_b[:, k, 1:512],
                                         in1=h_b[:, k, 0:511])
                            v.drain()
                            for zz in range(4):
                                zt = hf * 8 + g2 * 4 + zz
                                cc0 = zz * 128
                                v.tensor_sub(out=t1b[:, cc0:cc0 + 1],
                                             in0=h_b[:, k, cc0:cc0 + 1],
                                             in1=prevh[:, k, zt:zt + 1])
                            v.drain()
                            v.scalar_tensor_tensor(
                                out=xkb[:, k, 1:512], in0=t1b[:, 1:512],
                                scalar=consts["c_tmv"][:, k:k + 1],
                                in1=h_b[:, k, 0:511],
                                op0=ALU.mult, op1=ALU.add)
                            v.drain()
                            for zz in range(4):
                                zt = hf * 8 + g2 * 4 + zz
                                cc0 = zz * 128
                                v.scalar_tensor_tensor(
                                    out=xkb[:, k, cc0:cc0 + 1],
                                    in0=t1b[:, cc0:cc0 + 1],
                                    scalar=consts["c_tmv"][:, k:k + 1],
                                    in1=prevh[:, k, zt:zt + 1],
                                    op0=ALU.mult, op1=ALU.add)
                            v.drain()
                        c2m2.bump(v)

                    @blk.scalar
                    def _(s):
                        for oc in range(8):
                            qi = gg * 8 + oc
                            c2vps.wait(s, qi + 1)
                            s.copy(out=vb[:, oc, hsl], in_=ps_v[:, oc % 2, :])
                            s.drain()
                            c2vev.bump(s)

                @blk.scalar
                def _(s):
                    for pas in range(2):
                        for j in range(NCH):
                            jj = (hf * 2 + pas) * 8 + j
                            c2kev.wait(s, 16 * (hf + 1))
                            c2vev.wait(s, 16 * (hf + 1))
                            if pas == 0:
                                if hf * 8 + j > 0:
                                    c3f.wait(s, hf * 8 + j)
                            else:
                                c3st.wait(s, 8 * (hf + 1))
                                if j > 0:
                                    c3wv.wait(s, hf * 8 + j)
                            s.activation(out=pkf[:], in_=kb[:, j, :], func=AF.Exp)
                            s.drain()
                            c3pk.bump(s)
                            if pas == 1:
                                c3dn.wait(s, hf * 8 + j + 1)
                                s.activation(out=pkv[:], in_=denb[:], func=AF.Ln)
                                s.drain()
                                s.activation(out=pkv[:], in_=pkv[:], func=AF.Exp,
                                             scale=-1.0)
                                s.drain()
                                c3iv.bump(s)

                @blk.vector
                def _(v):
                    def chunk_scans(pas, j):
                        jj = (hf * 2 + pas) * 8 + j
                        c3pk.wait(v, jj + 1)
                        v.tensor_tensor(out=pkv[:], in0=pkf[:],
                                        in1=vb[:, j, :], op=ALU.mult)
                        v.drain()
                        v.memset(aext[:, :, 0:1], 0.0)
                        v.memset(bext[:, :, 0:1], 0.0)
                        v.drain()
                        for z in range(HT):
                            zsl = slice(z * 128, (z + 1) * 128)
                            v.tensor_tensor_scan(
                                out=aext[:, z, 1:129], data0=dcol[:, j, :],
                                data1=pkv[:, zsl], initial=0.0,
                                op0=ALU.mult, op1=ALU.add)
                            v.tensor_tensor_scan(
                                out=bext[:, z, 1:129], data0=dcol[:, j, :],
                                data1=pkf[:, zsl], initial=0.0,
                                op0=ALU.mult, op1=ALU.add)
                        v.drain()

                    for j in range(NCH):
                        chunk_scans(0, j)
                        v.tensor_copy(out=afin[:, j, :, 0], in_=aext[:, :, 128])
                        v.tensor_copy(out=afin[:, j, :, 1], in_=bext[:, :, 128])
                        v.drain()
                        c3f.bump(v)
                    cd3b.wait(v, 16 * (hf + 1))
                    for j in range(NCH):
                        for ab in range(2):
                            v.tensor_copy(out=numb[:, 0:16:2],
                                          in_=agts[:, 0, j, :, ab])
                            v.tensor_copy(out=numb[:, 1:16:2],
                                          in_=agts[:, 1, j, :, ab])
                            v.drain()
                            ini = (zero8[:, j:j + 1] if hf == 0
                                   else carry[:, ab, j:j + 1])
                            v.tensor_tensor_scan(
                                out=gext[:, ab, j, 1:17], data0=d128c[:, j, :],
                                data1=numb[:, 0:16], initial=ini,
                                op0=ALU.mult, op1=ALU.add)
                            v.drain()
                            v.tensor_copy(out=gext[:, ab, j, 0:1], in_=ini)
                            v.drain()
                            v.tensor_copy(out=carry[:, ab, j:j + 1],
                                          in_=gext[:, ab, j, 16:17])
                            v.drain()
                            for p_ in range(2):
                                v.tensor_copy(out=ablk[:, p_, ab, j, :],
                                              in_=gext[:, ab, j, p_:16:2])
                            v.drain()
                    v.tensor_scalar_mul(
                        out=ain.ap().rearrange("p a k z -> p (a k z)"),
                        in0=ablk.ap().rearrange("p q a k z -> p (q a k z)")
                        [:, 0:2 * NCH * 8],
                        scalar1=st.nparv[:, 0:1])
                    v.drain()
                    v.scalar_tensor_tensor(
                        out=ain.ap().rearrange("p a k z -> p (a k z)"),
                        in0=ablk.ap().rearrange("p q a k z -> p (q a k z)")
                        [:, 2 * NCH * 8:4 * NCH * 8],
                        scalar=st.parv[:, 0:1],
                        in1=ain.ap().rearrange("p a k z -> p (a k z)"),
                        op0=ALU.mult, op1=ALU.add)
                    v.drain()
                    c3st.bump(v, 8)
                    for j in range(NCH):
                        chunk_scans(1, j)
                        for z in range(HT):
                            zsl = slice(z * 128, (z + 1) * 128)
                            v.scalar_tensor_tensor(
                                out=numb[:, zsl], in0=dprof[:, j, :],
                                scalar=ain[:, 0, j, z:z + 1],
                                in1=aext[:, z, 0:128],
                                op0=ALU.mult, op1=ALU.add)
                            v.scalar_tensor_tensor(
                                out=denb[:, zsl], in0=dprof[:, j, :],
                                scalar=ain[:, 1, j, z:z + 1],
                                in1=bext[:, z, 0:128],
                                op0=ALU.mult, op1=ALU.add)
                        v.drain()
                        v.tensor_scalar_mul(out=pkf[:], in0=pkf[:],
                                            scalar1=euvec[:, j:j + 1])
                        v.drain()
                        v.tensor_tensor(out=pkv[:], in0=pkf[:],
                                        in1=vb[:, j, :], op=ALU.mult)
                        v.drain()
                        v.tensor_add(out=numb[:], in0=numb[:], in1=pkv[:])
                        v.drain()
                        v.tensor_add(out=denb[:], in0=denb[:], in1=pkf[:])
                        v.drain()
                        c3dn.bump(v)
                        c3iv.wait(v, hf * 8 + j + 1)
                        v.tensor_tensor(out=wkv[:, j, :], in0=numb[:],
                                        in1=pkv[:], op=ALU.mult)
                        v.drain()
                        c3wv.bump(v)

                @blk.gpsimd
                def _(g_):
                    c3f.wait(g_, 8 * (hf + 1))
                    g_.dma_start(out=cins[hf],
                                 in_=afin.ap().rearrange("p k z a -> p (k z a)")
                                 ).then_inc(cd3.h, 16)
                    cd3.note(16)
                    cd3.wait(g_)
                    cc = g_.collective_compute(
                        "AllGather", ALU.bypass,
                        replica_groups=[[0, 1], [2, 3], [4, 5], [6, 7]],
                        ins=[cins[hf]], outs=[couts[hf]])
                    cag3.inc(cc)
                    cag3.wait(g_)
                    g_.dma_start(out=agts.ap().rearrange("p r k z a -> p r (k z a)"),
                                 in_=couts[hf].rearrange("(r p) n -> p r n", p=128)
                                 ).then_inc(cd3b.h, 16)
                    cd3b.note(16)


                for g2 in range(2):
                    gg = hf * 2 + g2
                    gsl = slice(gg * 512, (gg + 1) * 512)
                    hsl = slice(g2 * 512, (g2 + 1) * 512)

                    @blk.sync
                    def _(s):
                        for oc in range(8):
                            qi = gg * 8 + oc
                            if qi >= 2:
                                c4rps.wait(s, qi - 1)
                            s.dma_start(out=wst1[:, :, oc % 2, :],
                                        in_=rw_r[:, :, oc * 128:(oc + 1) * 128]
                                        ).then_inc(c4wr[qi % 2].h, 16)
                            c4wr[qi % 2].note(16)
                        for cc_ in range(8):
                            qi = gg * 8 + cc_
                            if qi >= 2:
                                c4o.wait(s, qi - 1)
                            s.dma_start(out=wst2[:, :, cc_ % 2, :],
                                        in_=ow_r[:, :, cc_ * 128:(cc_ + 1) * 128]
                                        ).then_inc(c4wo[qi % 2].h, 16)
                            c4wo[qi % 2].note(16)

                    @blk.tensor
                    def _(t):
                        c4m.wait(t, gg)
                        c4ev.wait(t, gg * 8)
                        if g2 == 0:
                            c2m.wait(t, (hf + 1) * 2)
                            c3wv.wait(t, 8 * (hf + 1))
                        t.matmul(ps_mr2[:, 0:512], st.onesr_b[0:1, :], rows[0:1, gsl],
                                 start=True, stop=True)
                        t.matmul(ps_mr2[:, 512:1024], st.onesr_b[32:33, :],
                                 rows[32:33, gsl], start=True, stop=True)
                        mm = t.matmul(ps_mr2[:, 1024:1536], st.onesr_b[64:65, :],
                                      rows[64:65, gsl], start=True, stop=True)
                        c4rep.inc(mm)
                        c4m.wait(t, gg + 1)
                        for oc in range(8):
                            qi = gg * 8 + oc
                            c4wr[qi % 2].wait(t, 16 * (qi // 2 + 1))
                            if qi >= 2:
                                c4sg.wait(t, qi - 1)
                            for ic in range(NCH):
                                mm = t.matmul(ps_k[:, oc % 2, :],
                                              wst1[:, ic, oc % 2, :],
                                              xkb[:, ic, :], start=(ic == 0),
                                              stop=(ic == NCH - 1))
                            c4rps.inc(mm)
                        for cc_ in range(8):
                            qi = gg * 8 + cc_
                            c4wo[qi % 2].wait(t, 16 * (qi // 2 + 1))
                            c4rw.wait(t, gg * 8 + 8)
                            if qi >= 2:
                                c4ev.wait(t, qi - 1)
                            for ic in range(NCH):
                                mm = t.matmul(ps_v[:, cc_ % 2, :],
                                              wst2[:, ic, cc_ % 2, :],
                                              wkv[:, ic, hsl], start=(ic == 0),
                                              stop=(ic == NCH - 1))
                            c4o.inc(mm)

                    @blk.vector
                    def _(v):
                        c4rep.wait(v, gg + 1)
                        for k in range(NCH):
                            v.tensor_sub(out=t1b[:], in0=x[:, k, gsl],
                                         in1=ps_mr2[:, 0:512])
                            v.drain()
                            v.tensor_tensor(out=h_b[:, k, :], in0=t1b[:],
                                            in1=ps_mr2[:, 512:1024], op=ALU.mult)
                            v.drain()
                            v.tensor_scalar(out=h_b[:, k, :], in0=h_b[:, k, :],
                                            scalar1=consts["c_ln1w"][:, k:k + 1],
                                            scalar2=consts["c_ln1b"][:, k:k + 1],
                                            op0=ALU.mult, op1=ALU.add)
                            v.drain()
                            v.tensor_sub(out=t1b[:, 1:512], in0=h_b[:, k, 1:512],
                                         in1=h_b[:, k, 0:511])
                            v.drain()
                            for zz in range(4):
                                zt = hf * 8 + g2 * 4 + zz
                                cc0 = zz * 128
                                v.tensor_sub(out=t1b[:, cc0:cc0 + 1],
                                             in0=h_b[:, k, cc0:cc0 + 1],
                                             in1=prevh[:, k, zt:zt + 1])
                            v.drain()
                            v.scalar_tensor_tensor(
                                out=xkb[:, k, 1:512], in0=t1b[:, 1:512],
                                scalar=consts["c_tmr"][:, k:k + 1],
                                in1=h_b[:, k, 0:511], op0=ALU.mult, op1=ALU.add)
                            v.drain()
                            for zz in range(4):
                                zt = hf * 8 + g2 * 4 + zz
                                cc0 = zz * 128
                                v.scalar_tensor_tensor(
                                    out=xkb[:, k, cc0:cc0 + 1],
                                    in0=t1b[:, cc0:cc0 + 1],
                                    scalar=consts["c_tmr"][:, k:k + 1],
                                    in1=prevh[:, k, zt:zt + 1],
                                    op0=ALU.mult, op1=ALU.add)
                            v.drain()
                        c4m.bump(v)
                        for oc in range(8):
                            qi = gg * 8 + oc
                            c4sg.wait(v, qi + 1)
                            v.tensor_tensor(out=wkv[:, oc, hsl],
                                            in0=wkv[:, oc, hsl],
                                            in1=srb[:, oc % 2, :], op=ALU.mult)
                            v.drain()
                            c4rw.bump(v)
                        for cc_ in range(8):
                            qi = gg * 8 + cc_
                            c4o.wait(v, qi + 1)
                            v.tensor_sub(out=t1b[:], in0=x[:, cc_, gsl],
                                         in1=ps_mr2[:, 0:512])
                            v.drain()
                            v.tensor_tensor(out=t1b[:], in0=t1b[:],
                                            in1=ps_mr2[:, 1024:1536], op=ALU.mult)
                            v.drain()
                            v.tensor_tensor(out=xb[:, cc_, gsl], in0=t1b[:],
                                            in1=ps_v[:, cc_ % 2, :], op=ALU.add)
                            v.drain()
                            v.tensor_tensor(out=x[:, cc_, gsl], in0=x[:, cc_, gsl],
                                            in1=xb[:, cc_, gsl], op=ALU.add)
                            v.drain()
                            c4ev.bump(v)

                    @blk.scalar
                    def _(s):
                        for oc in range(8):
                            qi = gg * 8 + oc
                            c4rps.wait(s, qi + 1)
                            if qi >= 2:
                                c4rw.wait(s, qi - 1)
                            s.activation(out=srb[:, oc % 2, :], in_=ps_k[:, oc % 2, :],
                                         func=AF.Sigmoid)
                            s.drain()
                            c4sg.bump(s)

        p5 = contextlib.ExitStack()
        with p5:
            s5 = lambda n, s_, d: p5.enter_context(nc.sbuf_tensor(f"{n}_{tag}", s_, d))
            xqb5 = s5("xqb5", [128, NCH, 512], BF16)
            kk_b = s5("kk_b", [128, 32, 512], BF16)
            sr2b = s5("sr2b", [128, NCH, 512], BF16)
            h2b = s5("h2b", [128, NCH, 512], BF16)
            t1b5 = s5("t1b5", [128, 512], BF16)
            tf5 = s5("tf5", [128, 512], F32)
            xk2b = s5("xk2b", [128, NCH, 512], BF16)
            xr2b = s5("xr2b", [128, NCH, 512], BF16)
            wfr = s5("wfr", [128, NCH, 2, 128], BF16)
            wfk = s5("wfk", [128, NCH, 2, 128], BF16)
            wfv = s5("wfv", [128, 2, 512], BF16)
            ps_mr5 = p5.enter_context(nc.psum_tensor(f"ps5mr_{tag}", [128, 1024], F32))
            p5s = contextlib.ExitStack()
            ps_st5 = p5s.enter_context(nc.psum_tensor(f"ps5st_{tag}", [1, 1024], F32))

            @blk.vector
            def _(v):
                c4ev.wait(v)
                for g in range(NGRP):
                    gsl = slice(g * 512, (g + 1) * 512)
                    c5sm.wait(v, g + 1)
                    v.tensor_scalar_mul(out=rowf[0:1, 0:512], in0=ps_st5[0:1, 0:512],
                                        scalar1=1.0 / C)
                    v.drain()
                    v.tensor_mul(out=rowf[0:1, 1024:1536], in0=rowf[0:1, 0:512],
                                 in1=rowf[0:1, 0:512])
                    v.drain()
                    v.tensor_scalar_mul(out=rowf[0:1, 512:1024],
                                        in0=ps_st5[0:1, 512:1024], scalar1=1.0 / C)
                    v.drain()
                    v.tensor_sub(out=rowf[0:1, 512:1024], in0=rowf[0:1, 512:1024],
                                 in1=rowf[0:1, 1024:1536])
                    v.drain()
                    v.tensor_scalar_add(out=rowf[0:1, 512:1024],
                                        in0=rowf[0:1, 512:1024], scalar1=EPS)
                    v.drain()
                    c5v.bump(v)
                    c5sq.wait(v, g + 1)
                    v.tensor_copy(out=rows[0:1, gsl], in_=rowf[0:1, 0:512])
                    v.drain()
                    v.reciprocal(out=rowf[0:1, 0:512], in_=rowf[0:1, 512:1024])
                    v.drain()
                    v.tensor_copy(out=rows[32:33, gsl], in_=rowf[0:1, 0:512])
                    v.drain()
                    c5r.bump(v)
                    c5rep.wait(v, g + 1)
                    for zz in range(4):
                        zt = g * 4 + zz
                        col = zz * 128 + 127
                        for k in range(NCH):
                            v.tensor_scalar(out=hlast[:, k, zt:zt + 1],
                                            in0=xb[:, k, g * 512 + col:g * 512 + col + 1],
                                            scalar1=ps_mr5[:, col:col + 1],
                                            scalar2=ps_mr5[:, 512 + col:512 + col + 1],
                                            op0=ALU.subtract, op1=ALU.mult)
                        v.drain()
                        for k in range(NCH):
                            v.tensor_scalar(out=hlast[:, k, zt:zt + 1],
                                            in0=hlast[:, k, zt:zt + 1],
                                            scalar1=consts["c_ln2w"][:, k:k + 1],
                                            scalar2=consts["c_ln2b"][:, k:k + 1],
                                            op0=ALU.mult, op1=ALU.add)
                        v.drain()
                    c5h.bump(v)

            @blk.scalar
            def _(s):
                c4ev.wait(s)
                for g in range(NGRP):
                    gsl = slice(g * 512, (g + 1) * 512)
                    if g > 0:
                        c5sm.wait(s, g)
                    for k in range(NCH):
                        s.activation(out=xqb5[:, k, :], in_=xb[:, k, gsl],
                                     func=AF.Square)
                    s.drain()
                    c5q.bump(s)
                    c5v.wait(s, g + 1)
                    s.activation(out=rowf[0:1, 512:1024], in_=rowf[0:1, 512:1024],
                                 func=AF.Sqrt)
                    s.drain()
                    c5sq.bump(s)

            @blk.tensor
            def _(t):
                c4ev.wait(t)
                for g in range(NGRP):
                    gsl = slice(g * 512, (g + 1) * 512)
                    c5q.wait(t, g + 1)
                    c5r.wait(t, g)
                    for k in range(NCH):
                        t.matmul(ps_st5[0:1, 0:512], st.onesc_b[:], xb[:, k, gsl],
                                 start=(k == 0), stop=(k == NCH - 1))
                    for k in range(NCH):
                        last = k == NCH - 1
                        mm = t.matmul(ps_st5[0:1, 512:1024], st.onesc_b[:],
                                      xqb5[:, k, :], start=(k == 0), stop=last)
                        if last:
                            c5sm.inc(mm)
                    c5r.wait(t, g + 1)
                    c5h.wait(t, g)
                    t.matmul(ps_mr5[:, 0:512], st.onesr_b[0:1, :], rows[0:1, gsl],
                             start=True, stop=True)
                    mm = t.matmul(ps_mr5[:, 512:1024], st.onesr_b[32:33, :],
                                  rows[32:33, gsl], start=True, stop=True)
                    c5rep.inc(mm)

            @blk.gpsimd
            def _(g_):
                c5h.wait(g_, NGRP)
                g_.dma_start(out=cin2, in_=hlast.ap().rearrange("p k z -> p (k z)")
                             ).then_inc(cd5.h, 16)
                cd5.note(16)
                cd5.wait(g_)
                cc = g_.collective_compute(
                    "AllGather", ALU.bypass,
                    replica_groups=[[0, 1], [2, 3], [4, 5], [6, 7]],
                    ins=[cin2], outs=[cout2])
                cag5.inc(cc)
                cag5.wait(g_)
                g_.dma_start(out=agt1.ap().rearrange("p r k z -> p r (k z)"),
                             in_=cout2.rearrange("(r p) n -> p r n", p=128)
                             ).then_inc(cd5b.h, 16)
                cd5b.note(16)


            @blk.vector
            def _(v):
                cd5b.wait(v)
                for k in range(NCH):
                    v.memset(prevblk[:, 0, k, 0:1], 0.0)
                v.drain()
                v.tensor_copy(out=prevblk[:, 0, :, 1:16], in_=agt1[:, 1, :, 0:15])
                v.drain()
                v.tensor_copy(out=prevblk[:, 1, :, :], in_=agt1[:, 0, :, :])
                v.drain()
                v.tensor_scalar_mul(
                    out=prevh.ap().rearrange("p k z -> p (k z)"),
                    in0=prevblk.ap().rearrange("p q k z -> p (q k z)")[:, 0:NCH * 16],
                    scalar1=st.nparv[:, 0:1])
                v.drain()
                v.scalar_tensor_tensor(
                    out=prevh.ap().rearrange("p k z -> p (k z)"),
                    in0=prevblk.ap().rearrange("p q k z -> p (q k z)")
                    [:, NCH * 16:2 * NCH * 16],
                    scalar=st.parv[:, 0:1],
                    in1=prevh.ap().rearrange("p k z -> p (k z)"),
                    op0=ALU.mult, op1=ALU.add)
                v.drain()
                c55v.bump(v)
                cdp5.bump(v, 16)

            p5s.close()
            for g in range(NGRP):
                gsl = slice(g * 512, (g + 1) * 512)
                pg = contextlib.ExitStack()
                ps_r5 = pg.enter_context(
                    nc.psum_tensor(f"ps5r_{tag}{g}", [128, 2, 512], F32))
                ps_kk = pg.enter_context(
                    nc.psum_tensor(f"ps5kk_{tag}{g}", [128, 2, 512], F32))

                @blk.sync
                def _(s):
                    if g == 0:
                        c4ev.wait(s, 32)
                    for oc in range(8):
                        qi = g * 8 + oc
                        if qi >= 2:
                            c5rps.wait(s, qi - 1)
                        s.dma_start(out=wfr[:, :, oc % 2, :],
                                    in_=fr_r[:, :, oc * 128:(oc + 1) * 128]
                                    ).then_inc(c5wr[qi % 2].h, 16)
                        c5wr[qi % 2].note(16)
                    for f in range(32):
                        qi = g * 32 + f
                        if qi >= 2:
                            c5kps.wait(s, qi - 1)
                        s.dma_start(out=wfk[:, :, f % 2, :],
                                    in_=fk_r[:, :, f * 128:(f + 1) * 128]
                                    ).then_inc(c5wk[qi % 2].h, 16)
                        c5wk[qi % 2].note(16)

                @blk.tensor
                def _(t):
                    c5r.wait(t, NGRP)
                    c5h.wait(t, NGRP)
                    c5m.wait(t, g)
                    if g == 0:
                        c5ev.wait(t, 0)
                    else:
                        c5ev.wait(t, (g - 1) * 8 + 8)
                    t.matmul(ps_mr5[:, 0:512], st.onesr_b[0:1, :], rows[0:1, gsl],
                             start=True, stop=True)
                    mm = t.matmul(ps_mr5[:, 512:1024], st.onesr_b[32:33, :],
                                  rows[32:33, gsl], start=True, stop=True)
                    c5rep.inc(mm)
                    c5m.wait(t, g + 1)
                    for oc in range(8):
                        qi = g * 8 + oc
                        c5wr[qi % 2].wait(t, 16 * (qi // 2 + 1))
                        if qi >= 2:
                            c5sg.wait(t, qi - 1)
                        for ic in range(NCH):
                            mm = t.matmul(ps_r5[:, oc % 2, :],
                                          wfr[:, ic, oc % 2, :],
                                          xr2b[:, ic, :], start=(ic == 0),
                                          stop=(ic == NCH - 1))
                        c5rps.inc(mm)
                    for f in range(32):
                        qi = g * 32 + f
                        c5wk[qi % 2].wait(t, 16 * (qi // 2 + 1))
                        if qi >= 2:
                            c5kr.wait(t, qi - 1)
                        for ic in range(NCH):
                            mm = t.matmul(ps_kk[:, f % 2, :],
                                          wfk[:, ic, f % 2, :],
                                          xk2b[:, ic, :], start=(ic == 0),
                                          stop=(ic == NCH - 1))
                        c5kps.inc(mm)

                @blk.vector
                def _(v):
                    c5rep.wait(v, NGRP + g + 1)
                    cdp5.wait(v)
                    for k in range(NCH):
                        v.tensor_tensor(out=tf5[:], in0=xb[:, k, gsl],
                                        in1=ps_mr5[:, 0:512], op=ALU.subtract)
                        v.drain()
                        v.tensor_tensor(out=h2b[:, k, :], in0=tf5[:],
                                        in1=ps_mr5[:, 512:1024], op=ALU.mult)
                        v.drain()
                        v.tensor_scalar(out=h2b[:, k, :], in0=h2b[:, k, :],
                                        scalar1=consts["c_ln2w"][:, k:k + 1],
                                        scalar2=consts["c_ln2b"][:, k:k + 1],
                                        op0=ALU.mult, op1=ALU.add)
                        v.drain()
                        v.tensor_sub(out=t1b5[:, 1:512], in0=h2b[:, k, 1:512],
                                     in1=h2b[:, k, 0:511])
                        v.drain()
                        for zz in range(4):
                            zt = g * 4 + zz
                            cc0 = zz * 128
                            v.tensor_sub(out=t1b5[:, cc0:cc0 + 1],
                                         in0=h2b[:, k, cc0:cc0 + 1],
                                         in1=prevh[:, k, zt:zt + 1])
                        v.drain()
                        for dst, tm in ((xk2b, "c_ftmk"), (xr2b, "c_ftmr")):
                            v.scalar_tensor_tensor(
                                out=dst[:, k, 1:512], in0=t1b5[:, 1:512],
                                scalar=consts[tm][:, k:k + 1],
                                in1=h2b[:, k, 0:511], op0=ALU.mult, op1=ALU.add)
                            v.drain()
                            for zz in range(4):
                                zt = g * 4 + zz
                                cc0 = zz * 128
                                v.scalar_tensor_tensor(
                                    out=dst[:, k, cc0:cc0 + 1],
                                    in0=t1b5[:, cc0:cc0 + 1],
                                    scalar=consts[tm][:, k:k + 1],
                                    in1=prevh[:, k, zt:zt + 1],
                                    op0=ALU.mult, op1=ALU.add)
                            v.drain()
                    c5m.bump(v)
                    for f in range(32):
                        qi = g * 32 + f
                        c5kps.wait(v, qi + 1)
                        if qi > 0:
                            c5kk.wait(v, qi)
                        v.tensor_scalar_max(out=tf5[:], in0=ps_kk[:, f % 2, :],
                                            scalar1=0.0)
                        v.drain()
                        c5kr.bump(v)

                @blk.scalar
                def _(s):
                    for oc in range(8):
                        qi = g * 8 + oc
                        c5rps.wait(s, qi + 1)
                        s.activation(out=sr2b[:, oc, :], in_=ps_r5[:, oc % 2, :],
                                     func=AF.Sigmoid)
                        s.drain()
                        c5sg.bump(s)
                    for f in range(32):
                        qi = g * 32 + f
                        c5kr.wait(s, qi + 1)
                        s.activation(out=kk_b[:, f, :], in_=tf5[:], func=AF.Square)
                        s.drain()
                        c5kk.bump(s)

                pg.close()
                pf = contextlib.ExitStack()
                with pf:
                    ps_fv = pf.enter_context(
                        nc.psum_tensor(f"ps5fv_{tag}{g}", [128, 4, 512], F32))

                    @blk.sync
                    def _(s):
                        for pas in range(2):
                            for f in range(32):
                                qi = (g * 2 + pas) * 32 + f
                                if qi >= 2:
                                    c5fv.wait(s, qi - 1)
                                s.dma_start(
                                    out=wfv[:, f % 2, :],
                                    in_=fv_r[:, f, pas * 512:(pas + 1) * 512]
                                    ).then_inc(c5wv[qi % 2].h, 16)
                                c5wv[qi % 2].note(16)

                    @blk.tensor
                    def _(t):
                        c5kk.wait(t, (g + 1) * 32)
                        for pas in range(2):
                            if g > 0 or pas > 0:
                                c5ev.wait(t, g * 8 + pas * 4)
                            for f in range(32):
                                qi = (g * 2 + pas) * 32 + f
                                c5wv[qi % 2].wait(t, 16 * (qi // 2 + 1))
                                for oc4 in range(4):
                                    last = f == 31
                                    mm = t.matmul(
                                        ps_fv[:, oc4, :],
                                        wfv[:, f % 2, oc4 * 128:(oc4 + 1) * 128],
                                        kk_b[:, f, :], start=(f == 0), stop=last)
                                c5fv.inc(mm)

                    @blk.vector
                    def _(v):
                        for pas in range(2):
                            c5fv.wait(v, (g * 2 + pas) * 32 + 32)
                            for oc4 in range(4):
                                cc_ = pas * 4 + oc4
                                c5sg.wait(v, g * 8 + 8)
                                v.tensor_tensor(out=tf5[:], in0=sr2b[:, cc_, :],
                                                in1=ps_fv[:, oc4, :], op=ALU.mult)
                                v.drain()
                                v.tensor_tensor(out=x[:, cc_, gsl],
                                                in0=x[:, cc_, gsl], in1=tf5[:],
                                                op=ALU.add)
                                v.drain()
                                c5ev.bump(v)

    st.x_ready = (c5ev.h, c5ev.c.n)


def emit_l1ffn(st, tag):
    nc, blk, x = st.nc, st.blk, st.x
    dr, wb, consts = st.dr, st.wb, st.consts
    CCAP = float(int(1.25 * (B * T) / 8))
    xr_h, xr_n = st.x_ready
    wcB = st.cWC1.n

    st.stage_reset()
    Ce, Ch, Cg = st.Ce, st.Ch, st.Cg
    fA_x, fA_q, fA_sm, fA_v, fA_sq, fA_r, fA_rep, fA_y = (
        Ce(), Ce(), Ce(), Ce(), Ce(), Ce(), Ce(), Ce())
    fLG, fVm, fEX, fRT, fCNT = Ce(), Ce(), Ce(), Ce(), Ce()
    fVcnt, fAG, fVag, fBase, fThr = Ce(), Ce(), Ce(), Ce(), Ce()
    fCum, fGb, fTp, fGT = Ce(), Ce(), Ce(), Ce()
    fDcnt, fDag, fSel = Cg(), Cg(), Cg()
    fLB, fB2w, fWgt = Ch(), Ch(), Ch()
    fB2, fAcc, fUp, fRelu, fGrpP, fGrpV, fDn, fEvac, fXup = (
        Ce(), Ce(), Ce(), Ce(), Ce(), Ce(), Ce(), Ce(), Ce())
    fW1 = [Ch(), Ch()]
    fW2 = [Ch(), Ch()]

    cin_f = st.dint(f"cinf_{tag}", [1, 128])
    cout_f = st.dint(f"coutf_{tag}", [8, 128])

    loc = contextlib.ExitStack()
    sb = lambda n, s, d: loc.enter_context(nc.sbuf_tensor(f"{n}_{tag}", s, d))
    with loc:
        yn = sb("yn", [128, NCH, TPC], BF16)
        gT_b = sb("gT", [8, TPC], BF16)
        h_b = sb("hb1", [128, 32, 512], BF16)
        acc = sb("acc1", [128, NCH, 512], F32)
        grep_s = sb("grep", [128, 512], F32)
        sel16_b = sb("sel16f", [16, 16 * 128], BF16)
        wgt = sb("wgt1", [128, NCH, 8], BF16)
        b1_s = sb("b1s", [128, 8, 32], F32)
        b2w = sb("b2w", [8, C], BF16)
        w1s = sb("w1s", [128, NCH, 2, 128], BF16)
        w2s = sb("w2s", [128, 2, C], BF16)
        wte_all = sb("wte", [128, NTILE, 8], F32)
        mask_b = sb("mskb", [128, NTILE, 8], BF16)
        agcnt_s = sb("agc", [128, 8], F32)
        agcnt_b = sb("agcb", [128, 8], BF16)
        thr_s = sb("thr", [16, 8], F32)
        thr_b = sb("thrb", [16, 8], BF16)
        cntrow = sb("cntr", [1, 128], F32)
        lgf = sb("lgf", [128, 2, 8], F32)
        expd = sb("expd", [128, 2, 8], F32)
        nmax = sb("nmax", [128, 2, 1], F32)
        prb = sb("prb", [128, 8], F32)
        t8f = sb("t8f", [128, 8], F32)
        sel_f = sb("self", [128, 8], F32)
        wsm = sb("wsm", [128, 1], F32)
        g_b = sb("gb1", [128, 2, 8], BF16)
        tfv = sb("tfv1", [128, 512], F32)

        w1_r = wb["w_l1f_w1"].rearrange("e (k p) h -> e p k h", p=128)
        w2_r = wb["w_l1f_w2"].rearrange("e (h p) c -> e p h c", p=128)

        @blk.sync
        def _(s):
            s.wait_ge(xr_h, xr_n)
            s.wait_ge(st.cWC1.h, wcB)
            s.dma_start(out=wgt[:], in_=wb["w_l1f_gate"]
                        .rearrange("(k p) e -> p k e", p=128)).then_inc(fWgt.h, 16)
            fWgt.note(16)
            s.dma_start(out=b1_s[:], in_=dr["c_l1b1"]
                        .rearrange("e p h -> p e h")).then_inc(fLB.h, 16)
            fLB.note(16)
            s.dma_start(out=b2w[:], in_=wb["r_l1b2"]).then_inc(fB2w.h, 16)
            fB2w.note(16)

        @blk.gpsimd
        def _(g_):
            g_.wait_ge(xr_h, xr_n)
            g_.dma_start(out=sel16_b[:], in_=dr["k_sel16"]).then_inc(fSel.h, 16)
            fSel.note(16)

        pa = contextlib.ExitStack()
        with pa:
            sba = lambda n, s, d: pa.enter_context(
                nc.sbuf_tensor(f"{n}_{tag}", s, d))
            mrow = sba("mrowf", [1, 512], F32)
            trow = sba("trowf", [1, 1024], F32)
            r0row = sba("r0rowf", [1, 512], F32)
            mrow_b = sba("mrowfb", [1, 512], BF16)
            r0row_b = sba("r0rowfb", [1, 512], BF16)
            xq = sba("xqf", [128, NCH, 512], BF16)
            ps_st = pa.enter_context(nc.psum_tensor(f"fpst_{tag}", [1, 1024], F32))
            ps_mr = pa.enter_context(nc.psum_tensor(f"fpmr_{tag}", [128, 1024], F32))

            @blk.vector
            def _(v):
                v.wait_ge(xr_h, xr_n)
                for g in range(NGRP):
                    gsl = slice(g * 512, (g + 1) * 512)
                    xsb = h_b[:, 0:NCH, :]
                    fA_sm.wait(v, g)
                    for k in range(NCH):
                        v.tensor_copy(out=xsb[:, k, :], in_=x[:, k, gsl])
                    v.drain()
                    fA_x.bump(v)
                    fA_sm.wait(v, g + 1)
                    v.tensor_scalar_mul(out=mrow[:], in0=ps_st[0:1, 0:512],
                                        scalar1=1.0 / C)
                    v.tensor_scalar_mul(out=trow[0:1, 0:512],
                                        in0=ps_st[0:1, 512:1024], scalar1=1.0 / C)
                    v.drain()
                    v.tensor_mul(out=trow[0:1, 512:1024], in0=mrow[:], in1=mrow[:])
                    v.drain()
                    v.tensor_sub(out=trow[0:1, 0:512], in0=trow[0:1, 0:512],
                                 in1=trow[0:1, 512:1024])
                    v.drain()
                    v.tensor_scalar_add(out=trow[0:1, 512:1024],
                                        in0=trow[0:1, 0:512], scalar1=EPS)
                    v.drain()
                    fA_v.bump(v)
                    fA_sq.wait(v, g + 1)
                    v.reciprocal(out=r0row[:], in_=trow[0:1, 512:1024])
                    v.drain()
                    v.tensor_copy(out=mrow_b[:], in_=mrow[:])
                    v.tensor_copy(out=r0row_b[:], in_=r0row[:])
                    v.drain()
                    fA_r.bump(v)
                    fA_rep.wait(v, g + 1)
                    for k in range(NCH):
                        v.tensor_sub(out=tfv[:], in0=x[:, k, gsl],
                                     in1=ps_mr[:, 0:512])
                        v.drain()
                        v.tensor_tensor(out=yn[:, k, gsl], in0=tfv[:],
                                        in1=ps_mr[:, 512:1024], op=ALU.mult)
                        v.drain()
                    fA_y.bump(v)

            @blk.scalar
            def _(s):
                for g in range(NGRP):
                    fA_x.wait(s, g + 1)
                    xsb = h_b[:, 0:NCH, :]
                    fA_sm.wait(s, g)
                    for k in range(NCH):
                        s.activation(out=xq[:, k, :], in_=xsb[:, k, :],
                                     func=AF.Square)
                    s.drain()
                    fA_q.bump(s)
                    fA_v.wait(s, g + 1)
                    s.activation(out=trow[0:1, 512:1024], in_=trow[0:1, 512:1024],
                                 func=AF.Sqrt)
                    s.drain()
                    fA_sq.bump(s)

            @blk.tensor
            def _(t):
                t.wait_ge(st.cKC.h, st.cKC.n)
                for g in range(NGRP):
                    gsl = slice(g * 512, (g + 1) * 512)
                    xsb = h_b[:, 0:NCH, :]
                    fA_x.wait(t, g + 1)
                    fA_q.wait(t, g + 1)
                    fA_r.wait(t, g)
                    for k in range(NCH):
                        t.matmul(ps_st[0:1, 0:512], st.onesc_b[:], xsb[:, k, :],
                                 start=(k == 0), stop=(k == NCH - 1))
                    for k in range(NCH):
                        last = (k == NCH - 1)
                        mm = t.matmul(ps_st[0:1, 512:1024], st.onesc_b[:],
                                      xq[:, k, :], start=(k == 0), stop=last)
                        if last:
                            fA_sm.inc(mm)
                    fA_r.wait(t, g + 1)
                    fA_y.wait(t, g)
                    t.matmul(ps_mr[:, 0:512], st.onesr_b[0:1, :], mrow_b[:],
                             start=True, stop=True)
                    mm = t.matmul(ps_mr[:, 512:1024], st.onesr_b[0:1, :],
                                  r0row_b[:], start=True, stop=True)
                    fA_rep.inc(mm)

        pr = contextlib.ExitStack()
        with pr:
            ps_lg = pr.enter_context(nc.psum_tensor(f"fplg_{tag}", [128, 2, 8], F32))
            ps_cnt = pr.enter_context(nc.psum_tensor(f"fpcnt_{tag}", [1, 128], F32))
            ps_cum = pr.enter_context(nc.psum_tensor(f"fpcum_{tag}", [128, 2, 8], F32))
            ps_thr = pr.enter_context(nc.psum_tensor(f"fpthr_{tag}", [128, 2, 8], F32))
            ps_base = pr.enter_context(nc.psum_tensor(f"fpbase_{tag}", [16, 8], F32))
            psT_g = pr.enter_context(nc.psum_tensor(f"fptg_{tag}", [8, 2, 128], BF16))

            @blk.tensor
            def _(t):
                fWgt.wait(t)
                fA_y.wait(t, NGRP)
                for z in range(NTILE):
                    bsl = slice(z * 128, (z + 1) * 128)
                    if z >= 2:
                        fVm.wait(t, z - 1)
                    for k in range(NCH):
                        mm = t.matmul(ps_lg[:, z % 2, :], yn[:, k, bsl],
                                      wgt[:, k, :], start=(k == 0),
                                      stop=(k == NCH - 1))
                    fLG.inc(mm)
                    fRT.wait(t, z + 1)
                    mm = t.matmul(ps_cnt[0:1, z * 8:(z + 1) * 8], st.onesc_b[:],
                                  mask_b[:, z, :], start=True, stop=True)
                    fCNT.inc(mm)
                fVag.wait(t, 1)
                mm = t.matmul(ps_base[:], st.pfx_b[:], agcnt_b[:],
                              start=True, stop=True)
                fBase.inc(mm)
                fThr.wait(t, 1)
                fSel.wait(t)
                for z in range(NTILE):
                    if z >= 2:
                        fGb.wait(t, z - 1)
                    t.matmul(ps_cum[:, z % 2, :], st.tri_b[:], mask_b[:, z, :],
                             start=True, stop=True)
                    mm = t.matmul(ps_thr[:, z % 2, :],
                                  sel16_b[0:16, z * 128:(z + 1) * 128],
                                  thr_b[:], start=True, stop=True)
                    fCum.inc(mm)
                    fGb.wait(t, z + 1)
                    if z >= 2:
                        fGT.wait(t, z - 1)
                    tp = t.transpose(psT_g[0:8, z % 2, :], g_b[:, z % 2, :],
                                     st.ident_b[:])
                    fTp.inc(tp)

            @blk.vector
            def _(v):
                for z in range(NTILE):
                    fLG.wait(v, z + 1)
                    v.tensor_tensor(out=lgf[:, z % 2, :], in0=ps_lg[:, z % 2, :],
                                    in1=consts["k_gb8"][:], op=ALU.add)
                    v.drain()
                    v.max(out=t8f[:], in_=lgf[:, z % 2, :])
                    v.drain()
                    v.tensor_scalar_mul(out=nmax[:, z % 2, :], in0=t8f[:, 0:1],
                                        scalar1=-1.0)
                    v.drain()
                    fVm.bump(v)
                    fEX.wait(v, z + 1)
                    v.reduce_sum(out=wsm[:], in_=expd[:, z % 2, :], axis=AXT.X)
                    v.drain()
                    v.reciprocal(out=wsm[:], in_=wsm[:])
                    v.drain()
                    v.tensor_scalar(out=prb[:], in0=expd[:, z % 2, :],
                                    scalar1=wsm[:, 0:1], scalar2=None,
                                    op0=ALU.mult)
                    v.drain()
                    v.max(out=t8f[:], in_=prb[:])
                    v.drain()
                    v.tensor_scalar(out=sel_f[:], in0=prb[:],
                                    scalar1=t8f[:, 1:2], scalar2=None,
                                    op0=ALU.is_ge)
                    v.drain()
                    v.tensor_add(out=wsm[:], in0=t8f[:, 0:1], in1=t8f[:, 1:2])
                    v.drain()
                    v.reciprocal(out=wsm[:], in_=wsm[:])
                    v.drain()
                    v.tensor_scalar(out=prb[:], in0=prb[:], scalar1=wsm[:, 0:1],
                                    scalar2=None, op0=ALU.mult)
                    v.drain()
                    v.tensor_tensor(out=wte_all[:, z, :], in0=prb[:],
                                    in1=sel_f[:], op=ALU.mult)
                    v.drain()
                    v.tensor_copy(out=mask_b[:, z, :], in_=sel_f[:])
                    v.drain()
                    fRT.bump(v)
                fCNT.wait(v, NTILE)
                v.tensor_copy(out=cntrow[:], in_=ps_cnt[0:1, :])
                v.drain()
                fVcnt.bump(v)
                fDag.wait(v, 16)
                v.tensor_copy(out=agcnt_b[:], in_=agcnt_s[:])
                v.drain()
                fVag.bump(v)
                fBase.wait(v, 1)
                v.tensor_scalar(out=thr_s[:], in0=ps_base[:], scalar1=-1.0,
                                scalar2=CCAP, op0=ALU.mult, op1=ALU.add)
                v.drain()
                v.tensor_scalar(out=thr_s[:], in0=thr_s[:], scalar1=129.0,
                                scalar2=-1.0, op0=ALU.min, op1=ALU.max)
                v.drain()
                v.tensor_copy(out=thr_b[:], in_=thr_s[:])
                v.drain()
                fThr.bump(v)
                for z in range(NTILE):
                    fCum.wait(v, z + 1)
                    v.tensor_copy(out=t8f[:], in_=ps_cum[:, z % 2, :])
                    v.drain()
                    v.tensor_tensor(out=prb[:], in0=ps_thr[:, z % 2, :],
                                    in1=t8f[:], op=ALU.is_ge)
                    v.drain()
                    v.tensor_tensor(out=sel_f[:], in0=prb[:],
                                    in1=wte_all[:, z, :], op=ALU.mult)
                    v.drain()
                    v.tensor_copy(out=g_b[:, z % 2, :], in_=sel_f[:])
                    v.drain()
                    fGb.bump(v)
                    fTp.wait(v, z + 1)
                    v.tensor_copy(out=gT_b[0:8, z * 128:(z + 1) * 128],
                                  in_=psT_g[0:8, z % 2, :])
                    v.drain()
                    fGT.bump(v)

            @blk.scalar
            def _(s):
                for z in range(NTILE):
                    fVm.wait(s, z + 1)
                    s.activation(out=expd[:, z % 2, :], in_=lgf[:, z % 2, :],
                                 func=AF.Exp, bias=nmax[:, z % 2, 0:1])
                    s.drain()
                    fEX.bump(s)

            @blk.gpsimd
            def _(g_):
                fVcnt.wait(g_, 1)
                g_.dma_start(out=cin_f, in_=cntrow[:]).then_inc(fDcnt.h, 16)
                fDcnt.note(16)
                fDcnt.wait(g_)
                cc = g_.collective_compute(
                    "AllGather", ALU.bypass,
                    replica_groups=[[0, 1, 2, 3, 4, 5, 6, 7]],
                    ins=[cin_f], outs=[cout_f])
                fAG.inc(cc)
                fAG.wait(g_)
                g_.dma_start(out=agcnt_s[:],
                             in_=cout_f.rearrange("r (z e) -> (r z) e", e=8)
                             ).then_inc(fDag.h, 16)
                fDag.note(16)

        ps_g = loc.enter_context(nc.psum_tensor(f"fpg_{tag}", [128, 8, 512], F32))

        @blk.sync
        def _(s):
            for g in range(NGRP):
                for e in range(8):
                    q = g * 8 + e
                    for hb in range(32):
                        qh = q * 32 + hb
                        if qh >= 2:
                            fUp.wait(s, qh - 1)
                        s.dma_start(out=w1s[:, :, qh % 2, :],
                                    in_=w1_r[e][:, :, hb * 128:(hb + 1) * 128]
                                    ).then_inc(fW1[qh % 2].h, 16)
                        fW1[qh % 2].note(16)

        @blk.tensor
        def _(t):
            fGT.wait(t, NTILE)
            fB2w.wait(t)
            for g in range(NGRP):
                gsl = slice(g * 512, (g + 1) * 512)
                fGrpV.wait(t, g * 8)
                fEvac.wait(t, g * 64)
                for ck in range(NCH):
                    mm = t.matmul(ps_g[:, ck, :],
                                  b2w[0:8, ck * 128:(ck + 1) * 128],
                                  gT_b[0:8, gsl], start=True, stop=True)
                    fB2.inc(mm)
                for e in range(8):
                    q = g * 8 + e
                    if e == 0:
                        fAcc.wait(t, g * 8 + 2)
                    else:
                        fEvac.wait(t, (q - 1) * 8 + 2)
                    for hb in range(32):
                        qh = q * 32 + hb
                        fW1[qh % 2].wait(t, 16 * (qh // 2 + 1))
                        if qh >= 2:
                            fRelu.wait(t, qh - 1)
                        for k in range(NCH):
                            mm = t.matmul(ps_g[:, qh % 2, :],
                                          w1s[:, k, qh % 2, :], yn[:, k, gsl],
                                          start=(k == 0), stop=(k == NCH - 1))
                        fUp.inc(mm)
                    if e == 0:
                        fAcc.wait(t, g * 8 + 8)
                    else:
                        fEvac.wait(t, q * 8)
                    mm = t.matmul(ps_g[:, 7, :],
                                  sel16_b[0:8, e * 128:(e + 1) * 128],
                                  gT_b[0:8, gsl], start=True, stop=True)
                    fGrpP.inc(mm)
                    fRelu.wait(t, (q + 1) * 32)
                    fGrpV.wait(t, q + 1)
                    for hb in range(32):
                        qh = q * 32 + hb
                        fW2[qh % 2].wait(t, 16 * (qh // 2 + 1))
                        for ck in range(NCH):
                            mm = t.matmul(
                                ps_g[:, ck, :],
                                w2s[:, qh % 2, ck * 128:(ck + 1) * 128],
                                h_b[:, hb, :], start=(hb == 0), stop=(hb == 31))
                        fDn.inc(mm)

        @blk.scalar
        def _(s):
            fLB.wait(s)
            for g in range(NGRP):
                for e in range(8):
                    q = g * 8 + e
                    for hb in range(32):
                        qh = q * 32 + hb
                        fUp.wait(s, qh + 1)
                        if q > 0:
                            fDn.wait(s, (q - 1) * 32 + hb + 1)
                        s.activation(out=h_b[:, hb, :], in_=ps_g[:, qh % 2, :],
                                     func=AF.Relu, bias=b1_s[:, e, hb:hb + 1])
                        s.drain()
                        fRelu.bump(s)
                    for hb in range(32):
                        qh = q * 32 + hb
                        if qh >= 2:
                            fDn.wait(s, qh - 1)
                        s.dma_start(out=w2s[:, qh % 2, :],
                                    in_=w2_r[e][:, hb, :]
                                    ).then_inc(fW2[qh % 2].h, 16)
                        fW2[qh % 2].note(16)

        @blk.vector
        def _(v):
            for g in range(NGRP):
                gsl = slice(g * 512, (g + 1) * 512)
                for ck in range(NCH):
                    fB2.wait(v, g * 8 + ck + 1)
                    v.tensor_copy(out=acc[:, ck, :], in_=ps_g[:, ck, :])
                    v.drain()
                    fAcc.bump(v)
                for e in range(8):
                    q = g * 8 + e
                    fGrpP.wait(v, q + 1)
                    v.tensor_copy(out=grep_s[:], in_=ps_g[:, 7, :])
                    v.drain()
                    fGrpV.bump(v)
                    fDn.wait(v, (q + 1) * 32)
                    for ck in range(NCH):
                        v.tensor_tensor(out=tfv[:], in0=grep_s[:],
                                        in1=ps_g[:, ck, :], op=ALU.mult)
                        v.drain()
                        v.tensor_tensor(out=acc[:, ck, :], in0=acc[:, ck, :],
                                        in1=tfv[:], op=ALU.add)
                        v.drain()
                        fEvac.bump(v)
                for ck in range(NCH):
                    v.scalar_tensor_tensor(out=x[:, ck, gsl], in0=acc[:, ck, :],
                                           scalar=0.5, in1=x[:, ck, gsl],
                                           op0=ALU.mult, op1=ALU.add)
                    v.drain()
                    fXup.bump(v)

    st.x_ready = (fXup.h, fXup.c.n)


def emit_l1attn(st, tag):
    nc, blk, x = st.nc, st.blk, st.x
    dr, wb, consts = st.dr, st.wb, st.consts
    par_known = None
    xr_h, xr_n = st.x_ready
    wcB = st.cWC1.n

    st.stage_reset()
    Ce, Ch, Cg = st.Ce, st.Ch, st.Cg
    aA_x, aA_q, aA_sm, aA_v, aA_sq, aA_r, aA_rep, aA_y = (
        Ce(), Ce(), Ce(), Ce(), Ce(), Ce(), Ce(), Ce())
    aWL, aBV = Ch(), Cg()
    aQKP, aQKE, aVP, aVE = Ce(), Ce(), Ce(), Ce()
    aKDs = [Ch(), Ch()]
    aVDs = [Ch(), Ch()]
    cagK, cagV = Ce(), Ce()
    aSC, aSE, aMX, aEXP, aRS = Ce(), Ce(), Ce(), Ce(), Ce()
    aPT, aPTE, aPV, aYB, aYT, aXE = Ce(), Ce(), Ce(), Ce(), Ce(), Ce()
    aKS = [Ch(), Ch()]
    aVS = [Ch(), Ch()]

    cin_k = st.dint(f"cink_{tag}", [1024, TPC], BF16)
    cout_k = st.dint(f"coutk_{tag}", [2048, TPC], BF16)
    cin_v = st.dint(f"cinv_{tag}", [TPC, 1024], BF16)
    cout_v = st.dint(f"coutv_{tag}", [2 * TPC, 1024], BF16)
    cinkr = cin_k.rearrange("(k p) t -> p k t", p=128)
    cinvr = cin_v.rearrange("(z p) c -> p z c", p=128)

    loc = contextlib.ExitStack()
    sb = lambda n, s, d: loc.enter_context(nc.sbuf_tensor(f"{n}_{tag}", s, d))
    with loc:
        qfm = sb("qfm", [128, NCH, TPC], BF16)
        bvrow = sb("bvrow", [1, 1024], BF16)

        pp = contextlib.ExitStack()
        with pp:
            sbp = lambda n, s, d: pp.enter_context(
                nc.sbuf_tensor(f"{n}_{tag}", s, d))
            xn = sbp("xn", [128, NCH, TPC], BF16)
            wqk = sbp("wqk", [128, NCH, 2048], BF16)
            wv = sbp("wv", [128, NCH, 1024], BF16)
            kq_s = sbp("kqs", [128, 2, 512], BF16)
            vt_s = sbp("vts", [128, 2, 512], BF16)
            xq = sbp("xqa", [128, NCH, 512], BF16)
            mrow = sbp("mrowa", [1, 512], F32)
            trow = sbp("trowa", [1, 1024], F32)
            r0row = sbp("r0rowa", [1, 512], F32)
            mrow_b = sbp("mrowab", [1, 512], BF16)
            r0row_b = sbp("r0rowab", [1, 512], BF16)
            ps_st = pp.enter_context(nc.psum_tensor(f"apst_{tag}", [1, 1024], F32))
            ps_mr = pp.enter_context(nc.psum_tensor(f"apmr_{tag}", [128, 1024], F32))
            ps_p = pp.enter_context(nc.psum_tensor(f"app_{tag}", [128, 2, 512], F32))
            ps_v = pp.enter_context(nc.psum_tensor(f"apv_{tag}", [128, 2, 512], F32))

            qkv_r = wb["w_l1a_qkv"].rearrange("(k p) f -> p k f", p=128)

            @blk.sync
            def _(s):
                s.wait_ge(xr_h, xr_n)
                s.wait_ge(st.cWC1.h, wcB)
                s.dma_start(out=wqk[:], in_=qkv_r[:, :, 0:2048]
                            ).then_inc(aWL.h, 16)
                aWL.note(16)
                s.dma_start(out=wv[:], in_=qkv_r[:, :, 2048:3072]
                            ).then_inc(aWL.h, 16)
                aWL.note(16)
                for kk in range(32):
                    fk, g = kk // 4, kk % 4
                    gsl = slice(g * 512, (g + 1) * 512)
                    aQKE.wait(s, 32 + kk + 1)
                    s.dma_start(out=cinkr[:, fk, gsl], in_=kq_s[:, kk % 2, :]
                                ).then_inc(aKDs[kk % 2].h, 16)
                    aKDs[kk % 2].note(16)
                for vz in range(32):
                    z, h = vz // 2, vz % 2
                    aVE.wait(s, vz + 1)
                    s.dma_start(out=cinvr[:, z, h * 512:(h + 1) * 512],
                                in_=vt_s[:, vz % 2, :]).then_inc(aVDs[vz % 2].h, 16)
                    aVDs[vz % 2].note(16)

            @blk.gpsimd
            def _(g_):
                g_.wait_ge(xr_h, xr_n)
                g_.dma_start(out=bvrow[:], in_=dr["r_bv"]).then_inc(aBV.h, 16)
                aBV.note(16)
                for c_ in aKDs:
                    c_.wait(g_)
                for c_ in aVDs:
                    c_.wait(g_)
                cc = g_.collective_compute(
                    "AllGather", ALU.bypass,
                    replica_groups=[[0, 1], [2, 3], [4, 5], [6, 7]],
                    ins=[cin_k], outs=[cout_k])
                cagK.inc(cc)
                cc = g_.collective_compute(
                    "AllGather", ALU.bypass,
                    replica_groups=[[0, 1], [2, 3], [4, 5], [6, 7]],
                    ins=[cin_v], outs=[cout_v])
                cagV.inc(cc)

            @blk.vector
            def _(v):
                v.wait_ge(xr_h, xr_n)
                for g in range(NGRP):
                    gsl = slice(g * 512, (g + 1) * 512)
                    xsb = xn[:, :, gsl]
                    aA_sm.wait(v, g)
                    for k in range(NCH):
                        v.tensor_copy(out=xsb[:, k, :], in_=x[:, k, gsl])
                    v.drain()
                    aA_x.bump(v)
                    aA_sm.wait(v, g + 1)
                    v.tensor_scalar_mul(out=mrow[:], in0=ps_st[0:1, 0:512],
                                        scalar1=1.0 / C)
                    v.tensor_scalar_mul(out=trow[0:1, 0:512],
                                        in0=ps_st[0:1, 512:1024], scalar1=1.0 / C)
                    v.drain()
                    v.tensor_mul(out=trow[0:1, 512:1024], in0=mrow[:], in1=mrow[:])
                    v.drain()
                    v.tensor_sub(out=trow[0:1, 0:512], in0=trow[0:1, 0:512],
                                 in1=trow[0:1, 512:1024])
                    v.drain()
                    v.tensor_scalar_add(out=trow[0:1, 512:1024],
                                        in0=trow[0:1, 0:512], scalar1=EPS)
                    v.drain()
                    aA_v.bump(v)
                    aA_sq.wait(v, g + 1)
                    v.reciprocal(out=r0row[:], in_=trow[0:1, 512:1024])
                    v.drain()
                    v.tensor_copy(out=mrow_b[:], in_=mrow[:])
                    v.tensor_copy(out=r0row_b[:], in_=r0row[:])
                    v.drain()
                    aA_r.bump(v)
                    aA_rep.wait(v, g + 1)
                    for k in range(NCH):
                        v.tensor_tensor(out=xsb[:, k, :], in0=x[:, k, gsl],
                                        in1=ps_mr[:, 0:512], op=ALU.subtract)
                        v.drain()
                        v.tensor_tensor(out=xsb[:, k, :], in0=xsb[:, k, :],
                                        in1=ps_mr[:, 512:1024], op=ALU.mult)
                        v.drain()
                    aA_y.bump(v)

            @blk.scalar
            def _(s):
                for g in range(NGRP):
                    aA_x.wait(s, g + 1)
                    aA_sm.wait(s, g)
                    xsb = xn[:, :, slice(g * 512, (g + 1) * 512)]
                    for k in range(NCH):
                        s.activation(out=xq[:, k, :], in_=xsb[:, k, :],
                                     func=AF.Square)
                    s.drain()
                    aA_q.bump(s)
                    aA_v.wait(s, g + 1)
                    s.activation(out=trow[0:1, 512:1024], in_=trow[0:1, 512:1024],
                                 func=AF.Sqrt)
                    s.drain()
                    aA_sq.bump(s)
                for pj in range(64):
                    j, g = pj // 4, pj % 4
                    gsl = slice(g * 512, (g + 1) * 512)
                    aQKP.wait(s, pj + 1)
                    if j < 8:
                        s.add(out=qfm[:, j, gsl], in_=ps_p[:, pj % 2, :],
                              add=consts["c_bqkv"][:, j:j + 1])
                        s.drain()
                    else:
                        kk = pj - 32
                        if kk >= 2:
                            aKDs[kk % 2].wait(s, 16 * (kk // 2))
                        s.add(out=kq_s[:, kk % 2, :], in_=ps_p[:, pj % 2, :],
                              add=consts["c_bqkv"][:, j:j + 1])
                        s.drain()
                    aQKE.bump(s)
                for vz in range(32):
                    aVP.wait(s, vz + 1)
                    if vz >= 2:
                        aVDs[vz % 2].wait(s, 16 * (vz // 2))
                    s.copy(out=vt_s[:, vz % 2, :], in_=ps_v[:, vz % 2, :])
                    s.drain()
                    aVE.bump(s)

            @blk.tensor
            def _(t):
                t.wait_ge(st.cKC.h, st.cKC.n)
                for g in range(NGRP):
                    gsl = slice(g * 512, (g + 1) * 512)
                    xsb = xn[:, :, gsl]
                    aA_x.wait(t, g + 1)
                    aA_q.wait(t, g + 1)
                    aA_r.wait(t, g)
                    for k in range(NCH):
                        t.matmul(ps_st[0:1, 0:512], st.onesc_b[:], xsb[:, k, :],
                                 start=(k == 0), stop=(k == NCH - 1))
                    for k in range(NCH):
                        last = (k == NCH - 1)
                        mm = t.matmul(ps_st[0:1, 512:1024], st.onesc_b[:],
                                      xq[:, k, :], start=(k == 0), stop=last)
                        if last:
                            aA_sm.inc(mm)
                    aA_r.wait(t, g + 1)
                    aA_y.wait(t, g)
                    t.matmul(ps_mr[:, 0:512], st.onesr_b[0:1, :], mrow_b[:],
                             start=True, stop=True)
                    mm = t.matmul(ps_mr[:, 512:1024], st.onesr_b[0:1, :],
                                  r0row_b[:], start=True, stop=True)
                    aA_rep.inc(mm)
                aWL.wait(t, 32)
                for pj in range(64):
                    j, g = pj // 4, pj % 4
                    gsl = slice(g * 512, (g + 1) * 512)
                    aA_y.wait(t, g + 1)
                    if pj >= 2:
                        aQKE.wait(t, pj - 1)
                    for k in range(NCH):
                        mm = t.matmul(ps_p[:, pj % 2, :],
                                      wqk[:, k, j * 128:(j + 1) * 128],
                                      xn[:, k, gsl], start=(k == 0),
                                      stop=(k == NCH - 1))
                    aQKP.inc(mm)
                aWL.wait(t, 32)
                aBV.wait(t)
                for vz in range(32):
                    z, h = vz // 2, vz % 2
                    bsl = slice(z * 128, (z + 1) * 128)
                    aA_y.wait(t, z // 4 + 1)
                    if vz >= 2:
                        aVE.wait(t, vz - 1)
                    for k in range(NCH):
                        t.matmul(ps_v[:, vz % 2, :], xn[:, k, bsl],
                                 wv[:, k, h * 512:(h + 1) * 512],
                                 start=(k == 0), stop=False)
                    mm = t.matmul(ps_v[:, vz % 2, :], st.onesr_b[0:1, :],
                                  bvrow[0:1, h * 512:(h + 1) * 512],
                                  start=False, stop=True)
                    aVP.inc(mm)

        pt = contextlib.ExitStack()
        with pt:
            sbt = lambda n, s, d: pt.enter_context(
                nc.sbuf_tensor(f"{n}_{tag}", s, d))
            sc_s = sbt("scs", [128, 4096], F32)
            p_b = sbt("pb", [128, 4096], BF16)
            kst = sbt("kst", [128, NCH, 2, 512], BF16)
            vst = sbt("vst", [128, 2, 1024], BF16)
            pT_b = sbt("pTb", [128, 2, 128], BF16)
            yt_b = sbt("ytb", [128, 1024], BF16)
            mx = sbt("mx", [128, 1], F32)
            nm32 = sbt("nm32", [128, 1], F32)
            rs = sbt("rs", [128, 1], F32)
            rsv = sbt("rsv", [128, 1], F32)
            npv9 = sbt("npv9", [128, 1], F32)
            ps_sc = pt.enter_context(nc.psum_tensor(f"apsc_{tag}", [128, 2, 512], F32))
            ps_y = pt.enter_context(nc.psum_tensor(f"apy_{tag}", [128, 2, 512], F32))
            psT_p = pt.enter_context(nc.psum_tensor(f"apTp_{tag}", [128, 2, 128], BF16))
            psT_y = pt.enter_context(nc.psum_tensor(f"apTy_{tag}", [128, 1024], BF16))

            CH = []
            VT = []
            ks_tot = [0, 0]
            vs_tot = [0, 0]
            scq = 0
            ktg = 0
            for z in range(NTILE):
                ntk = 2 * z + 2
                nch512 = (ntk + 3) // 4
                for kc in range(nch512):
                    tiles = min(4, ntk - kc * 4)
                    ks_tot[scq % 2] += 16 * tiles
                    CH.append((z, kc, scq, tiles, ks_tot[scq % 2]))
                    scq += 1
                for kt in range(ntk):
                    vs_tot[ktg % 2] += 16
                    VT.append((z, kt, ktg, vs_tot[ktg % 2]))
                    ktg += 1
            KTEND = {}
            SCEND = {}
            a = 0
            b_ = 0
            for z in range(NTILE):
                a += 2 * z + 2
                b_ += (2 * z + 2 + 3) // 4
                KTEND[z] = a
                SCEND[z] = b_

            @blk.sync
            def _(s):
                cagK.wait(s, 1)
                cagV.wait(s, 1)
                ckr = cout_k.rearrange("(r k p) t -> p r k t", p=128, r=2)
                cvr = cout_v.rearrange("(r z p) c -> p r z c", p=128, r=2)
                chz = {}
                vtz = {}
                for item in CH:
                    chz.setdefault(item[0], []).append(item)
                for item in VT:
                    vtz.setdefault(item[0], []).append(item)
                for z in range(NTILE):
                    for (_z, kc, sq, tiles, _tot) in chz[z]:
                        if sq >= 2:
                            aSE.wait(s, sq - 1)
                        for tt in range(tiles):
                            kt = kc * 4 + tt
                            s.dma_start(
                                out=kst[:, :, sq % 2, tt * 128:(tt + 1) * 128],
                                in_=ckr[:, kt % 2, :,
                                        (kt // 2) * 128:(kt // 2 + 1) * 128]
                                ).then_inc(aKS[sq % 2].h, 16)
                            aKS[sq % 2].note(16)
                    for (_z, kt, kg, _tot) in vtz[z]:
                        if kg >= 2:
                            aPV.wait(s, kg - 1)
                        s.dma_start(out=vst[:, kg % 2, :],
                                    in_=cvr[:, kt % 2, kt // 2, :]
                                    ).then_inc(aVS[kg % 2].h, 16)
                        aVS[kg % 2].note(16)

            @blk.tensor
            def _(t):
                aQKE.wait(t, 32)
                ci = 0
                vi = 0
                for z in range(NTILE):
                    ntk = 2 * z + 2
                    nch512 = (ntk + 3) // 4
                    bsl = slice(z * 128, (z + 1) * 128)
                    for kc in range(nch512):
                        (_z, _kc, sq, tiles, tot) = CH[ci]
                        ci += 1
                        nkc = tiles * 128
                        aKS[sq % 2].wait(t, tot)
                        if sq >= 2:
                            aSE.wait(t, sq - 1)
                        for k in range(NCH):
                            mm = t.matmul(ps_sc[:, sq % 2, 0:nkc],
                                          qfm[:, k, bsl],
                                          kst[:, k, sq % 2, 0:nkc],
                                          start=(k == 0), stop=(k == NCH - 1))
                        aSC.inc(mm)
                    aEXP.wait(t, z + 1)
                    if z >= 1:
                        aYB.wait(t, z)
                    for kt in range(ntk):
                        (_z, _kt, kg, vtot) = VT[vi]
                        vi += 1
                        if kg >= 2:
                            aPTE.wait(t, kg - 1)
                        tp = t.transpose(psT_p[:, kg % 2, :],
                                         p_b[:, kt * 128:(kt + 1) * 128],
                                         st.ident_b[:])
                        aPT.inc(tp)
                        aPTE.wait(t, kg + 1)
                        aVS[kg % 2].wait(t, vtot)
                        for h in range(2):
                            mm = t.matmul(ps_y[:, h, :], pT_b[:, kg % 2, :],
                                          vst[:, kg % 2, h * 512:(h + 1) * 512],
                                          start=(kt == 0), stop=(kt == ntk - 1))
                        aPV.inc(mm)
                    aYB.wait(t, z + 1)
                    if z >= 1:
                        aXE.wait(t, z)
                    for ck in range(NCH):
                        tp = t.transpose(psT_y[:, ck * 128:(ck + 1) * 128],
                                         yt_b[:, ck * 128:(ck + 1) * 128],
                                         st.ident_b[:])
                        aYT.inc(tp)

            @blk.vector
            def _(v):
                cagK.wait(v, 1)
                cagV.wait(v, 1)
                v.tensor_scalar_mul(out=npv9[:], in0=st.nparv[:], scalar1=1e9)
                v.drain()
                ci = 0
                ktg_v = 0
                for z in range(NTILE):
                    ntk = 2 * z + 2
                    nch512 = (ntk + 3) // 4
                    nk = ntk * 128
                    for kc in range(nch512):
                        (_z, _kc, sq, tiles, _tot) = CH[ci]
                        ci += 1
                        nkc = tiles * 128
                        aSC.wait(v, sq + 1)
                        v.tensor_copy(out=sc_s[:, kc * 512:kc * 512 + nkc],
                                      in_=ps_sc[:, sq % 2, 0:nkc])
                        v.drain()
                        aSE.bump(v)
                    d0 = slice(2 * z * 128, 2 * z * 128 + 128)
                    d1 = slice((2 * z + 1) * 128, (2 * z + 1) * 128 + 128)
                    v.scalar_tensor_tensor(out=sc_s[:, d0],
                                           in0=consts["k_causal"][:],
                                           scalar=st.nparv[:, 0:1],
                                           in1=sc_s[:, d0],
                                           op0=ALU.mult, op1=ALU.add)
                    v.drain()
                    v.scalar_tensor_tensor(out=sc_s[:, d1],
                                           in0=consts["k_causal"][:],
                                           scalar=st.parv[:, 0:1],
                                           in1=sc_s[:, d1],
                                           op0=ALU.mult, op1=ALU.add)
                    v.drain()
                    v.tensor_scalar(out=sc_s[:, d1], in0=sc_s[:, d1],
                                    scalar1=npv9[:, 0:1], scalar2=None,
                                    op0=ALU.subtract)
                    v.drain()
                    v.reduce_max(out=mx[:], in_=sc_s[:, 0:nk], axis=AXT.X)
                    v.drain()
                    v.tensor_scalar_mul(out=nm32[:], in0=mx[:], scalar1=-1.0 / 32)
                    v.drain()
                    aMX.bump(v)
                    aEXP.wait(v, z + 1)
                    v.reciprocal(out=rsv[:], in_=rs[:])
                    v.drain()
                    aRS.bump(v)
                    for kt in range(ntk):
                        kg = ktg_v
                        ktg_v += 1
                        aPT.wait(v, kg + 1)
                        if kg >= 2:
                            aPV.wait(v, kg - 1)
                        v.tensor_copy(out=pT_b[:, kg % 2, :],
                                      in_=psT_p[:, kg % 2, :])
                        v.drain()
                        aPTE.bump(v)
                    aPV.wait(v, KTEND[z])
                    for h in range(2):
                        v.tensor_scalar(out=yt_b[:, h * 512:(h + 1) * 512],
                                        in0=ps_y[:, h, :], scalar1=rsv[:, 0:1],
                                        scalar2=None, op0=ALU.mult)
                        v.drain()
                    aYB.bump(v)
                    aYT.wait(v, (z + 1) * NCH)
                    for ck in range(NCH):
                        v.tensor_tensor(out=x[:, ck, z * 128:(z + 1) * 128],
                                        in0=x[:, ck, z * 128:(z + 1) * 128],
                                        in1=psT_y[:, ck * 128:(ck + 1) * 128],
                                        op=ALU.add)
                        v.drain()
                    aXE.bump(v)

            @blk.scalar
            def _(s):
                for z in range(NTILE):
                    ntk = 2 * z + 2
                    nk = ntk * 128
                    aMX.wait(s, z + 1)
                    s.activation(out=p_b[:, 0:nk], in_=sc_s[:, 0:nk],
                                 func=AF.Exp, scale=1.0 / 32,
                                 bias=nm32[:, 0:1], accum_out=rs[:])
                    s.drain()
                    aEXP.bump(s)

    st.x_ready = (aXE.h, aXE.c.n)




def kernel(**inputs):
    cores = _prep_inputs(inputs)
    nc = build_program()
    res = run_bass_kernel_spmd(nc, cores, core_ids=list(range(8)))
    return _gather_output(res.results)
